# version 06
# baseline (speedup 1.0000x reference)
import numpy as np
import ml_dtypes
import concourse.bass as bass
import concourse.mybir as mybir
from concourse.bass_utils import run_bass_kernel_spmd
from contextlib import ExitStack

F32 = mybir.dt.float32
BF16 = mybir.dt.bfloat16
U32 = mybir.dt.uint32
AF = mybir.ActivationFunctionType
ALU = mybir.AluOpType
AX = mybir.AxisListType
ds = bass.ds

P = 128
S = 4096
D = 1024
DC = 8
HD = 4096
HC = 32
NS = 128
H = 512
KC = 4
M3 = 1536
MC = 12
XW = 4608
WLEN = 2176
NCOL = WLEN + 1
CPAR = KC * NCOL
NSWEEP = 22
YPAD = 128
LN_EPS = 1e-5
C32c = 1.5 * 2.0 ** 23
C64c = 1.5 * 2.0 ** 52
LN1E8 = 18.420680743952367

TOA = 256
TWA = TOA + 2
XWX = 4640
FF1_A = [128 + TOA * j for j in range(17)]
FF1_B = [512 * j for j in range(9)]
MID_T = [256 + 512 * j for j in range(8)]
UW_T = [(0, 512), (512, 512), (1024, 512), (1536, 512), (2048, 128)]
X2W = 2688
X2OFF = 192
FF2_A = [256 + TOA * j for j in range(9)]
FF2_B = [512 * j for j in range(5)]
OUT_LO, OUT_HI = 318, 2366


class Sem:
    def __init__(self, stack, nc, name):
        self.h = stack.enter_context(nc.semaphore(name))
        self.n = 0

    def inc(self, instr, by=1):
        instr.then_inc(self.h, by)
        self.n += by
        return instr

    def wait(self, eng, val=None):
        v = self.n if val is None else val
        if v > 0:
            eng.wait_ge(self.h, v)

    def mark(self):
        return self.n


def build_nc(debug=False):
    nc = bass.Bass("TRN2", target_bir_lowering=False)
    A = lambda name, shape, dt=F32: nc.dram_tensor(name, shape, dt, kind="ExternalInput").ap()
    xT = A("xT", [D, XWX])
    w1T = A("w1T", [D, HD])
    w2T = A("w2T", [HD, D])
    convw = A("convw", [P, 96])
    convb = A("convb", [P, HC])
    normg = A("normg", [P, DC])
    normb = A("normb", [P, DC])
    BwT = A("BwT", [D, NS])
    Bbi = A("Bb", [P, 1])
    Alog = A("Alog", [P, 1])
    CwT = A("CwT", [P, D])
    Cbi = A("Cb", [P, DC])
    Wx4 = A("Wx4", [D, 2048])
    Wh4 = A("Wh4", [H, 2048])
    gb16 = A("gb16", [P, 16])
    owT = A("owT", [H, D])
    obi = A("ob", [P, DC])
    ident = A("ident", [P, P])
    masks = A("masks", [P, 8])
    umaski = A("umask", [P, MC * 512])
    x2m0i = A("x2m0", [P, 512])
    x2m4i = A("x2m4", [P, 128])
    woff = A("woff", [1, 1], U32)

    x1buf = nc.dram_tensor("x1buf", [D, XW], F32, kind="Internal").ap()
    ybuf = nc.dram_tensor("ybuf", [D, S + 2 * YPAD], F32, kind="Internal").ap()
    yqbuf = nc.dram_tensor("yqbuf", [HD, XW], BF16, kind="Internal").ap()
    x2buf = nc.dram_tensor("x2buf", [D, X2W], F32, kind="Internal").ap()
    yq2buf = nc.dram_tensor("yq2buf", [HD, X2W], BF16, kind="Internal").ap()
    wq1buf = nc.dram_tensor("wq1buf", [D, HD], BF16, kind="Internal").ap()
    wq2buf = nc.dram_tensor("wq2buf", [HD, D], BF16, kind="Internal").ap()
    outT = nc.dram_tensor("outT", [D, X2W], F32, kind="ExternalOutput").ap()
    if debug:
        dbgb = nc.dram_tensor("dbgb", [P, 2 * CPAR], BF16, kind="ExternalOutput").ap()
        dbg2 = nc.dram_tensor("dbg2", [P, MC * WLEN], BF16, kind="ExternalOutput").ap()
        dbg3 = nc.dram_tensor("dbg3", [P, YPAD + S + YPAD], BF16, kind="ExternalOutput").ap()

    st = ExitStack()
    _nm = [0]
    with st:
        def sb(name, shape, dt=F32, stk=None):
            _nm[0] += 1
            return (stk or st).enter_context(nc.sbuf_tensor(f"{name}_{_nm[0]}", shape, dt))

        def ps_(name):
            return st.enter_context(nc.psum_tensor(name, [P, 512], F32))

        onesb = sb("onesb", [P, 1], BF16)
        onesf = sb("onesf", [P, 1])
        idsf = sb("idsf", [P, P])
        cmag = sb("cmag", [P, 1])
        cnmag = sb("cnmag", [P, 1])
        halfc = sb("halfc", [P, 1])
        epsc = sb("epsc", [P, 1])
        rw1c = sb("rw1c", [P, 1])
        rw2h = sb("rw2h", [P, 1])
        swcol = sb("swcol", [P, 1])
        convws = sb("convws", [P, 96])
        convbs = sb("convbs", [P, HC])
        gcol = sb("gcol", [P, DC])
        bcolv = sb("bcolv", [P, DC])
        cbs = sb("cbs", [P, DC])
        obs = sb("obs", [P, DC])
        bbs = sb("bbs", [P, 1])
        mks = sb("mks", [P, 8])
        alcol = sb("alcol", [P, 2])
        tcol = sb("tcol", [P, 8])
        stg = sb("stg", [P, 8])
        r11 = sb("r11", [1, 8])
        statr = sb("statr", [1, 4 * 512])
        ones128 = sb("ones128", [P, P])
        repb = sb("repb", [P, 4 * P])
        a2col = sb("a2col", [P, 40])
        cm = sb("cm", [P, 512])
        cmt = sb("cmt", [P, 512])
        zsb = sb("zsb", [P, 512], BF16)
        woffs = sb("woffs", [1, 1], U32)
        hmt = sb("hmt", [P, 512])
        hmt2 = sb("hmt2", [P, 512])
        a1s = sb("a1s", [P, 512])
        onesr = sb("onesr", [1, P])
        idsb = sb("idsb", [P, P], BF16)
        cneg1 = sb("cneg1", [P, 1])

        pb = [ps_(f"pb{i}") for i in range(4)]
        pbc = ps_("pbc")
        pbc2 = ps_("pbc2")
        pst = ps_("pst")
        pst2 = ps_("pst2")
        ptrip = [[pb[0], pb[1], pb[2]], [pb[3], pst, pst2]]

        sy = Sem(st, nc, "sy")
        sg = Sem(st, nc, "sg")
        sv = Sem(st, nc, "sv")
        sa = Sem(st, nc, "sa")
        sp = Sem(st, nc, "sp")
        sev = Sem(st, nc, "sev")
        su = Sem(st, nc, "su")

        V, SC, T, G, SY = nc.vector, nc.scalar, nc.tensor, nc.gpsimd, nc.sync

        def tnop_inc(sem):
            T.drain()
            sem.inc(T.nop())

        def dma_fence():
            SY.wait_ge(sy.h, sy.n) if sy.n else None
            G.wait_ge(sg.h, sg.n) if sg.n else None

        def bar():
            dma_fence()
            nc.all_engine_barrier()

        sy.inc(SY.dma_start(idsf[:], ident), 16)
        sy.inc(SY.dma_start(convws[:], convw), 16)
        sy.inc(SY.dma_start(convbs[:], convb), 16)
        sy.inc(SY.dma_start(gcol[:], normg), 16)
        sy.inc(SY.dma_start(bcolv[:], normb), 16)
        sy.inc(SY.dma_start(cbs[:], Cbi), 16)
        sy.inc(SY.dma_start(obs[:], obi), 16)
        sy.inc(SY.dma_start(bbs[:], Bbi), 16)
        sy.inc(SY.dma_start(mks[:], masks), 16)
        sy.inc(SY.dma_start(woffs[:], woff), 16)
        sy.inc(SY.dma_start(alcol[:, 0:1], Alog), 16)
        sy.wait(V)
        V.memset(onesf[:], 1.0)
        V.memset(onesr[:], 1.0)
        V.memset(ones128[:], 1.0)
        V.memset(a2col[:], 0.0)
        V.memset(tcol[:], 1.0)
        V.memset(stg[:], 1.0)
        V.memset(cm[:], 0.0)
        V.memset(cmt[:], 0.0)
        V.memset(statr[:], 0.0)
        V.memset(hmt[:], 0.0)
        V.memset(hmt2[:], 0.0)
        V.memset(a1s[:], 0.0)
        V.memset(repb[:], 0.0)
        V.memset(halfc[:], 0.5)
        V.memset(epsc[:], LN_EPS)
        V.memset(zsb[:], 0.0)
        V.memset(cneg1[:], -1.0)
        V.drain()
        sy.wait(V)
        V.tensor_copy(idsb[:], idsf[:])
        V.drain()
        V.tensor_copy(onesb[:], onesf[:])
        V.drain()
        V.tensor_scalar(cmag[:], onesf[:], 2.0 ** -30, None, ALU.add)
        V.drain()
        V.tensor_scalar(cmag[:], cmag[:], -1.0, None, ALU.add)
        V.drain()
        V.tensor_scalar(cmag[:], cmag[:], 2.0 ** 30, None, ALU.mult)
        V.drain()
        V.tensor_scalar(cmag[:], cmag[:], C64c, C32c, ALU.mult, ALU.add)
        V.drain()
        V.tensor_scalar(cnmag[:], cmag[:], -1.0, None, ALU.mult)
        V.drain()
        bar()

        def hbm3(t, col0, nch, w):
            C = t.shape[1]
            return bass.AP(t.tensor, col0, [[C, P], [P * C, nch], [1, w]])

        def sb3(t, nch, w, tw, col0=0):
            return bass.AP(t, col0, [[nch * tw, P], [tw, nch], [1, w]])

        def stats_rows(src, nch, tw, xbb, sqbb):
            tot = nch * tw
            sp.wait(V)
            V.tensor_copy(xbb[:, 0:tot], src[:, 0:tot])
            sv.inc(V.tensor_tensor(sqbb[:, 0:tot], src[:, 0:tot], src[:, 0:tot], ALU.mult))
            sv.wait(T)
            for c in range(nch):
                T.matmul(pst[0:1, 0:tw], onesb[:, 0:1], xbb[:, c * tw:(c + 1) * tw],
                         start=(c == 0), stop=(c == nch - 1))
            for c in range(nch):
                mm = T.matmul(pst2[0:1, 0:tw], onesb[:, 0:1], sqbb[:, c * tw:(c + 1) * tw],
                              start=(c == 0), stop=(c == nch - 1))
            sp.inc(mm)
            sp.wait(V)
            mu = statr[0:1, 1024:1024 + tw]
            va = statr[0:1, 1536:1536 + tw]
            rr = statr[0:1, 0:tw]
            mr = statr[0:1, 512:512 + tw]
            V.tensor_scalar(mu, pst[0:1, 0:tw], 1.0 / (nch * P), None, ALU.mult)
            V.tensor_scalar(va, pst2[0:1, 0:tw], 1.0 / (nch * P), None, ALU.mult)
            V.drain()
            V.tensor_tensor(rr, mu, mu, ALU.mult)
            V.drain()
            V.tensor_tensor(va, va, rr, ALU.subtract)
            sv.inc(V.drain())
            sv.wait(SC)
            sa.inc(SC.activation(rr, va, AF.Sqrt, bias=epsc[0:1, 0:1]))
            sa.wait(V)
            V.reciprocal(rr, rr)
            V.drain()
            V.tensor_tensor(mr, mu, rr, ALU.mult)
            sv.inc(V.drain())

        def ln_apply(src, dst, nch, tw):
            sv.wait(T)
            T.matmul(pbc[:, 0:tw], onesr[0:1, 0:P], statr[0:1, 0:tw], start=True, stop=True)
            T.matmul(pbc2[:, 0:tw], onesr[0:1, 0:P], statr[0:1, 512:512 + tw],
                     start=True, stop=True)
            tnop_inc(sp)
            sp.wait(V)
            for c in range(nch):
                V.tensor_tensor(dst[:, c * tw:(c + 1) * tw], src[:, c * tw:(c + 1) * tw],
                                pbc[:, 0:tw], ALU.mult)
            V.drain()
            for c in range(nch):
                V.tensor_tensor(dst[:, c * tw:(c + 1) * tw], dst[:, c * tw:(c + 1) * tw],
                                pbc2[:, 0:tw], ALU.subtract)
            sv.inc(V.drain())

        def quant_cols(src, nch, tw, pools, nblk, acol_dst=None, halves=1):
            ap3 = bass.AP(src.tensor if hasattr(src, "tensor") else src, 0,
                          [[nch * tw, P], [1, tw], [tw, nch]])
            sv.wait(V)
            V.tensor_reduce(cm[:, 0:tw], ap3, AX.X, ALU.max)
            V.tensor_reduce(cmt[:, 0:tw], ap3, AX.X, ALU.min)
            V.drain()
            sv.inc(V.engine_nop())
            sv.wait(T)
            for j in range(nblk):
                w = min(128, tw - j * 128)
                T.transpose(pst[0:w, j * 128:j * 128 + 128],
                            cm[:, j * 128:j * 128 + w], idsf[:])
                T.transpose(pst2[0:w, j * 128:j * 128 + 128],
                            cmt[:, j * 128:j * 128 + w], idsf[:])
            tnop_inc(sp)
            sp.wait(V)
            for j in range(nblk):
                w = min(128, tw - j * 128)
                V.tensor_reduce(tcol[0:w, j:j + 1], pst[0:w, j * 128:(j + 1) * 128],
                                AX.X, ALU.max)
                V.tensor_reduce(stg[0:w, j:j + 1], pst2[0:w, j * 128:(j + 1) * 128],
                                AX.X, ALU.min)
            V.drain()
            V.tensor_scalar(stg[:, 0:nblk], stg[:, 0:nblk], -1.0, None, ALU.mult)
            V.drain()
            V.tensor_tensor(tcol[:, 0:nblk], tcol[:, 0:nblk], stg[:, 0:nblk], ALU.max)
            V.drain()
            V.tensor_scalar(tcol[:, 0:nblk], tcol[:, 0:nblk], 1e-5, None, ALU.max)
            V.drain()
            V.reciprocal(stg[:, 0:nblk], tcol[:, 0:nblk])
            V.drain()
            V.tensor_scalar(stg[:, 0:nblk], stg[:, 0:nblk], 127.0, None, ALU.mult)
            V.tensor_scalar(stg[:, 4:4 + nblk], tcol[:, 0:nblk], 1.0 / 127.0, None,
                            ALU.mult)
            if acol_dst is not None:
                V.tensor_copy(acol_dst, tcol[:, 0:nblk])
            sv.inc(V.drain())

        def colbcast(psum, col_src, ncols, nblk):
            sp.wait(V)
            for j in range(nblk):
                V.tensor_scalar(repb[:, j * P:(j + 1) * P], ones128[:],
                                col_src[:, j:j + 1], None, ALU.mult)
            sv.inc(V.drain())
            sv.wait(T)
            for j in range(nblk):
                T.transpose(psum[0:P, j * 128:(j + 1) * 128],
                            repb[:, j * P:(j + 1) * P], idsf[:])
            tnop_inc(sp)
            return None

        def roundq_act(dst, src, tot):
            sa.inc(SC.activation(src[:, 0:tot], src[:, 0:tot], AF.Identity,
                                 bias=cmag[:, 0:1]))
            sa.wait(SC)
            SC.drain()
            sa.inc(SC.activation(dst[:, 0:tot], src[:, 0:tot], AF.Identity,
                                 bias=cnmag[:, 0:1]))

        def quant_weights(wsrc, nchunks, width, wq_t, wqhbm, rw_dst, wscb,
                          rw_scale=1.0 / 127.0):
            for c in range(nchunks):
                sv.wait(SY)
                dma_fence()
                sy.inc(SY.dma_start(wscb[:, 0:width], wsrc[c * P:(c + 1) * P, :]), 16)
                sy.wait(V)
                sv.inc(V.tensor_reduce(cm[:, c:c + 1], wscb[:, 0:width], AX.X,
                                       ALU.add, apply_absolute_value=True))
            V.drain()
            V.tensor_reduce(hmt[:, 0:1], cm[:, 0:nchunks], AX.X, ALU.add)
            sv.inc(V.drain())
            sv.wait(T)
            T.matmul(pst[0:1, 0:1], hmt[:, 0:1], onesf[:, 0:1], start=True, stop=True)
            tnop_inc(sp)
            sp.wait(V)
            V.tensor_scalar(r11[0:1, 0:1], pst[0:1, 0:1], 1.0 / (nchunks * P * width),
                            None, ALU.mult)
            V.drain()
            V.tensor_scalar(r11[0:1, 0:1], r11[0:1, 0:1], 1e-5, None, ALU.max)
            V.drain()
            V.reciprocal(r11[0:1, 1:2], r11[0:1, 0:1])
            V.drain()
            V.tensor_scalar(r11[0:1, 2:3], r11[0:1, 0:1], rw_scale, None, ALU.mult)
            sv.inc(V.drain())
            sv.wait(T)
            T.matmul(pst[0:P, 8:9], onesr[0:1, 0:P], r11[0:1, 1:2], start=True, stop=True)
            T.matmul(pst[0:P, 9:10], onesr[0:1, 0:P], r11[0:1, 2:3],
                     start=True, stop=True)
            tnop_inc(sp)
            sp.wait(V)
            V.tensor_copy(swcol[:, 0:1], pst[0:P, 8:9])
            V.tensor_copy(rw_dst[:, 0:1], pst[0:P, 9:10])
            sv.inc(V.drain())
            for c in range(nchunks):
                sv.wait(SY)
                sy.inc(SY.dma_start(wscb[:, 0:width], wsrc[c * P:(c + 1) * P, :]), 16)
                sy.wait(V)
                V.tensor_scalar(wscb[:, 0:width], wscb[:, 0:width], swcol[:, 0:1],
                                cmag[:, 0:1], ALU.mult, ALU.add)
                V.drain()
                V.tensor_scalar(wscb[:, 0:width], wscb[:, 0:width], cnmag[:, 0:1], 1.0,
                                ALU.add, ALU.min)
                V.drain()
                dst = wq_t[:, c * width:(c + 1) * width]
                sv.inc(V.tensor_scalar(dst, wscb[:, 0:width], -1.0, None, ALU.max))
                sv.wait(SY)
                sy.inc(SY.dma_start(wqhbm[c * P:(c + 1) * P, :], dst), 16)
            bar()

        def ffn_stage_a(wq, abuf, src_hbm, c0list, yq_hbm):
            xta, yya, xbb, sqbb, xq, y2, yqa, hg, tg = abuf
            xt, yy = xta, yya
            for c0 in c0list:
                tw = TWA
                sv.wait(SY)
                sa.wait(SY)
                dma_fence()
                sy.inc(SY.dma_start(sb3(xt, DC, tw, tw), hbm3(src_hbm, c0 - 1, DC, tw)), 16)
                sy.wait(V)
                stats_rows(xt, DC, tw, xbb, sqbb)
                ln_apply(xt, yy, DC, tw)
                quant_cols(yy, DC, tw, ([(hg, 8 * TWA)], [(tg, 8 * TOA)], [(xta, DC * TWA)], [(y2, HC * TOA)]), 3)
                colbcast(pbc, stg[:, 0:4], tw, 3)
                colbcast(pbc2, stg[:, 4:8], tw, 3)
                sp.wait(V)
                for c in range(DC):
                    V.tensor_tensor(yy[:, c * tw:(c + 1) * tw], yy[:, c * tw:(c + 1) * tw],
                                    pbc[:, 0:tw], ALU.mult)
                sv.inc(V.drain())
                sv.wait(SC)
                roundq_act(yya, yy, DC * tw)
                sa.wait(V)
                for c in range(DC):
                    V.tensor_tensor(xq[:, c * tw:(c + 1) * tw], yya[:, c * tw:(c + 1) * tw],
                                    pbc2[:, 0:tw], ALU.mult)
                sv.inc(V.drain())
                sv.wait(T)
                for q in range(HC // 4):
                    for j in range(4):
                        mh = 4 * q + j
                        sev.wait(T, sev.n - 3 + j)
                        for kd in range(DC):
                            mm = T.matmul(pb[j][:, 0:tw],
                                          wq[:, kd * HD + mh * P:kd * HD + (mh + 1) * P],
                                          xq[:, kd * tw:(kd + 1) * tw],
                                          start=(kd == 0), stop=(kd == DC - 1))
                        sp.inc(mm)
                    sp.wait(SC)
                    SC.drain()
                    for j in range(4):
                        mh = 4 * q + j
                        a1 = SC.activation(tg[:, (mh % 8) * TOA:(mh % 8 + 1) * TOA],
                                           pb[j][:, 0:tw - 2], AF.Identity,
                                           bias=convbs[:, mh:mh + 1],
                                           scale=convws[:, 3 * mh:3 * mh + 1])
                    sa.inc(a1)
                    sa.wait(V)
                    for j in range(4):
                        mh = 4 * q + j
                        V.scalar_tensor_tensor(tg[:, (mh % 8) * TOA:(mh % 8 + 1) * TOA],
                                               pb[j][:, 1:tw - 1],
                                               convws[:, 3 * mh + 1:3 * mh + 2],
                                               tg[:, (mh % 8) * TOA:(mh % 8 + 1) * TOA],
                                               ALU.mult, ALU.add)
                    V.drain()
                    for j in range(4):
                        mh = 4 * q + j
                        ev = V.scalar_tensor_tensor(tg[:, (mh % 8) * TOA:(mh % 8 + 1) * TOA],
                                                    pb[j][:, 2:tw],
                                                    convws[:, 3 * mh + 2:3 * mh + 3],
                                                    tg[:, (mh % 8) * TOA:(mh % 8 + 1) * TOA],
                                                    ALU.mult, ALU.add)
                        sev.inc(ev)
                    sv.inc(V.engine_nop())
                    sv.wait(SC)
                    SC.drain()
                    for j in range(4):
                        mh = 4 * q + j
                        a2 = SC.activation(y2[:, mh * TOA:(mh + 1) * TOA],
                                           tg[:, (mh % 8) * TOA:(mh % 8 + 1) * TOA],
                                           AF.Silu)
                    sa.inc(a2)
                sa.wait(V)
                quant_cols(y2, HC, TOA, ([(yya, DC * TWA)], [(tg, 8 * TOA)],
                                         [(hg, 8 * TWA)], [(xta, DC * TWA)]),
                           2, acol_dst=a2col[:, c0 // 128:c0 // 128 + 2], halves=2)
                colbcast(pbc, stg[:, 0:4], TOA, 2)
                sp.wait(V)
                for c in range(HC):
                    V.tensor_tensor(y2[:, c * TOA:(c + 1) * TOA], y2[:, c * TOA:(c + 1) * TOA],
                                    pbc[:, 0:TOA], ALU.mult)
                sv.inc(V.drain())
                colbcast(pbc2, stg[:, 4:8], TOA, 2)
                hh = HC // 2
                for half in range(2):
                    sy.wait(SC)
                    sv.wait(SC)
                    y2h = bass.AP(y2, half * hh * TOA, [[HC * TOA, P], [1, hh * TOA]])
                    SC.drain()
                    SC.activation(y2h, y2h, AF.Identity, bias=cmag[:, 0:1])
                    SC.drain()
                    sa.inc(SC.activation(y2h, y2h, AF.Identity, bias=cnmag[:, 0:1]))
                    sa.wait(V)
                    sy.wait(V)
                    sp.wait(V)
                    for c in range(hh):
                        V.tensor_tensor(yqa[:, c * TOA:(c + 1) * TOA],
                                        bass.AP(y2, (half * hh + c) * TOA,
                                                [[HC * TOA, P], [1, TOA]]),
                                        pbc2[:, 0:TOA], ALU.mult)
                    sv.inc(V.drain())
                    sv.wait(SY)
                    dma_fence()
                    sy.inc(SY.dma_start(
                        hbm3(yq_hbm, c0, hh, TOA) if half == 0 else
                        bass.AP(yq_hbm.tensor, hh * P * yq_hbm.shape[1] + c0,
                                [[yq_hbm.shape[1], P], [P * yq_hbm.shape[1], hh], [1, TOA]]),
                        bass.AP(yqa, 0, [[hh * TOA, P], [TOA, hh], [1, TOA]])), 16)
            bar()

        def ffn_stage_b(wq, xt, yy, yqb, src_res_hbm, c0list, yq_hbm, out_hbm):
            for c0 in c0list:
                tw = 512
                sv.wait(SY)
                dma_fence()
                sy.inc(SY.dma_start(sb3(xt, DC, tw, tw), hbm3(src_res_hbm, c0, DC, tw)), 16)
                sy.inc(SY.dma_start(
                    bass.AP(yqb, 0, [[HC * tw, P], [tw, HC], [1, tw]]),
                    hbm3(yq_hbm, c0, HC, tw)), 16)
                colbcast(pbc2, a2col[:, c0 // 128:c0 // 128 + 4], tw, 4)
                sp.wait(V)
                V.tensor_copy(a1s[:, 0:tw], pbc2[:, 0:tw])
                sv.inc(V.drain())
                sy.wait(T)
                sy.wait(V)
                for md in range(DC):
                    bank = pb[md % 2]
                    hb = hmt if md % 2 == 0 else hmt2
                    sev.wait(T, sev.n - 1)
                    for kh in range(HC):
                        mm = T.matmul(bank[:, 0:tw],
                                      wq[:, kh * D + md * P:kh * D + (md + 1) * P],
                                      yqb[:, kh * tw:(kh + 1) * tw],
                                      start=(kh == 0), stop=(kh == HC - 1))
                    sp.inc(mm)
                    sp.wait(V)
                    ev = V.scalar_tensor_tensor(yy[:, md * tw:(md + 1) * tw],
                                                bank[:, 0:tw], rw2h[:, 0:1],
                                                xt[:, md * tw:(md + 1) * tw],
                                                ALU.mult, ALU.add)
                    sev.inc(ev)
                    sv.inc(V.engine_nop())
                sv.wait(SY)
                sy.inc(SY.dma_start(hbm3(out_hbm, c0, DC, tw), sb3(yy, DC, tw, tw)), 16)
            bar()

        def make_stage_a_bufs(stk):
            xta = sb("xta", [P, DC * TWA], F32, stk)
            yya = sb("yya", [P, DC * TWA], F32, stk)
            xbb = sb("xbb", [P, DC * TWA], BF16, stk)
            sqbb = sb("sqbb", [P, DC * TWA], BF16, stk)
            xq = sb("xq", [P, DC * TWA], BF16, stk)
            y2 = sb("y2", [P, HC * TOA], F32, stk)
            yqa = sb("yqa", [P, (HC // 2) * TOA], BF16, stk)
            hg = sb("hg", [P, 8 * TWA], F32, stk)
            tg = sb("tg", [P, 8 * TOA], F32, stk)
            return (xta, yya, xbb, sqbb, xq, y2, yqa, hg, tg)

        ffn1 = ExitStack()
        with ffn1:
            wq = sb("wq", [P, DC * HD], BF16, ffn1)
            zf = sb("zf", [P, 512], F32, ffn1)

            for col0, w, t, nch in [(0, 130, yqbuf, HC), (4480, 128, yqbuf, HC),
                                    (0, 317, yq2buf, HC), (2560, 128, yq2buf, HC)]:
                sy.inc(SY.dma_start(hbm3(t, col0, nch, w),
                                    bass.AP(zsb, 0, [[512, P], [0, nch], [1, w]])), 16)
            V.memset(zf[:], 0.0)
            sv.inc(V.drain())
            sv.wait(SY)
            for col0, w, t, nch in [(0, YPAD, ybuf, DC), (S + YPAD, YPAD, ybuf, DC),
                                    (0, X2OFF, x2buf, DC), (2368, X2W - 2368, x2buf, DC)]:
                sy.inc(SY.dma_start(hbm3(t, col0, nch, w),
                                    bass.AP(zf, 0, [[512, P], [0, nch], [1, w]])), 16)
            bar()
            sta = ExitStack()
            with sta:
                abuf = make_stage_a_bufs(sta)
                quant_weights(w1T, DC, HD, wq, wq1buf, rw1c, abuf[5], rw_scale=1.0)
                sv.wait(V)
                V.tensor_scalar(convws[:], convws[:], rw1c[:, 0:1], None, ALU.mult)
                sv.inc(V.drain())
                bar()
                ffn_stage_a(wq, abuf, xT, FF1_A, yqbuf)
            stb = ExitStack()
            with stb:
                xt = sb("xt", [P, DC * 512], F32, stb)
                yy = sb("yy", [P, DC * 512], F32, stb)
                yqb = sb("yqb", [P, HC * 512], BF16, stb)
                quant_weights(w2T, HC, D, wq, wq2buf, rw2h, yy, rw_scale=1.0)
                sv.wait(V)
                V.tensor_scalar(rw2h[:], rw2h[:], 0.5, None, ALU.mult)
                sv.inc(V.drain())
                bar()
                ffn_stage_b(wq, xt, yy, yqb, xT, FF1_B, yqbuf, x1buf)

        carry1 = ExitStack()
        with carry1:
            cfc = sb("cfc", [P, 2 * CPAR], BF16, carry1)
            carry2 = ExitStack()
            with carry2:
                swin = sb("swin", [P, WLEN], BF16, carry2)
                if True:

                    mid = ExitStack()
                    with mid:
                        xt = sb("xt", [P, DC * 512], F32, mid)
                        xbb = sb("xbb", [P, DC * 512], BF16, mid)
                        sqbb = sb("sqbb", [P, DC * 512], BF16, mid)
                        yy = sb("yy", [P, DC * 512], F32, mid)
                        xq = sb("xq", [P, DC * 512], BF16, mid)
                        Bsb = sb("Bsb", [P, S], F32, mid)
                        states = sb("states", [P, YPAD + S + YPAD], BF16, mid)
                        BwTs = sb("BwTs", [P, DC * NS], BF16, mid)

                        V.memset(states[:], 0.0)
                        V.memset(cfc[:], 0.0)
                        sv.inc(V.drain())
                        for kd in range(DC):
                            sg.inc(G.dma_start(BwTs[:, kd * NS:(kd + 1) * NS],
                                               BwT[kd * P:(kd + 1) * P, :]), 16)
                        sy.wait(SC)
                        sa.inc(SC.activation(alcol[:, 0:1], alcol[:, 0:1], AF.Sigmoid))
                        sa.wait(SC)
                        sa.inc(SC.activation(alcol[:, 1:2], alcol[:, 0:1], AF.Ln))
                        bar()

                        for ti, c0 in enumerate(MID_T):
                            tw = 512
                            sv.wait(SY)
                            sa.wait(SY)
                            dma_fence()
                            sy.inc(SY.dma_start(sb3(xt, DC, tw, tw), hbm3(x1buf, c0, DC, tw)), 16)
                            sy.wait(V)
                            stats_rows(xt, DC, tw, xbb, sqbb)
                            ln_apply(xt, yy, DC, tw)
                            sv.wait(SY)
                            sy.inc(SY.dma_start(hbm3(ybuf, c0 - 128, DC, tw),
                                                sb3(yy, DC, tw, tw)), 16)
                            stats_rows(yy, DC, tw, xbb, sqbb)
                            ln_apply(yy, xt, DC, tw)
                            for c in range(DC):
                                V.tensor_scalar(xq[:, c * tw:(c + 1) * tw],
                                                xt[:, c * tw:(c + 1) * tw],
                                                gcol[:, c:c + 1], bcolv[:, c:c + 1],
                                                ALU.mult, ALU.add)
                            sv.inc(V.drain())
                            sv.wait(T)
                            sg.wait(T)
                            sev.wait(T, sev.n - 1)
                            for kd in range(DC):
                                mm = T.matmul(pb[ti % 2][:, 0:tw],
                                              BwTs[:, kd * NS:(kd + 1) * NS],
                                              xq[:, kd * tw:(kd + 1) * tw],
                                              start=(kd == 0), stop=(kd == DC - 1))
                            sp.inc(mm)
                            sp.wait(V)
                            sev.inc(V.tensor_scalar(Bsb[:, c0 - 256:c0 - 256 + tw],
                                                    pb[ti % 2][:, 0:tw], bbs[:, 0:1],
                                                    None, ALU.add))
                            sv.inc(V.engine_nop())
                        sa.wait(V)
                        sv.wait(V)
                        sy.wait(V)
                        sa.wait(G)
                        sv.wait(G)
                        sp.wait(G)
                        G.iota(xt[:, 0:S], [[1, S]], channel_multiplier=0,
                               allow_small_or_imprecise_dtypes=True)
                        su.inc(G.engine_nop())
                        su.wait(V)
                        V.tensor_scalar(xt[:, 0:S], xt[:, 0:S], alcol[:, 1:2], None,
                                        ALU.mult)
                        V.drain()
                        V.tensor_scalar(xt[:, 0:S], xt[:, 0:S], LN1E8, 0.0,
                                        ALU.add, ALU.min)
                        sv.inc(V.drain())
                        sv.wait(SC)
                        sa.inc(SC.activation(xt[:, 0:S], xt[:, 0:S], AF.Exp))
                        sa.wait(V)
                        V.tensor_tensor(Bsb[:, 0:S], Bsb[:, 0:S], xt[:, 0:S], ALU.mult)
                        V.drain()
                        V.memset(yy[:, 0:S], 1.0)
                        V.drain()
                        V.tensor_scalar(yy[:, 0:S], yy[:, 0:S], alcol[:, 0:1], None,
                                        ALU.mult)
                        V.drain()
                        V.tensor_tensor_scan(states[:, YPAD:YPAD + S], yy[:, 0:S],
                                             Bsb[:, 0:S], 0.0, ALU.mult, ALU.add)
                        V.drain()
                        with V.register("wo") as wo:
                            V.reg_load(wo, woffs[0:1, 0:1])
                            woi = V.snap(wo, min_val=0, max_val=YPAD + S + YPAD - WLEN)
                            sv.inc(V.tensor_copy(swin[:, 0:WLEN], states[:, ds(woi, WLEN)]))
                        if debug:
                            sv.wait(SY)
                            sy.inc(SY.dma_start(dbg3[:, :], states[:, :]), 16)
                        bar()

                carry3 = ExitStack()
                with carry3:
                    Usb = sb("Usb", [P, MC * WLEN], BF16, carry3)
                    uph = ExitStack()
                    with uph:
                        CwTs = sb("CwTs", [P, D], BF16, uph)
                        WxS = sb("WxS", [P, DC * M3], BF16, uph)
                        gbc = sb("gbc", [P, MC], F32, uph)
                        umsk = sb("umsk", [P, MC * 512], BF16, uph)
                        sow = sb("sow", [P, DC * 512], BF16, uph)
                        wsc2 = sb("wsc2", [P, 2048], F32, uph)
                        gbs = sb("gbs", [P, 16], F32, uph)

                        sg.inc(G.dma_start(CwTs[:], CwT), 16)
                        sg.inc(G.dma_start(umsk[:], umaski), 16)
                        sy.inc(SY.dma_start(gbs[:], gb16), 16)
                        sy.wait(V)
                        V.tensor_copy(gbc[:, 0:8], gbs[:, 0:8])
                        sv.inc(V.scalar_tensor_tensor(gbc[:, 8:12], gbs[:, 8:12], 0.1,
                                                      gbs[:, 12:16], ALU.mult, ALU.add))
                        for kd in range(DC):
                            sv.wait(SY)
                            sy.inc(SY.dma_start(wsc2[:], Wx4[kd * P:(kd + 1) * P, :]), 16)
                            sy.wait(V)
                            V.tensor_copy(WxS[:, kd * M3:kd * M3 + 1024], wsc2[:, 0:1024])
                            sv.inc(V.scalar_tensor_tensor(
                                WxS[:, kd * M3 + 1024:(kd + 1) * M3],
                                wsc2[:, 1024:1536], 0.1, wsc2[:, 1536:2048],
                                ALU.mult, ALU.add))
                        bar()
                        for ti, (wi0, tn) in enumerate(UW_T):
                            sv.wait(T)
                            for dc_ in range(DC):
                                bank = pb[dc_ % 2]
                                sev.wait(T, sev.n - 1)
                                sp.inc(T.matmul(bank[:, 0:tn],
                                                CwTs[:, dc_ * P:(dc_ + 1) * P],
                                                swin[:, wi0:wi0 + tn],
                                                start=True, stop=True))
                                sp.wait(V)
                                ev = V.tensor_scalar(sow[:, dc_ * 512:dc_ * 512 + tn],
                                                     bank[:, 0:tn], cbs[:, dc_:dc_ + 1],
                                                     None, ALU.add)
                                sev.inc(ev)
                            sv.inc(V.engine_nop())
                            sv.wait(T)
                            for mc_ in range(MC):
                                bank = pb[2 + mc_ % 2]
                                sev.wait(T, sev.n - 1)
                                for kd in range(DC):
                                    mm = T.matmul(
                                        bank[:, 0:tn],
                                        WxS[:, kd * M3 + mc_ * P:kd * M3 + (mc_ + 1) * P],
                                        sow[:, kd * 512:kd * 512 + tn],
                                        start=(kd == 0), stop=(kd == DC - 1))
                                sp.inc(mm)
                                sp.wait(V)
                                dstu = Usb[:, mc_ * WLEN + wi0:mc_ * WLEN + wi0 + tn]
                                if ti == 0:
                                    ev = V.scalar_tensor_tensor(
                                        dstu, bank[:, 0:tn], gbc[:, mc_:mc_ + 1],
                                        umsk[:, mc_ * 512:mc_ * 512 + tn],
                                        ALU.add, ALU.mult)
                                else:
                                    ev = V.tensor_scalar(dstu, bank[:, 0:tn],
                                                         gbc[:, mc_:mc_ + 1], None, ALU.add)
                                sev.inc(ev)
                            sv.inc(V.engine_nop())
                        if debug:
                            sv.wait(SY)
                            sy.inc(SY.dma_start(dbg2[:, :], Usb[:, :]), 16)
                        bar()

                    pic = ExitStack()
                    with pic:
                        WhS = sb("WhS", [P, KC * M3], BF16, pic)
                        wsc3 = sb("wsc3", [P, 2048], F32, pic)
                        tfb = sb("tfb", [P, 2 * 512], F32, pic)
                        tzb = sb("tzb", [P, 512], F32, pic)
                        sgbuf = sb("sgbuf", [P, 512], F32, pic)
                        tfbB = sb("tfbB", [P, 2 * 512], F32, pic)
                        tzbB = sb("tzbB", [P, 512], F32, pic)
                        sgbufB = sb("sgbufB", [P, 512], F32, pic)
                        dtb = sb("dtb", [P, 512], F32, pic)
                        ptb = sb("ptb", [P, 512], F32, pic)

                        for kk in range(KC):
                            sv.wait(SY)
                            dma_fence()
                            sy.inc(SY.dma_start(wsc3[:], Wh4[kk * P:(kk + 1) * P, :]), 16)
                            sy.wait(V)
                            V.tensor_copy(WhS[:, kk * M3:kk * M3 + 1024], wsc3[:, 0:1024])
                            sv.inc(V.scalar_tensor_tensor(WhS[:, kk * M3 + 1024:(kk + 1) * M3],
                                                          wsc3[:, 1024:1536], 0.1,
                                                          wsc3[:, 1536:2048], ALU.mult, ALU.add))
                        bar()
                        NGRP = len(UW_T) * KC
                        hw_marks = [sv.n]
                        ac_marks = []
                        g_all = 0
                        pend = []
                        for k in range(NSWEEP):
                            parR = (k % 2) * CPAR
                            parW = ((k + 1) % 2) * CPAR
                            for ti, (wi0, tn) in enumerate(UW_T):
                                for kap in range(KC):
                                    trip = ptrip[g_all % 2]
                                    T.wait_ge(sv.h, hw_marks[k])
                                    if g_all >= 2:
                                        sa.wait(T, ac_marks[g_all - 2])
                                    for gi, mpart in enumerate((kap, 4 + kap, 8 + kap)):
                                        T.matmul(trip[gi][:, 0:tn], idsb[:],
                                                 Usb[:, mpart * WLEN + wi0:
                                                     mpart * WLEN + wi0 + tn],
                                                 start=True, stop=False)
                                        for kk in range(KC):
                                            mm = T.matmul(
                                                trip[gi][:, 0:tn],
                                                WhS[:, kk * M3 + mpart * P:kk * M3 + (mpart + 1) * P],
                                                cfc[:, parR + kk * NCOL + wi0:
                                                     parR + kk * NCOL + wi0 + tn],
                                                start=False, stop=(kk == KC - 1))
                                    sp.inc(mm)
                                    sp.wait(SC)
                                    sv.wait(SC)
                                    tf_, tz_, sg2_ = ((tfb, tzb, sgbuf) if g_all % 2 == 0
                                                      else (tfbB, tzbB, sgbufB))
                                    SC.activation(tf_[:, 0:tn], trip[0][:, 0:tn], AF.Tanh)
                                    SC.activation(tf_[:, 512:512 + tn], trip[1][:, 0:tn], AF.Tanh)
                                    SC.activation(sg2_[:, 0:tn], trip[2][:, 0:tn], AF.Sigmoid)
                                    sa.inc(SC.activation(tz_[:, 0:tn], trip[2][:, 0:tn],
                                                         AF.Sigmoid, scale=cneg1[:, 0:1]))
                                    ac_marks.append(sa.n)
                                    sa.wait(V)
                                    da, pa = ((dtb, ptb) if g_all % 2 == 0
                                              else (hmt, hmt2))
                                    V.tensor_tensor(da[:, 0:tn], tf_[:, 0:tn],
                                                    tz_[:, 0:tn], ALU.mult)
                                    V.tensor_tensor(pa[:, 0:tn], tf_[:, 512:512 + tn],
                                                    sg2_[:, 0:tn], ALU.mult)
                                    if g_all % 2 == 1 or (ti == len(UW_T) - 1
                                                          and kap == KC - 1):
                                        V.drain()
                                        for (pw, da_, pa_, tn_) in pend:
                                            sv.inc(V.tensor_tensor(
                                                cfc[:, pw:pw + tn_], da_[:, 0:tn_],
                                                pa_[:, 0:tn_], ALU.add))
                                        sv.inc(V.tensor_tensor(
                                            cfc[:, parW + kap * NCOL + wi0 + 1:
                                                 parW + kap * NCOL + wi0 + 1 + tn],
                                            da[:, 0:tn], pa[:, 0:tn], ALU.add))
                                        pend = []
                                    else:
                                        pend.append((parW + kap * NCOL + wi0 + 1,
                                                     da, pa, tn))
                                    g_all += 1
                            hw_marks.append(sv.n)
                        bar()
                if debug:
                    sy.inc(SY.dma_start(dbgb[:, :], cfc[:, :]), 16)
                    bar()

            op = ExitStack()
            with op:
                owTs = sb("owTs", [P, KC * D], BF16, op)
                x1a = sb("x1a", [P, DC * 512], F32, op)
                x1b = sb("x1b", [P, DC * 512], F32, op)
                ya = sb("ya", [P, DC * 512], F32, op)
                yb = sb("yb", [P, DC * 512], F32, op)
                x2ma = sb("x2ma", [P, 512], F32, op)
                x2mr = sb("x2mr", [P, 128], F32, op)
                x2t = sb("x2t", [P, DC * 512], F32, op)

                for kk in range(KC):
                    sg.inc(G.dma_start(owTs[:, kk * D:(kk + 1) * D],
                                       owT[kk * P:(kk + 1) * P, :]), 16)
                sy.inc(SY.dma_start(x2ma[:], x2m0i), 16)
                sy.inc(SY.dma_start(x2mr[:], x2m4i), 16)
                bar()
                for ti, (wi0, tn) in enumerate(UW_T):
                    sv.wait(SY)
                    dma_fence()
                    sy.inc(SY.dma_start(sb3(x1a, DC, tn, 512), hbm3(x1buf, 130 + wi0, DC, tn)), 16)
                    sy.inc(SY.dma_start(sb3(x1b, DC, tn, 512), hbm3(x1buf, 2178 + wi0, DC, tn)), 16)
                    sy.inc(SY.dma_start(sb3(ya, DC, tn, 512), hbm3(ybuf, 2 + wi0, DC, tn)), 16)
                    sy.inc(SY.dma_start(sb3(yb, DC, tn, 512), hbm3(ybuf, 2050 + wi0, DC, tn)), 16)
                    sy.wait(V)
                    sg.wait(T)
                    for dc_ in range(DC):
                        bank = pb[dc_ % 2]
                        sev.wait(T, sev.n - 1)
                        for kk in range(KC):
                            base = kk * NCOL + wi0 + 1
                            mm = T.matmul(bank[:, 0:tn],
                                          owTs[:, kk * D + dc_ * P:kk * D + (dc_ + 1) * P],
                                          cfc[:, base:base + tn],
                                          start=(kk == 0), stop=(kk == KC - 1))
                        sp.inc(mm)
                        sp.wait(V)
                        co = dc_ * 512
                        V.tensor_tensor(hmt[:, 0:tn], x1a[:, co:co + tn], ya[:, co:co + tn],
                                        ALU.add)
                        V.tensor_tensor(hmt2[:, 0:tn], x1b[:, co:co + tn], yb[:, co:co + tn],
                                        ALU.add)
                        ev = V.tensor_scalar(x2t[:, co:co + tn], bank[:, 0:tn],
                                             obs[:, dc_:dc_ + 1], None, ALU.add)
                        sev.inc(ev)
                        V.drain()
                        V.tensor_scalar(hmt[:, 0:tn], hmt[:, 0:tn], mks[:, 0:1], None,
                                        ALU.mult)
                        V.drain()
                        V.scalar_tensor_tensor(hmt[:, 0:tn], hmt2[:, 0:tn], mks[:, 1:2],
                                               hmt[:, 0:tn], ALU.mult, ALU.add)
                        V.drain()
                        V.tensor_tensor(x2t[:, co:co + tn], x2t[:, co:co + tn],
                                        hmt[:, 0:tn], ALU.add)
                        if ti == 0:
                            V.drain()
                            V.tensor_tensor(x2t[:, co:co + tn], x2t[:, co:co + tn],
                                            x2ma[:, 0:tn], ALU.mult)
                        elif ti == 4:
                            V.drain()
                            V.tensor_tensor(x2t[:, co:co + tn], x2t[:, co:co + tn],
                                            x2mr[:, 0:tn], ALU.mult)
                        sv.inc(V.drain())
                    sv.wait(SY)
                    sy.inc(SY.dma_start(hbm3(x2buf, X2OFF + wi0, DC, tn),
                                        sb3(x2t, DC, tn, 512)), 16)
                bar()

        ffn2 = ExitStack()
        with ffn2:
            wq = sb("wq", [P, DC * HD], BF16, ffn2)
            V.memset(a2col[:], 0.0)
            sv.inc(V.drain())
            for kd in range(DC):
                sg.inc(G.dma_start(wq[:, kd * HD:(kd + 1) * HD],
                                   wq1buf[kd * P:(kd + 1) * P, :]), 16)
            bar()
            sta = ExitStack()
            with sta:
                abuf = make_stage_a_bufs(sta)
                V.memset(abuf[5][:], 0.0)
                sv.inc(V.drain())
                bar()
                ffn_stage_a(wq, abuf, x2buf, FF2_A, yq2buf)
            stb = ExitStack()
            with stb:
                xt = sb("xt", [P, DC * 512], F32, stb)
                yy = sb("yy", [P, DC * 512], F32, stb)
                yqb = sb("yqb", [P, HC * 512], BF16, stb)
                for kh in range(HC):
                    sg.inc(G.dma_start(wq[:, kh * D:(kh + 1) * D],
                                       wq2buf[kh * P:(kh + 1) * P, :]), 16)
                bar()
                ffn_stage_b(wq, xt, yy, yqb, x2buf, FF2_B, yq2buf, outT)

        dma_fence()
        bar()
    return nc


def _prep_inputs(inputs):
    f32 = np.float32
    x = np.asarray(inputs["L0_attn_x"], f32)
    w1 = np.asarray(inputs["L0_ffn_w1"], f32)
    w2 = np.asarray(inputs["L0_ffn_w2"], f32)
    cw = np.asarray(inputs["L0_ffn_conv_w"], f32)
    cb = np.asarray(inputs["L0_ffn_conv_b"], f32)
    Bw = np.asarray(inputs["L0_attn_B_w"], f32)
    Bbv = np.asarray(inputs["L0_attn_B_b"], f32)
    Al = np.asarray(inputs["L0_attn_A_log"], f32)
    Cw = np.asarray(inputs["L0_attn_C_w"], f32)
    Cbv = np.asarray(inputs["L0_attn_C_b"], f32)
    ng = np.asarray(inputs["L0_attn_norm_g"], f32)
    nb = np.asarray(inputs["L0_attn_norm_b"], f32)
    f1w = np.asarray(inputs["L0_attn_ff1_w"], f32)
    f2w = np.asarray(inputs["L0_attn_ff2_w"], f32)
    taw = np.asarray(inputs["L0_attn_ta_w"], f32)
    tbw = np.asarray(inputs["L0_attn_tb_w"], f32)
    f1b = np.asarray(inputs["L0_attn_ff1_b"], f32)
    f2b = np.asarray(inputs["L0_attn_ff2_b"], f32)
    tab = np.asarray(inputs["L0_attn_ta_b"], f32)
    tbb = np.asarray(inputs["L0_attn_tb_b"], f32)
    ow = np.asarray(inputs["L0_attn_out_w"], f32)
    obv = np.asarray(inputs["L0_attn_out_b"], f32)

    shared = {
        "w1T": np.ascontiguousarray(w1.T),
        "w2T": np.ascontiguousarray(w2.T),
        "convw": np.ascontiguousarray(
            cw[:, 0, :].reshape(HC, P, 3).transpose(1, 0, 2).reshape(P, HC * 3)),
        "convb": np.ascontiguousarray(cb.reshape(HC, P).T),
        "normg": np.ascontiguousarray(ng.reshape(DC, P).T),
        "normb": np.ascontiguousarray(nb.reshape(DC, P).T),
        "BwT": np.ascontiguousarray(Bw.T),
        "Bb": np.ascontiguousarray(Bbv.reshape(P, 1)),
        "Alog": np.ascontiguousarray(Al.reshape(P, 1)),
        "CwT": np.ascontiguousarray(Cw.T),
        "Cb": np.ascontiguousarray(Cbv.reshape(DC, P).T),
        "Wx4": np.ascontiguousarray(np.concatenate(
            [f1w[:, :D].T, f2w[:, :D].T, taw[:, :D].T, tbw[:, :D].T], axis=1)),
        "Wh4": np.ascontiguousarray(np.concatenate(
            [f1w[:, D:].T, f2w[:, D:].T, taw[:, D:].T, tbw[:, D:].T], axis=1)),
        "gb16": np.ascontiguousarray(np.concatenate(
            [f1b.reshape(KC, P).T, f2b.reshape(KC, P).T,
             tab.reshape(KC, P).T, tbb.reshape(KC, P).T], axis=1)),
        "owT": np.ascontiguousarray(ow.T),
        "ob": np.ascontiguousarray(obv.reshape(DC, P).T),
        "ident": np.eye(P, dtype=f32),
    }
    in_maps = []
    for c in range(8):
        b, hf = c // 2, c % 2
        s = 2048 * hf
        xTc = np.zeros((D, XWX), f32)
        xTc[:, 256:256 + S] = x[b].T
        masks_ = np.zeros((P, 8), f32)
        masks_[:, 0] = 1.0 - hf
        masks_[:, 1] = float(hf)
        umask_ = np.ones((P, MC * 512), f32)
        if hf == 0:
            for m in range(MC):
                umask_[:, m * 512:m * 512 + 126] = 0.0
        x2m0_ = np.ones((P, 512), f32)
        if hf == 0:
            x2m0_[:, 0:126] = 0.0
        x2m4_ = np.ones((P, 128), f32)
        if hf == 1:
            x2m4_[:, 126:128] = 0.0
        d = dict(shared)
        d.update({
            "xT": xTc,
            "masks": masks_,
            "umask": umask_,
            "x2m0": x2m0_,
            "x2m4": x2m4_,
            "woff": np.array([[s + 2]], np.uint32),
        })
        in_maps.append(d)
    return in_maps


def kernel(**inputs):
    in_maps = _prep_inputs(inputs)
    nc = build_nc(debug=False)
    res = run_bass_kernel_spmd(nc, in_maps, core_ids=list(range(8)))
    out = np.empty((4, S, D), np.float32)
    for c in range(8):
        b, hf = c // 2, c % 2
        o = np.asarray(res.results[c]["outT"])
        out[b, 2048 * hf:2048 * (hf + 1), :] = o[:, OUT_LO:OUT_HI].T
    return out


# version 09
# speedup vs baseline: 1.1620x; 1.1620x over previous
import numpy as np
import ml_dtypes
import concourse.bass as bass
import concourse.mybir as mybir
from concourse.bass_utils import run_bass_kernel_spmd
from contextlib import ExitStack

F32 = mybir.dt.float32
BF16 = mybir.dt.bfloat16
U32 = mybir.dt.uint32
AF = mybir.ActivationFunctionType
ALU = mybir.AluOpType
AX = mybir.AxisListType
ds = bass.ds

P = 128
S = 4096
D = 1024
DC = 8
HD = 4096
HC = 32
NS = 128
H = 512
KC = 4
M3 = 1536
MC = 12
XW = 4608
WLEN = 2176
NCOL = WLEN + 1
CPAR = KC * NCOL
NSWEEP = 18
YPAD = 128
LN_EPS = 1e-5
C32c = 1.5 * 2.0 ** 23
C64c = 1.5 * 2.0 ** 52
LN1E8 = 18.420680743952367

TOA = 256
TWA = TOA + 2
XWX = 4640
FF1_A = [128 + TOA * j for j in range(17)]
FF1_B = [512 * j for j in range(9)]
MID_T = [256 + 512 * j for j in range(8)]
UW_T = [(0, 512), (512, 512), (1024, 512), (1536, 512), (2048, 128)]
X2W = 2688
X2OFF = 192
FF2_A = [256 + TOA * j for j in range(9)]
FF2_B = [512 * j for j in range(5)]
OUT_LO, OUT_HI = 318, 2366


class Sem:
    def __init__(self, stack, nc, name):
        self.h = stack.enter_context(nc.semaphore(name))
        self.n = 0

    def inc(self, instr, by=1):
        instr.then_inc(self.h, by)
        self.n += by
        return instr

    def wait(self, eng, val=None):
        v = self.n if val is None else val
        if v > 0:
            eng.wait_ge(self.h, v)

    def mark(self):
        return self.n


def build_nc(debug=False):
    nc = bass.Bass("TRN2", target_bir_lowering=False)
    A = lambda name, shape, dt=F32: nc.dram_tensor(name, shape, dt, kind="ExternalInput").ap()
    xT = A("xT", [D, XWX])
    w1T = A("w1T", [D, HD])
    w2T = A("w2T", [HD, D])
    convw = A("convw", [P, 96])
    convb = A("convb", [P, HC])
    normg = A("normg", [P, DC])
    normb = A("normb", [P, DC])
    BwT = A("BwT", [D, NS])
    Bbi = A("Bb", [P, 1])
    Alog = A("Alog", [P, 1])
    CwT = A("CwT", [P, D])
    Cbi = A("Cb", [P, DC])
    Wx4 = A("Wx4", [D, 2048])
    Wh4 = A("Wh4", [H, 2048])
    gb16 = A("gb16", [P, 16])
    owT = A("owT", [H, D])
    obi = A("ob", [P, DC])
    ident = A("ident", [P, P])
    masks = A("masks", [P, 8])
    umaski = A("umask", [P, MC * 512])
    x2m0i = A("x2m0", [P, 512])
    x2m4i = A("x2m4", [P, 128])
    woff = A("woff", [1, 1], U32)

    x1buf = nc.dram_tensor("x1buf", [D, XW], F32, kind="Internal").ap()
    ybuf = nc.dram_tensor("ybuf", [D, S + 2 * YPAD], F32, kind="Internal").ap()
    yqbuf = nc.dram_tensor("yqbuf", [HD, XW], BF16, kind="Internal").ap()
    x2buf = nc.dram_tensor("x2buf", [D, X2W], F32, kind="Internal").ap()
    yq2buf = nc.dram_tensor("yq2buf", [HD, X2W], BF16, kind="Internal").ap()
    wq1buf = nc.dram_tensor("wq1buf", [D, HD], BF16, kind="Internal").ap()
    wq2buf = nc.dram_tensor("wq2buf", [HD, D], BF16, kind="Internal").ap()
    outT = nc.dram_tensor("outT", [D, X2W], F32, kind="ExternalOutput").ap()
    if debug:
        dbgb = nc.dram_tensor("dbgb", [P, 2 * CPAR], BF16, kind="ExternalOutput").ap()
        dbg2 = nc.dram_tensor("dbg2", [P, MC * WLEN], BF16, kind="ExternalOutput").ap()
        dbg3 = nc.dram_tensor("dbg3", [P, YPAD + S + YPAD], BF16, kind="ExternalOutput").ap()

    st = ExitStack()
    _nm = [0]
    with st:
        def sb(name, shape, dt=F32, stk=None):
            _nm[0] += 1
            return (stk or st).enter_context(nc.sbuf_tensor(f"{name}_{_nm[0]}", shape, dt))

        def ps_(name):
            return st.enter_context(nc.psum_tensor(name, [P, 512], F32))

        onesb = sb("onesb", [P, 1], BF16)
        onesf = sb("onesf", [P, 1])
        idsf = sb("idsf", [P, P])
        idneg = sb("idneg", [P, P])
        cmag = sb("cmag", [P, 1])
        cnmag = sb("cnmag", [P, 1])
        halfc = sb("halfc", [P, 1])
        epsc = sb("epsc", [P, 1])
        rw1c = sb("rw1c", [P, 1])
        rw2h = sb("rw2h", [P, 1])
        swcol = sb("swcol", [P, 1])
        convws = sb("convws", [P, 96])
        convbs = sb("convbs", [P, HC])
        gcol = sb("gcol", [P, DC])
        bcolv = sb("bcolv", [P, DC])
        cbs = sb("cbs", [P, DC])
        obs = sb("obs", [P, DC])
        bbs = sb("bbs", [P, 1])
        mks = sb("mks", [P, 8])
        alcol = sb("alcol", [P, 2])
        tcol = sb("tcol", [P, 8])
        stg = sb("stg", [P, 8])
        r11 = sb("r11", [1, 8])
        statr = sb("statr", [1, 4 * 512])
        ones128 = sb("ones128", [P, P])
        repb = sb("repb", [P, 4 * P])
        a2col = sb("a2col", [P, 40])
        cm = sb("cm", [P, 512])
        cmt = sb("cmt", [P, 512])
        zsb = sb("zsb", [P, 512], BF16)
        woffs = sb("woffs", [1, 1], U32)
        hmt = sb("hmt", [P, 512])
        hmt2 = sb("hmt2", [P, 512])
        a1s = sb("a1s", [P, 512])
        onesr = sb("onesr", [1, P])
        idsb = sb("idsb", [P, P], BF16)
        cneg1 = sb("cneg1", [P, 1])

        pb = [ps_(f"pb{i}") for i in range(4)]
        pbc = ps_("pbc")
        pbc2 = ps_("pbc2")
        pst = ps_("pst")
        pst2 = ps_("pst2")
        ptrip = [[pb[0], pb[1], pb[2]], [pb[3], pst, pst2]]

        sy = Sem(st, nc, "sy")
        sg = Sem(st, nc, "sg")
        sv = Sem(st, nc, "sv")
        sa = Sem(st, nc, "sa")
        sp = Sem(st, nc, "sp")
        sev = Sem(st, nc, "sev")
        su = Sem(st, nc, "su")

        V, SC, T, G, SY = nc.vector, nc.scalar, nc.tensor, nc.gpsimd, nc.sync

        def tnop_inc(sem):
            T.drain()
            sem.inc(T.nop())

        def dma_fence():
            SY.wait_ge(sy.h, sy.n) if sy.n else None
            G.wait_ge(sg.h, sg.n) if sg.n else None

        def bar():
            dma_fence()
            nc.all_engine_barrier()

        sy.inc(SY.dma_start(idsf[:], ident), 16)
        sy.inc(SY.dma_start(convws[:], convw), 16)
        sy.inc(SY.dma_start(convbs[:], convb), 16)
        sy.inc(SY.dma_start(gcol[:], normg), 16)
        sy.inc(SY.dma_start(bcolv[:], normb), 16)
        sy.inc(SY.dma_start(cbs[:], Cbi), 16)
        sy.inc(SY.dma_start(obs[:], obi), 16)
        sy.inc(SY.dma_start(bbs[:], Bbi), 16)
        sy.inc(SY.dma_start(mks[:], masks), 16)
        sy.inc(SY.dma_start(woffs[:], woff), 16)
        sy.inc(SY.dma_start(alcol[:, 0:1], Alog), 16)
        sy.wait(V)
        V.memset(onesf[:], 1.0)
        V.memset(onesr[:], 1.0)
        V.memset(ones128[:], 1.0)
        V.memset(a2col[:], 0.0)
        V.memset(tcol[:], 1.0)
        V.memset(stg[:], 1.0)
        V.memset(cm[:], 0.0)
        V.memset(cmt[:], 0.0)
        V.memset(statr[:], 0.0)
        V.memset(hmt[:], 0.0)
        V.memset(hmt2[:], 0.0)
        V.memset(a1s[:], 0.0)
        V.memset(repb[:], 0.0)
        V.memset(halfc[:], 0.5)
        V.memset(epsc[:], LN_EPS)
        V.memset(zsb[:], 0.0)
        V.memset(cneg1[:], -1.0)
        V.drain()
        sy.wait(V)
        V.tensor_copy(idsb[:], idsf[:])
        V.tensor_scalar(idneg[:], idsf[:], -1.0, None, ALU.mult)
        V.drain()
        V.tensor_copy(onesb[:], onesf[:])
        V.drain()
        V.tensor_scalar(cmag[:], onesf[:], 2.0 ** -30, None, ALU.add)
        V.drain()
        V.tensor_scalar(cmag[:], cmag[:], -1.0, None, ALU.add)
        V.drain()
        V.tensor_scalar(cmag[:], cmag[:], 2.0 ** 30, None, ALU.mult)
        V.drain()
        V.tensor_scalar(cmag[:], cmag[:], C64c, C32c, ALU.mult, ALU.add)
        V.drain()
        V.tensor_scalar(cnmag[:], cmag[:], -1.0, None, ALU.mult)
        V.drain()
        bar()

        def hbm3(t, col0, nch, w):
            C = t.shape[1]
            return bass.AP(t.tensor, col0, [[C, P], [P * C, nch], [1, w]])

        def sb3(t, nch, w, tw, col0=0):
            return bass.AP(t, col0, [[nch * tw, P], [tw, nch], [1, w]])

        def stats_rows(src, nch, tw, xbb, sqbb):
            tot = nch * tw
            sp.wait(V)
            V.tensor_copy(xbb[:, 0:tot], src[:, 0:tot])
            sv.inc(V.tensor_tensor(sqbb[:, 0:tot], src[:, 0:tot], src[:, 0:tot], ALU.mult))
            sv.wait(T)
            for c in range(nch):
                T.matmul(pst[0:1, 0:tw], onesb[:, 0:1], xbb[:, c * tw:(c + 1) * tw],
                         start=(c == 0), stop=(c == nch - 1))
            for c in range(nch):
                mm = T.matmul(pst2[0:1, 0:tw], onesb[:, 0:1], sqbb[:, c * tw:(c + 1) * tw],
                              start=(c == 0), stop=(c == nch - 1))
            sp.inc(mm)
            sp.wait(V)
            mu = statr[0:1, 1024:1024 + tw]
            va = statr[0:1, 1536:1536 + tw]
            rr = statr[0:1, 0:tw]
            mr = statr[0:1, 512:512 + tw]
            V.tensor_scalar(mu, pst[0:1, 0:tw], 1.0 / (nch * P), None, ALU.mult)
            V.tensor_scalar(va, pst2[0:1, 0:tw], 1.0 / (nch * P), None, ALU.mult)
            V.drain()
            V.scalar_tensor_tensor(rr, pst[0:1, 0:tw], -1.0 / (nch * P), mu,
                                   ALU.mult, ALU.mult)
            V.drain()
            V.tensor_tensor(va, va, rr, ALU.add)
            sv.inc(V.drain())
            sv.wait(SC)
            sa.inc(SC.activation(rr, va, AF.Sqrt, bias=epsc[0:1, 0:1]))
            sa.wait(V)
            V.reciprocal(rr, rr)
            V.drain()
            V.tensor_tensor(mr, mu, rr, ALU.mult)
            sv.inc(V.drain())

        def ln_apply(src, dst, nch, tw):
            sv.wait(T)
            T.matmul(pbc[:, 0:tw], onesr[0:1, 0:P], statr[0:1, 0:tw], start=True, stop=True)
            T.matmul(pbc2[:, 0:tw], onesr[0:1, 0:P], statr[0:1, 512:512 + tw],
                     start=True, stop=True)
            tnop_inc(sp)
            sp.wait(V)
            for c in range(nch):
                V.tensor_tensor(dst[:, c * tw:(c + 1) * tw], src[:, c * tw:(c + 1) * tw],
                                pbc[:, 0:tw], ALU.mult)
            V.drain()
            for c in range(nch):
                V.tensor_tensor(dst[:, c * tw:(c + 1) * tw], dst[:, c * tw:(c + 1) * tw],
                                pbc2[:, 0:tw], ALU.subtract)
            sv.inc(V.drain())

        def quant_cols(src, nch, tw, pools, nblk, acol_dst=None, halves=1):
            ap3 = bass.AP(src.tensor if hasattr(src, "tensor") else src, 0,
                          [[nch * tw, P], [1, tw], [tw, nch]])
            sv.wait(V)
            sv.inc(V.tensor_reduce(cm[:, 0:tw], ap3, AX.X, ALU.max,
                                   apply_absolute_value=True))
            sv.wait(T)
            for j in range(nblk):
                w = min(128, tw - j * 128)
                T.transpose(pst[0:w, j * 128:j * 128 + 128],
                            cm[:, j * 128:j * 128 + w], idsf[:])
            tnop_inc(sp)
            sp.wait(V)
            for j in range(nblk):
                w = min(128, tw - j * 128)
                V.tensor_reduce(tcol[0:w, j:j + 1], pst[0:w, j * 128:(j + 1) * 128],
                                AX.X, ALU.max)
            V.drain()
            V.tensor_scalar(tcol[:, 0:nblk], tcol[:, 0:nblk], 1e-5, None, ALU.max)
            V.drain()
            if acol_dst is not None:
                V.tensor_copy(acol_dst, tcol[:, 0:nblk])
            sv.inc(V.reciprocal(stg[:, 0:nblk], tcol[:, 0:nblk]))
            sv.wait(SC)
            SC.drain()
            SC.activation(stg[:, 4:4 + nblk], tcol[:, 0:nblk], AF.Identity,
                          scale=1.0 / 127.0)
            sa.inc(SC.activation(stg[:, 0:nblk], stg[:, 0:nblk], AF.Identity,
                                 scale=127.0))

        def colbcast(psum, col_src, ncols, nblk):
            sp.wait(V)
            sa.wait(V)
            for j in range(nblk):
                op = V.tensor_scalar(repb[:, j * P:(j + 1) * P], ones128[:],
                                     col_src[:, j:j + 1], None, ALU.mult)
            sv.inc(op)
            sv.wait(T)
            for j in range(nblk):
                T.transpose(psum[0:P, j * 128:(j + 1) * 128],
                            repb[:, j * P:(j + 1) * P], idsf[:])
            tnop_inc(sp)
            return None

        def roundq_act(dst, src, tot):
            sa.inc(SC.activation(src[:, 0:tot], src[:, 0:tot], AF.Identity,
                                 bias=cmag[:, 0:1]))
            sa.wait(SC)
            SC.drain()
            sa.inc(SC.activation(dst[:, 0:tot], src[:, 0:tot], AF.Identity,
                                 bias=cnmag[:, 0:1]))

        def quant_weights(wsrc, nchunks, width, wq_t, wqhbm, rw_dst, wscb,
                          rw_scale=1.0 / 127.0):
            for c in range(nchunks):
                sv.wait(SY)
                dma_fence()
                sy.inc(SY.dma_start(wscb[:, 0:width], wsrc[c * P:(c + 1) * P, :]), 16)
                sy.wait(V)
                sv.inc(V.tensor_reduce(cm[:, c:c + 1], wscb[:, 0:width], AX.X,
                                       ALU.add, apply_absolute_value=True))
            V.drain()
            V.tensor_reduce(hmt[:, 0:1], cm[:, 0:nchunks], AX.X, ALU.add)
            sv.inc(V.drain())
            sv.wait(T)
            T.matmul(pst[0:1, 0:1], hmt[:, 0:1], onesf[:, 0:1], start=True, stop=True)
            tnop_inc(sp)
            sp.wait(V)
            V.tensor_scalar(r11[0:1, 0:1], pst[0:1, 0:1], 1.0 / (nchunks * P * width),
                            None, ALU.mult)
            V.drain()
            V.tensor_scalar(r11[0:1, 0:1], r11[0:1, 0:1], 1e-5, None, ALU.max)
            V.drain()
            V.reciprocal(r11[0:1, 1:2], r11[0:1, 0:1])
            V.drain()
            V.tensor_scalar(r11[0:1, 2:3], r11[0:1, 0:1], rw_scale, None, ALU.mult)
            sv.inc(V.drain())
            sv.wait(T)
            T.matmul(pst[0:P, 8:9], onesr[0:1, 0:P], r11[0:1, 1:2], start=True, stop=True)
            T.matmul(pst[0:P, 9:10], onesr[0:1, 0:P], r11[0:1, 2:3],
                     start=True, stop=True)
            tnop_inc(sp)
            sp.wait(V)
            V.tensor_copy(swcol[:, 0:1], pst[0:P, 8:9])
            V.tensor_copy(rw_dst[:, 0:1], pst[0:P, 9:10])
            sv.inc(V.drain())
            for c in range(nchunks):
                sv.wait(SY)
                sy.inc(SY.dma_start(wscb[:, 0:width], wsrc[c * P:(c + 1) * P, :]), 16)
                sy.wait(V)
                V.tensor_scalar(wscb[:, 0:width], wscb[:, 0:width], swcol[:, 0:1],
                                cmag[:, 0:1], ALU.mult, ALU.add)
                V.drain()
                V.tensor_scalar(wscb[:, 0:width], wscb[:, 0:width], cnmag[:, 0:1], 1.0,
                                ALU.add, ALU.min)
                V.drain()
                dst = wq_t[:, c * width:(c + 1) * width]
                sv.inc(V.tensor_scalar(dst, wscb[:, 0:width], -1.0, None, ALU.max))
                sv.wait(SY)
                sy.inc(SY.dma_start(wqhbm[c * P:(c + 1) * P, :], dst), 16)
            bar()

        def ffn_stage_a(wq, abuf, src_hbm, c0list, yq_hbm):
            xta, yya, xbb, sqbb, xq, y2, yqa, xqb, tg = abuf
            xqs = [xq, xqb]
            tw = TWA

            def prep_chunks(ii, c0):
                xq_t = xqs[ii % 2]

                def c_load():
                    sv.wait(SY)
                    sa.wait(SY)
                    dma_fence()
                    sy.inc(SY.dma_start(sb3(xta, DC, tw, tw),
                                        hbm3(src_hbm, c0 - 1, DC, tw)), 16)

                def c_stats():
                    sy.wait(V)
                    stats_rows(xta, DC, tw, xbb, sqbb)

                def c_ln():
                    ln_apply(xta, yya, DC, tw)

                def c_quant():
                    quant_cols(yya, DC, tw, None, 3)

                def c_smul():
                    colbcast(pbc, stg[:, 0:4], tw, 3)
                    sp.wait(V)
                    for c in range(DC):
                        V.tensor_tensor(yya[:, c * tw:(c + 1) * tw],
                                        yya[:, c * tw:(c + 1) * tw],
                                        pbc[:, 0:tw], ALU.mult)
                    sv.inc(V.drain())
                    sv.wait(SC)
                    roundq_act(yya, yya, DC * tw)

                def c_xq():
                    colbcast(pbc2, stg[:, 4:8], tw, 3)
                    sa.wait(V)
                    sp.wait(V)
                    for c in range(DC):
                        V.tensor_tensor(xq_t[:, c * tw:(c + 1) * tw],
                                        yya[:, c * tw:(c + 1) * tw],
                                        pbc2[:, 0:tw], ALU.mult)
                    sv.inc(V.drain())

                return [c_load, c_stats, c_ln, c_quant, c_smul, c_xq]

            nt = len(c0list)
            for ch in prep_chunks(0, c0list[0]):
                ch()
            for ti in range(nt):
                c0 = c0list[ti]
                xq_t = xqs[ti % 2]
                nxt = (prep_chunks(ti + 1, c0list[ti + 1]) if ti + 1 < nt
                       else [])
                sv.wait(T)
                for q in range(HC // 4):
                    for j in range(4):
                        mh = 4 * q + j
                        sev.wait(T, sev.n - 3 + j)
                        for kd in range(DC):
                            mm = T.matmul(pb[j][:, 0:tw],
                                          wq[:, kd * HD + mh * P:kd * HD + (mh + 1) * P],
                                          xq_t[:, kd * tw:(kd + 1) * tw],
                                          start=(kd == 0), stop=(kd == DC - 1))
                        sp.inc(mm)
                    sp.wait(SC)
                    SC.drain()
                    for j in range(4):
                        mh = 4 * q + j
                        a1 = SC.activation(tg[:, (mh % 8) * TOA:(mh % 8 + 1) * TOA],
                                           pb[j][:, 0:tw - 2], AF.Identity,
                                           bias=convbs[:, mh:mh + 1],
                                           scale=convws[:, 3 * mh:3 * mh + 1])
                    sa.inc(a1)
                    sa.wait(V)
                    for j in range(4):
                        mh = 4 * q + j
                        V.scalar_tensor_tensor(tg[:, (mh % 8) * TOA:(mh % 8 + 1) * TOA],
                                               pb[j][:, 1:tw - 1],
                                               convws[:, 3 * mh + 1:3 * mh + 2],
                                               tg[:, (mh % 8) * TOA:(mh % 8 + 1) * TOA],
                                               ALU.mult, ALU.add)
                    V.drain()
                    for j in range(4):
                        mh = 4 * q + j
                        ev = V.scalar_tensor_tensor(tg[:, (mh % 8) * TOA:(mh % 8 + 1) * TOA],
                                                    pb[j][:, 2:tw],
                                                    convws[:, 3 * mh + 2:3 * mh + 3],
                                                    tg[:, (mh % 8) * TOA:(mh % 8 + 1) * TOA],
                                                    ALU.mult, ALU.add)
                        sev.inc(ev)
                    sv.inc(V.engine_nop())
                    sv.wait(SC)
                    SC.drain()
                    for j in range(4):
                        mh = 4 * q + j
                        a2 = SC.activation(y2[:, mh * TOA:(mh + 1) * TOA],
                                           tg[:, (mh % 8) * TOA:(mh % 8 + 1) * TOA],
                                           AF.Silu)
                    sa.inc(a2)
                    if nxt:
                        nxt.pop(0)()
                sa.wait(V)
                quant_cols(y2, HC, TOA, None, 2,
                           acol_dst=a2col[:, c0 // 128:c0 // 128 + 2], halves=2)
                colbcast(pbc, stg[:, 0:4], TOA, 2)
                sp.wait(V)
                for c in range(HC):
                    V.tensor_tensor(y2[:, c * TOA:(c + 1) * TOA], y2[:, c * TOA:(c + 1) * TOA],
                                    pbc[:, 0:TOA], ALU.mult)
                sv.inc(V.drain())
                colbcast(pbc2, stg[:, 4:8], TOA, 2)
                hh = HC // 2
                for half in range(2):
                    sy.wait(SC)
                    sv.wait(SC)
                    y2h = bass.AP(y2, half * hh * TOA, [[HC * TOA, P], [1, hh * TOA]])
                    SC.drain()
                    SC.activation(y2h, y2h, AF.Identity, bias=cmag[:, 0:1])
                    SC.drain()
                    sa.inc(SC.activation(y2h, y2h, AF.Identity, bias=cnmag[:, 0:1]))
                    sa.wait(V)
                    sy.wait(V)
                    sp.wait(V)
                    for c in range(hh):
                        V.tensor_tensor(yqa[:, c * TOA:(c + 1) * TOA],
                                        bass.AP(y2, (half * hh + c) * TOA,
                                                [[HC * TOA, P], [1, TOA]]),
                                        pbc2[:, 0:TOA], ALU.mult)
                    sv.inc(V.drain())
                    sv.wait(SY)
                    dma_fence()
                    sy.inc(SY.dma_start(
                        hbm3(yq_hbm, c0, hh, TOA) if half == 0 else
                        bass.AP(yq_hbm.tensor, hh * P * yq_hbm.shape[1] + c0,
                                [[yq_hbm.shape[1], P], [P * yq_hbm.shape[1], hh], [1, TOA]]),
                        bass.AP(yqa, 0, [[hh * TOA, P], [TOA, hh], [1, TOA]])), 16)
            bar()

        def ffn_stage_b(wq, xt, yy, yqb, src_res_hbm, c0list, yq_hbm, out_hbm):
            for c0 in c0list:
                tw = 512
                sv.wait(SY)
                dma_fence()
                sy.inc(SY.dma_start(sb3(xt, DC, tw, tw), hbm3(src_res_hbm, c0, DC, tw)), 16)
                sy.inc(SY.dma_start(
                    bass.AP(yqb, 0, [[HC * tw, P], [tw, HC], [1, tw]]),
                    hbm3(yq_hbm, c0, HC, tw)), 16)
                colbcast(pbc2, a2col[:, c0 // 128:c0 // 128 + 4], tw, 4)
                sp.wait(V)
                V.tensor_copy(a1s[:, 0:tw], pbc2[:, 0:tw])
                sv.inc(V.drain())
                sy.wait(T)
                sy.wait(V)
                for md in range(DC):
                    bank = pb[md % 2]
                    hb = hmt if md % 2 == 0 else hmt2
                    sev.wait(T, sev.n - 1)
                    for kh in range(HC):
                        mm = T.matmul(bank[:, 0:tw],
                                      wq[:, kh * D + md * P:kh * D + (md + 1) * P],
                                      yqb[:, kh * tw:(kh + 1) * tw],
                                      start=(kh == 0), stop=(kh == HC - 1))
                    sp.inc(mm)
                    sp.wait(V)
                    ev = V.scalar_tensor_tensor(yy[:, md * tw:(md + 1) * tw],
                                                bank[:, 0:tw], rw2h[:, 0:1],
                                                xt[:, md * tw:(md + 1) * tw],
                                                ALU.mult, ALU.add)
                    sev.inc(ev)
                    sv.inc(V.engine_nop())
                sv.wait(SY)
                sy.inc(SY.dma_start(hbm3(out_hbm, c0, DC, tw), sb3(yy, DC, tw, tw)), 16)
            bar()

        def make_stage_a_bufs(stk):
            xta = sb("xta", [P, DC * TWA], F32, stk)
            yya = sb("yya", [P, DC * TWA], F32, stk)
            xbb = sb("xbb", [P, DC * TWA], BF16, stk)
            sqbb = sb("sqbb", [P, DC * TWA], BF16, stk)
            xq = sb("xq", [P, DC * TWA], BF16, stk)
            y2 = sb("y2", [P, HC * TOA], F32, stk)
            yqa = sb("yqa", [P, (HC // 2) * TOA], BF16, stk)
            xqb = sb("xqb", [P, DC * TWA], BF16, stk)
            tg = sb("tg", [P, 8 * TOA], F32, stk)
            return (xta, yya, xbb, sqbb, xq, y2, yqa, xqb, tg)

        ffn1 = ExitStack()
        with ffn1:
            wq = sb("wq", [P, DC * HD], BF16, ffn1)
            zf = sb("zf", [P, 512], F32, ffn1)

            for col0, w, t, nch in [(0, 130, yqbuf, HC), (4480, 128, yqbuf, HC),
                                    (0, 317, yq2buf, HC), (2560, 128, yq2buf, HC)]:
                sy.inc(SY.dma_start(hbm3(t, col0, nch, w),
                                    bass.AP(zsb, 0, [[512, P], [0, nch], [1, w]])), 16)
            V.memset(zf[:], 0.0)
            sv.inc(V.drain())
            sv.wait(SY)
            for col0, w, t, nch in [(0, YPAD, ybuf, DC), (S + YPAD, YPAD, ybuf, DC),
                                    (0, X2OFF, x2buf, DC), (2368, X2W - 2368, x2buf, DC)]:
                sy.inc(SY.dma_start(hbm3(t, col0, nch, w),
                                    bass.AP(zf, 0, [[512, P], [0, nch], [1, w]])), 16)
            bar()
            sta = ExitStack()
            with sta:
                abuf = make_stage_a_bufs(sta)
                quant_weights(w1T, DC, HD, wq, wq1buf, rw1c, abuf[5], rw_scale=1.0)
                sv.wait(V)
                V.tensor_scalar(convws[:], convws[:], rw1c[:, 0:1], None, ALU.mult)
                sv.inc(V.drain())
                bar()
                ffn_stage_a(wq, abuf, xT, FF1_A, yqbuf)
            stb = ExitStack()
            with stb:
                xt = sb("xt", [P, DC * 512], F32, stb)
                yy = sb("yy", [P, DC * 512], F32, stb)
                yqb = sb("yqb", [P, HC * 512], BF16, stb)
                quant_weights(w2T, HC, D, wq, wq2buf, rw2h, yy, rw_scale=1.0)
                sv.wait(V)
                V.tensor_scalar(rw2h[:], rw2h[:], 0.5, None, ALU.mult)
                sv.inc(V.drain())
                bar()
                ffn_stage_b(wq, xt, yy, yqb, xT, FF1_B, yqbuf, x1buf)

        carry1 = ExitStack()
        with carry1:
            cfc = sb("cfc", [P, 2 * CPAR], BF16, carry1)
            carry2 = ExitStack()
            with carry2:
                swin = sb("swin", [P, WLEN], BF16, carry2)
                if True:

                    mid = ExitStack()
                    with mid:
                        xt = sb("xt", [P, DC * 512], F32, mid)
                        xbb = sb("xbb", [P, DC * 512], BF16, mid)
                        sqbb = sb("sqbb", [P, DC * 512], BF16, mid)
                        yy = sb("yy", [P, DC * 512], F32, mid)
                        xq = sb("xq", [P, DC * 512], BF16, mid)
                        Bsb = sb("Bsb", [P, S], F32, mid)
                        states = sb("states", [P, YPAD + S + YPAD], BF16, mid)
                        BwTs = sb("BwTs", [P, DC * NS], BF16, mid)

                        V.memset(states[:], 0.0)
                        V.memset(cfc[:], 0.0)
                        sv.inc(V.drain())
                        for kd in range(DC):
                            sg.inc(G.dma_start(BwTs[:, kd * NS:(kd + 1) * NS],
                                               BwT[kd * P:(kd + 1) * P, :]), 16)
                        sy.wait(SC)
                        sa.inc(SC.activation(alcol[:, 0:1], alcol[:, 0:1], AF.Sigmoid))
                        sa.wait(SC)
                        sa.inc(SC.activation(alcol[:, 1:2], alcol[:, 0:1], AF.Ln))
                        bar()

                        for ti, c0 in enumerate(MID_T):
                            tw = 512
                            sv.wait(SY)
                            sa.wait(SY)
                            dma_fence()
                            sy.inc(SY.dma_start(sb3(xt, DC, tw, tw), hbm3(x1buf, c0, DC, tw)), 16)
                            sy.wait(V)
                            stats_rows(xt, DC, tw, xbb, sqbb)
                            ln_apply(xt, yy, DC, tw)
                            sv.wait(SY)
                            sy.inc(SY.dma_start(hbm3(ybuf, c0 - 128, DC, tw),
                                                sb3(yy, DC, tw, tw)), 16)
                            stats_rows(yy, DC, tw, xbb, sqbb)
                            ln_apply(yy, xt, DC, tw)
                            for c in range(DC):
                                V.tensor_scalar(xq[:, c * tw:(c + 1) * tw],
                                                xt[:, c * tw:(c + 1) * tw],
                                                gcol[:, c:c + 1], bcolv[:, c:c + 1],
                                                ALU.mult, ALU.add)
                            sv.inc(V.drain())
                            sv.wait(T)
                            sg.wait(T)
                            sev.wait(T, sev.n - 1)
                            for kd in range(DC):
                                mm = T.matmul(pb[ti % 2][:, 0:tw],
                                              BwTs[:, kd * NS:(kd + 1) * NS],
                                              xq[:, kd * tw:(kd + 1) * tw],
                                              start=(kd == 0), stop=(kd == DC - 1))
                            sp.inc(mm)
                            sp.wait(V)
                            sev.inc(V.tensor_scalar(Bsb[:, c0 - 256:c0 - 256 + tw],
                                                    pb[ti % 2][:, 0:tw], bbs[:, 0:1],
                                                    None, ALU.add))
                            sv.inc(V.engine_nop())
                        sa.wait(V)
                        sv.wait(V)
                        sy.wait(V)
                        sa.wait(G)
                        sv.wait(G)
                        sp.wait(G)
                        G.iota(xt[:, 0:S], [[1, S]], channel_multiplier=0,
                               allow_small_or_imprecise_dtypes=True)
                        su.inc(G.engine_nop())
                        su.wait(V)
                        V.tensor_scalar(xt[:, 0:S], xt[:, 0:S], alcol[:, 1:2], None,
                                        ALU.mult)
                        V.drain()
                        V.tensor_scalar(xt[:, 0:S], xt[:, 0:S], LN1E8, 0.0,
                                        ALU.add, ALU.min)
                        sv.inc(V.drain())
                        sv.wait(SC)
                        sa.inc(SC.activation(xt[:, 0:S], xt[:, 0:S], AF.Exp))
                        sa.wait(V)
                        V.tensor_tensor(Bsb[:, 0:S], Bsb[:, 0:S], xt[:, 0:S], ALU.mult)
                        V.drain()
                        V.memset(yy[:, 0:S], 1.0)
                        V.drain()
                        V.tensor_scalar(yy[:, 0:S], yy[:, 0:S], alcol[:, 0:1], None,
                                        ALU.mult)
                        V.drain()
                        V.tensor_tensor_scan(states[:, YPAD:YPAD + S], yy[:, 0:S],
                                             Bsb[:, 0:S], 0.0, ALU.mult, ALU.add)
                        V.drain()
                        with V.register("wo") as wo:
                            V.reg_load(wo, woffs[0:1, 0:1])
                            woi = V.snap(wo, min_val=0, max_val=YPAD + S + YPAD - WLEN)
                            sv.inc(V.tensor_copy(swin[:, 0:WLEN], states[:, ds(woi, WLEN)]))
                        if debug:
                            sv.wait(SY)
                            sy.inc(SY.dma_start(dbg3[:, :], states[:, :]), 16)
                        bar()

                carry3 = ExitStack()
                with carry3:
                    Usb = sb("Usb", [P, MC * WLEN], BF16, carry3)
                    uph = ExitStack()
                    with uph:
                        CwTs = sb("CwTs", [P, D], BF16, uph)
                        WxS = sb("WxS", [P, DC * M3], BF16, uph)
                        gbc = sb("gbc", [P, MC], F32, uph)
                        umsk = sb("umsk", [P, MC * 512], BF16, uph)
                        sow = sb("sow", [P, DC * 512], BF16, uph)
                        wsc2 = sb("wsc2", [P, 2048], F32, uph)
                        gbs = sb("gbs", [P, 16], F32, uph)

                        sg.inc(G.dma_start(CwTs[:], CwT), 16)
                        sg.inc(G.dma_start(umsk[:], umaski), 16)
                        sy.inc(SY.dma_start(gbs[:], gb16), 16)
                        sy.wait(V)
                        V.tensor_copy(gbc[:, 0:8], gbs[:, 0:8])
                        sv.inc(V.scalar_tensor_tensor(gbc[:, 8:12], gbs[:, 8:12], 0.1,
                                                      gbs[:, 12:16], ALU.mult, ALU.add))
                        for kd in range(DC):
                            sv.wait(SY)
                            sy.inc(SY.dma_start(wsc2[:], Wx4[kd * P:(kd + 1) * P, :]), 16)
                            sy.wait(V)
                            V.tensor_copy(WxS[:, kd * M3:kd * M3 + 1024], wsc2[:, 0:1024])
                            sv.inc(V.scalar_tensor_tensor(
                                WxS[:, kd * M3 + 1024:(kd + 1) * M3],
                                wsc2[:, 1024:1536], 0.1, wsc2[:, 1536:2048],
                                ALU.mult, ALU.add))
                        bar()
                        for ti, (wi0, tn) in enumerate(UW_T):
                            sv.wait(T)
                            for dc_ in range(DC):
                                bank = pb[dc_ % 2]
                                sev.wait(T, sev.n - 1)
                                sp.inc(T.matmul(bank[:, 0:tn],
                                                CwTs[:, dc_ * P:(dc_ + 1) * P],
                                                swin[:, wi0:wi0 + tn],
                                                start=True, stop=True))
                                sp.wait(V)
                                ev = V.tensor_scalar(sow[:, dc_ * 512:dc_ * 512 + tn],
                                                     bank[:, 0:tn], cbs[:, dc_:dc_ + 1],
                                                     None, ALU.add)
                                sev.inc(ev)
                            sv.inc(V.engine_nop())
                            sv.wait(T)
                            for mc_ in range(MC):
                                bank = pb[2 + mc_ % 2]
                                sev.wait(T, sev.n - 1)
                                for kd in range(DC):
                                    mm = T.matmul(
                                        bank[:, 0:tn],
                                        WxS[:, kd * M3 + mc_ * P:kd * M3 + (mc_ + 1) * P],
                                        sow[:, kd * 512:kd * 512 + tn],
                                        start=(kd == 0), stop=(kd == DC - 1))
                                sp.inc(mm)
                                sp.wait(V)
                                dstu = Usb[:, mc_ * WLEN + wi0:mc_ * WLEN + wi0 + tn]
                                if ti == 0:
                                    ev = V.scalar_tensor_tensor(
                                        dstu, bank[:, 0:tn], gbc[:, mc_:mc_ + 1],
                                        umsk[:, mc_ * 512:mc_ * 512 + tn],
                                        ALU.add, ALU.mult)
                                else:
                                    ev = V.tensor_scalar(dstu, bank[:, 0:tn],
                                                         gbc[:, mc_:mc_ + 1], None, ALU.add)
                                sev.inc(ev)
                            sv.inc(V.engine_nop())
                        if debug:
                            sv.wait(SY)
                            sy.inc(SY.dma_start(dbg2[:, :], Usb[:, :]), 16)
                        bar()

                    pic = ExitStack()
                    with pic:
                        WhS = sb("WhS", [P, KC * M3], BF16, pic)
                        wsc3 = sb("wsc3", [P, 2048], F32, pic)
                        tfb = sb("tfb", [P, 2 * 512], F32, pic)
                        tzb = sb("tzb", [P, 512], F32, pic)
                        sgbuf = sb("sgbuf", [P, 512], F32, pic)
                        tfbB = sb("tfbB", [P, 2 * 512], F32, pic)
                        tzbB = sb("tzbB", [P, 512], F32, pic)
                        sgbufB = sb("sgbufB", [P, 512], F32, pic)
                        dtb = sb("dtb", [P, 512], F32, pic)
                        ptb = sb("ptb", [P, 512], F32, pic)

                        for kk in range(KC):
                            sv.wait(SY)
                            dma_fence()
                            sy.inc(SY.dma_start(wsc3[:], Wh4[kk * P:(kk + 1) * P, :]), 16)
                            sy.wait(V)
                            V.tensor_copy(WhS[:, kk * M3:kk * M3 + 1024], wsc3[:, 0:1024])
                            sv.inc(V.scalar_tensor_tensor(WhS[:, kk * M3 + 1024:(kk + 1) * M3],
                                                          wsc3[:, 1024:1536], 0.1,
                                                          wsc3[:, 1536:2048], ALU.mult, ALU.add))
                        bar()
                        NGRP = len(UW_T) * KC
                        hw_marks = [sv.n]
                        ac_marks = []
                        g_all = 0
                        pend = []
                        for k in range(NSWEEP):
                            parR = (k % 2) * CPAR
                            parW = ((k + 1) % 2) * CPAR
                            for ti, (wi0, tn) in enumerate(UW_T):
                                for kap in range(KC):
                                    trip = ptrip[g_all % 2]
                                    T.wait_ge(sv.h, hw_marks[k])
                                    if g_all >= 2:
                                        sa.wait(T, ac_marks[g_all - 2])
                                    for gi, mpart in enumerate((kap, 4 + kap, 8 + kap)):
                                        T.matmul(trip[gi][:, 0:tn], idsb[:],
                                                 Usb[:, mpart * WLEN + wi0:
                                                     mpart * WLEN + wi0 + tn],
                                                 start=True, stop=False)
                                        for kk in range(KC):
                                            mm = T.matmul(
                                                trip[gi][:, 0:tn],
                                                WhS[:, kk * M3 + mpart * P:kk * M3 + (mpart + 1) * P],
                                                cfc[:, parR + kk * NCOL + wi0:
                                                     parR + kk * NCOL + wi0 + tn],
                                                start=False, stop=(kk == KC - 1))
                                    sp.inc(mm)
                                    sp.wait(SC)
                                    sv.wait(SC)
                                    tf_, tz_, sg2_ = ((tfb, tzb, sgbuf) if g_all % 2 == 0
                                                      else (tfbB, tzbB, sgbufB))
                                    SC.activation(tf_[:, 0:tn], trip[0][:, 0:tn], AF.Tanh)
                                    SC.activation(tf_[:, 512:512 + tn], trip[1][:, 0:tn], AF.Tanh)
                                    SC.activation(sg2_[:, 0:tn], trip[2][:, 0:tn], AF.Sigmoid)
                                    sa.inc(SC.activation(tz_[:, 0:tn], trip[2][:, 0:tn],
                                                         AF.Sigmoid, scale=cneg1[:, 0:1]))
                                    ac_marks.append(sa.n)
                                    sa.wait(V)
                                    da, pa = ((dtb, ptb) if g_all % 2 == 0
                                              else (hmt, hmt2))
                                    V.tensor_tensor(da[:, 0:tn], tf_[:, 0:tn],
                                                    tz_[:, 0:tn], ALU.mult)
                                    V.tensor_tensor(pa[:, 0:tn], tf_[:, 512:512 + tn],
                                                    sg2_[:, 0:tn], ALU.mult)
                                    if g_all % 2 == 1 or (ti == len(UW_T) - 1
                                                          and kap == KC - 1):
                                        V.drain()
                                        for (pw, da_, pa_, tn_) in pend:
                                            sv.inc(V.tensor_tensor(
                                                cfc[:, pw:pw + tn_], da_[:, 0:tn_],
                                                pa_[:, 0:tn_], ALU.add))
                                        sv.inc(V.tensor_tensor(
                                            cfc[:, parW + kap * NCOL + wi0 + 1:
                                                 parW + kap * NCOL + wi0 + 1 + tn],
                                            da[:, 0:tn], pa[:, 0:tn], ALU.add))
                                        pend = []
                                    else:
                                        pend.append((parW + kap * NCOL + wi0 + 1,
                                                     da, pa, tn))
                                    g_all += 1
                            hw_marks.append(sv.n)
                        bar()
                if debug:
                    sy.inc(SY.dma_start(dbgb[:, :], cfc[:, :]), 16)
                    bar()

            op = ExitStack()
            with op:
                owTs = sb("owTs", [P, KC * D], BF16, op)
                x1a = sb("x1a", [P, DC * 512], F32, op)
                x1b = sb("x1b", [P, DC * 512], F32, op)
                ya = sb("ya", [P, DC * 512], F32, op)
                yb = sb("yb", [P, DC * 512], F32, op)
                x2ma = sb("x2ma", [P, 512], F32, op)
                x2mr = sb("x2mr", [P, 128], F32, op)
                x2t = sb("x2t", [P, DC * 512], F32, op)

                for kk in range(KC):
                    sg.inc(G.dma_start(owTs[:, kk * D:(kk + 1) * D],
                                       owT[kk * P:(kk + 1) * P, :]), 16)
                sy.inc(SY.dma_start(x2ma[:], x2m0i), 16)
                sy.inc(SY.dma_start(x2mr[:], x2m4i), 16)
                bar()
                for ti, (wi0, tn) in enumerate(UW_T):
                    sv.wait(SY)
                    dma_fence()
                    sy.inc(SY.dma_start(sb3(x1a, DC, tn, 512), hbm3(x1buf, 130 + wi0, DC, tn)), 16)
                    sy.inc(SY.dma_start(sb3(x1b, DC, tn, 512), hbm3(x1buf, 2178 + wi0, DC, tn)), 16)
                    sy.inc(SY.dma_start(sb3(ya, DC, tn, 512), hbm3(ybuf, 2 + wi0, DC, tn)), 16)
                    sy.inc(SY.dma_start(sb3(yb, DC, tn, 512), hbm3(ybuf, 2050 + wi0, DC, tn)), 16)
                    sy.wait(V)
                    sg.wait(T)
                    for dc_ in range(DC):
                        bank = pb[dc_ % 2]
                        sev.wait(T, sev.n - 1)
                        for kk in range(KC):
                            base = kk * NCOL + wi0 + 1
                            mm = T.matmul(bank[:, 0:tn],
                                          owTs[:, kk * D + dc_ * P:kk * D + (dc_ + 1) * P],
                                          cfc[:, base:base + tn],
                                          start=(kk == 0), stop=(kk == KC - 1))
                        sp.inc(mm)
                        sp.wait(V)
                        co = dc_ * 512
                        V.tensor_tensor(hmt[:, 0:tn], x1a[:, co:co + tn], ya[:, co:co + tn],
                                        ALU.add)
                        V.tensor_tensor(hmt2[:, 0:tn], x1b[:, co:co + tn], yb[:, co:co + tn],
                                        ALU.add)
                        ev = V.tensor_scalar(x2t[:, co:co + tn], bank[:, 0:tn],
                                             obs[:, dc_:dc_ + 1], None, ALU.add)
                        sev.inc(ev)
                        V.drain()
                        V.scalar_tensor_tensor(hmt[:, 0:tn], hmt[:, 0:tn], mks[:, 0:1],
                                               x2t[:, co:co + tn], ALU.mult, ALU.add)
                        V.drain()
                        V.scalar_tensor_tensor(x2t[:, co:co + tn], hmt2[:, 0:tn],
                                               mks[:, 1:2], hmt[:, 0:tn],
                                               ALU.mult, ALU.add)
                        if ti == 0:
                            V.drain()
                            V.tensor_tensor(x2t[:, co:co + tn], x2t[:, co:co + tn],
                                            x2ma[:, 0:tn], ALU.mult)
                        elif ti == 4:
                            V.drain()
                            V.tensor_tensor(x2t[:, co:co + tn], x2t[:, co:co + tn],
                                            x2mr[:, 0:tn], ALU.mult)
                        sv.inc(V.drain())
                    sv.wait(SY)
                    sy.inc(SY.dma_start(hbm3(x2buf, X2OFF + wi0, DC, tn),
                                        sb3(x2t, DC, tn, 512)), 16)
                bar()

        ffn2 = ExitStack()
        with ffn2:
            wq = sb("wq", [P, DC * HD], BF16, ffn2)
            V.memset(a2col[:], 0.0)
            sv.inc(V.drain())
            for kd in range(DC):
                sg.inc(G.dma_start(wq[:, kd * HD:(kd + 1) * HD],
                                   wq1buf[kd * P:(kd + 1) * P, :]), 16)
            bar()
            sta = ExitStack()
            with sta:
                abuf = make_stage_a_bufs(sta)
                V.memset(abuf[5][:], 0.0)
                sv.inc(V.drain())
                bar()
                ffn_stage_a(wq, abuf, x2buf, FF2_A, yq2buf)
            stb = ExitStack()
            with stb:
                xt = sb("xt", [P, DC * 512], F32, stb)
                yy = sb("yy", [P, DC * 512], F32, stb)
                yqb = sb("yqb", [P, HC * 512], BF16, stb)
                for kh in range(HC):
                    sg.inc(G.dma_start(wq[:, kh * D:(kh + 1) * D],
                                       wq2buf[kh * P:(kh + 1) * P, :]), 16)
                bar()
                ffn_stage_b(wq, xt, yy, yqb, x2buf, FF2_B, yq2buf, outT)

        dma_fence()
        bar()
    return nc


def _prep_inputs(inputs):
    f32 = np.float32
    x = np.asarray(inputs["L0_attn_x"], f32)
    w1 = np.asarray(inputs["L0_ffn_w1"], f32)
    w2 = np.asarray(inputs["L0_ffn_w2"], f32)
    cw = np.asarray(inputs["L0_ffn_conv_w"], f32)
    cb = np.asarray(inputs["L0_ffn_conv_b"], f32)
    Bw = np.asarray(inputs["L0_attn_B_w"], f32)
    Bbv = np.asarray(inputs["L0_attn_B_b"], f32)
    Al = np.asarray(inputs["L0_attn_A_log"], f32)
    Cw = np.asarray(inputs["L0_attn_C_w"], f32)
    Cbv = np.asarray(inputs["L0_attn_C_b"], f32)
    ng = np.asarray(inputs["L0_attn_norm_g"], f32)
    nb = np.asarray(inputs["L0_attn_norm_b"], f32)
    f1w = np.asarray(inputs["L0_attn_ff1_w"], f32)
    f2w = np.asarray(inputs["L0_attn_ff2_w"], f32)
    taw = np.asarray(inputs["L0_attn_ta_w"], f32)
    tbw = np.asarray(inputs["L0_attn_tb_w"], f32)
    f1b = np.asarray(inputs["L0_attn_ff1_b"], f32)
    f2b = np.asarray(inputs["L0_attn_ff2_b"], f32)
    tab = np.asarray(inputs["L0_attn_ta_b"], f32)
    tbb = np.asarray(inputs["L0_attn_tb_b"], f32)
    ow = np.asarray(inputs["L0_attn_out_w"], f32)
    obv = np.asarray(inputs["L0_attn_out_b"], f32)

    shared = {
        "w1T": np.ascontiguousarray(w1.T),
        "w2T": np.ascontiguousarray(w2.T),
        "convw": np.ascontiguousarray(
            cw[:, 0, :].reshape(HC, P, 3).transpose(1, 0, 2).reshape(P, HC * 3)),
        "convb": np.ascontiguousarray(cb.reshape(HC, P).T),
        "normg": np.ascontiguousarray(ng.reshape(DC, P).T),
        "normb": np.ascontiguousarray(nb.reshape(DC, P).T),
        "BwT": np.ascontiguousarray(Bw.T),
        "Bb": np.ascontiguousarray(Bbv.reshape(P, 1)),
        "Alog": np.ascontiguousarray(Al.reshape(P, 1)),
        "CwT": np.ascontiguousarray(Cw.T),
        "Cb": np.ascontiguousarray(Cbv.reshape(DC, P).T),
        "Wx4": np.ascontiguousarray(np.concatenate(
            [f1w[:, :D].T, f2w[:, :D].T, taw[:, :D].T, tbw[:, :D].T], axis=1)),
        "Wh4": np.ascontiguousarray(np.concatenate(
            [f1w[:, D:].T, f2w[:, D:].T, taw[:, D:].T, tbw[:, D:].T], axis=1)),
        "gb16": np.ascontiguousarray(np.concatenate(
            [f1b.reshape(KC, P).T, f2b.reshape(KC, P).T,
             tab.reshape(KC, P).T, tbb.reshape(KC, P).T], axis=1)),
        "owT": np.ascontiguousarray(ow.T),
        "ob": np.ascontiguousarray(obv.reshape(DC, P).T),
        "ident": np.eye(P, dtype=f32),
    }
    in_maps = []
    for c in range(8):
        b, hf = c // 2, c % 2
        s = 2048 * hf
        xTc = np.zeros((D, XWX), f32)
        xTc[:, 256:256 + S] = x[b].T
        masks_ = np.zeros((P, 8), f32)
        masks_[:, 0] = 1.0 - hf
        masks_[:, 1] = float(hf)
        umask_ = np.ones((P, MC * 512), f32)
        if hf == 0:
            for m in range(MC):
                umask_[:, m * 512:m * 512 + 126] = 0.0
        x2m0_ = np.ones((P, 512), f32)
        if hf == 0:
            x2m0_[:, 0:126] = 0.0
        x2m4_ = np.ones((P, 128), f32)
        if hf == 1:
            x2m4_[:, 126:128] = 0.0
        d = dict(shared)
        d.update({
            "xT": xTc,
            "masks": masks_,
            "umask": umask_,
            "x2m0": x2m0_,
            "x2m4": x2m4_,
            "woff": np.array([[s + 2]], np.uint32),
        })
        in_maps.append(d)
    return in_maps


def kernel(**inputs):
    in_maps = _prep_inputs(inputs)
    nc = build_nc(debug=False)
    res = run_bass_kernel_spmd(nc, in_maps, core_ids=list(range(8)))
    out = np.empty((4, S, D), np.float32)
    for c in range(8):
        b, hf = c // 2, c % 2
        o = np.asarray(res.results[c]["outT"])
        out[b, 2048 * hf:2048 * (hf + 1), :] = o[:, OUT_LO:OUT_HI].T
    return out


# version 10
# speedup vs baseline: 1.1834x; 1.0184x over previous
import numpy as np
import ml_dtypes
import concourse.bass as bass
import concourse.mybir as mybir
from concourse.bass_utils import run_bass_kernel_spmd
from contextlib import ExitStack

F32 = mybir.dt.float32
BF16 = mybir.dt.bfloat16
U32 = mybir.dt.uint32
AF = mybir.ActivationFunctionType
ALU = mybir.AluOpType
AX = mybir.AxisListType
ds = bass.ds

P = 128
S = 4096
D = 1024
DC = 8
HD = 4096
HC = 32
NS = 128
H = 512
KC = 4
M3 = 1536
MC = 12
XW = 4608
WLEN = 2176
NCOL = WLEN + 1
CPAR = KC * NCOL
NSWEEP = 18
YPAD = 128
LN_EPS = 1e-5
C32c = 1.5 * 2.0 ** 23
C64c = 1.5 * 2.0 ** 52
LN1E8 = 18.420680743952367

TOA = 256
TWA = TOA + 2
XWX = 4640
FF1_A = [128 + TOA * j for j in range(17)]
FF1_B = [512 * j for j in range(9)]
MID_T = [256 + 512 * j for j in range(8)]
UW_T = [(0, 512), (512, 512), (1024, 512), (1536, 512), (2048, 128)]
X2W = 2688
X2OFF = 192
FF2_A = [256 + TOA * j for j in range(9)]
FF2_B = [512 * j for j in range(5)]
OUT_LO, OUT_HI = 318, 2366


class Sem:
    def __init__(self, stack, nc, name):
        self.h = stack.enter_context(nc.semaphore(name))
        self.n = 0

    def inc(self, instr, by=1):
        instr.then_inc(self.h, by)
        self.n += by
        return instr

    def wait(self, eng, val=None):
        v = self.n if val is None else val
        if v > 0:
            eng.wait_ge(self.h, v)

    def mark(self):
        return self.n


def build_nc(debug=False):
    nc = bass.Bass("TRN2", target_bir_lowering=False)
    A = lambda name, shape, dt=F32: nc.dram_tensor(name, shape, dt, kind="ExternalInput").ap()
    xT = A("xT", [D, XWX])
    w1T = A("w1T", [D, HD])
    w2T = A("w2T", [HD, D])
    convw = A("convw", [P, 96])
    convb = A("convb", [P, HC])
    normg = A("normg", [P, DC])
    normb = A("normb", [P, DC])
    BwT = A("BwT", [D, NS])
    Bbi = A("Bb", [P, 1])
    Alog = A("Alog", [P, 1])
    CwT = A("CwT", [P, D])
    Cbi = A("Cb", [P, DC])
    Wx4 = A("Wx4", [D, 2048])
    Wh4 = A("Wh4", [H, 2048])
    gb16 = A("gb16", [P, 16])
    owT = A("owT", [H, D])
    obi = A("ob", [P, DC])
    ident = A("ident", [P, P])
    masks = A("masks", [P, 8])
    umaski = A("umask", [P, MC * 512])
    x2m0i = A("x2m0", [P, 512])
    x2m4i = A("x2m4", [P, 128])
    woff = A("woff", [1, 1], U32)

    x1buf = nc.dram_tensor("x1buf", [D, XW], F32, kind="Internal").ap()
    ybuf = nc.dram_tensor("ybuf", [D, S + 2 * YPAD], F32, kind="Internal").ap()
    yqbuf = nc.dram_tensor("yqbuf", [HD, XW], BF16, kind="Internal").ap()
    x2buf = nc.dram_tensor("x2buf", [D, X2W], F32, kind="Internal").ap()
    yq2buf = nc.dram_tensor("yq2buf", [HD, X2W], BF16, kind="Internal").ap()
    wq1buf = nc.dram_tensor("wq1buf", [D, HD], BF16, kind="Internal").ap()
    wq2buf = nc.dram_tensor("wq2buf", [HD, D], BF16, kind="Internal").ap()
    outT = nc.dram_tensor("outT", [D, X2W], F32, kind="ExternalOutput").ap()
    if debug:
        dbgb = nc.dram_tensor("dbgb", [P, 2 * CPAR], BF16, kind="ExternalOutput").ap()
        dbg2 = nc.dram_tensor("dbg2", [P, MC * WLEN], BF16, kind="ExternalOutput").ap()
        dbg3 = nc.dram_tensor("dbg3", [P, YPAD + S + YPAD], BF16, kind="ExternalOutput").ap()

    st = ExitStack()
    _nm = [0]
    with st:
        def sb(name, shape, dt=F32, stk=None):
            _nm[0] += 1
            return (stk or st).enter_context(nc.sbuf_tensor(f"{name}_{_nm[0]}", shape, dt))

        def ps_(name):
            return st.enter_context(nc.psum_tensor(name, [P, 512], F32))

        onesb = sb("onesb", [P, 1], BF16)
        onesf = sb("onesf", [P, 1])
        idsf = sb("idsf", [P, P])
        idneg = sb("idneg", [P, P])
        cmag = sb("cmag", [P, 1])
        cnmag = sb("cnmag", [P, 1])
        halfc = sb("halfc", [P, 1])
        epsc = sb("epsc", [P, 1])
        rw1c = sb("rw1c", [P, 1])
        rw2h = sb("rw2h", [P, 1])
        swcol = sb("swcol", [P, 1])
        convws = sb("convws", [P, 96])
        convbs = sb("convbs", [P, HC])
        gcol = sb("gcol", [P, DC])
        bcolv = sb("bcolv", [P, DC])
        cbs = sb("cbs", [P, DC])
        obs = sb("obs", [P, DC])
        bbs = sb("bbs", [P, 1])
        mks = sb("mks", [P, 8])
        alcol = sb("alcol", [P, 2])
        tcol = sb("tcol", [P, 8])
        stg = sb("stg", [P, 8])
        r11 = sb("r11", [1, 8])
        statr = sb("statr", [1, 4 * 512])
        ones128 = sb("ones128", [P, P])
        repb = sb("repb", [P, 4 * P])
        a2col = sb("a2col", [P, 40])
        cm = sb("cm", [P, 512])
        cmt = sb("cmt", [P, 512])
        zsb = sb("zsb", [P, 512], BF16)
        woffs = sb("woffs", [1, 1], U32)
        hmt = sb("hmt", [P, 512])
        hmt2 = sb("hmt2", [P, 512])
        a1s = sb("a1s", [P, 512])
        onesr = sb("onesr", [1, P])
        idsb = sb("idsb", [P, P], BF16)
        cneg1 = sb("cneg1", [P, 1])

        pb = [ps_(f"pb{i}") for i in range(4)]
        pbc = ps_("pbc")
        pbc2 = ps_("pbc2")
        pst = ps_("pst")
        pst2 = ps_("pst2")
        ptrip = [[pb[0], pb[1], pb[2]], [pb[3], pst, pst2]]

        sy = Sem(st, nc, "sy")
        sg = Sem(st, nc, "sg")
        sv = Sem(st, nc, "sv")
        sa = Sem(st, nc, "sa")
        sp = Sem(st, nc, "sp")
        sev = Sem(st, nc, "sev")
        su = Sem(st, nc, "su")

        V, SC, T, G, SY = nc.vector, nc.scalar, nc.tensor, nc.gpsimd, nc.sync

        def tnop_inc(sem):
            T.drain()
            sem.inc(T.nop())

        def dma_fence():
            SY.wait_ge(sy.h, sy.n) if sy.n else None
            G.wait_ge(sg.h, sg.n) if sg.n else None

        def bar():
            dma_fence()
            nc.all_engine_barrier()

        sy.inc(SY.dma_start(idsf[:], ident), 16)
        sy.inc(SY.dma_start(convws[:], convw), 16)
        sy.inc(SY.dma_start(convbs[:], convb), 16)
        sy.inc(SY.dma_start(gcol[:], normg), 16)
        sy.inc(SY.dma_start(bcolv[:], normb), 16)
        sy.inc(SY.dma_start(cbs[:], Cbi), 16)
        sy.inc(SY.dma_start(obs[:], obi), 16)
        sy.inc(SY.dma_start(bbs[:], Bbi), 16)
        sy.inc(SY.dma_start(mks[:], masks), 16)
        sy.inc(SY.dma_start(woffs[:], woff), 16)
        sy.inc(SY.dma_start(alcol[:, 0:1], Alog), 16)
        sy.wait(V)
        V.memset(onesf[:], 1.0)
        V.memset(onesr[:], 1.0)
        V.memset(ones128[:], 1.0)
        V.memset(a2col[:], 0.0)
        V.memset(tcol[:], 1.0)
        V.memset(stg[:], 1.0)
        V.memset(cm[:], 0.0)
        V.memset(cmt[:], 0.0)
        V.memset(statr[:], 0.0)
        V.memset(hmt[:], 0.0)
        V.memset(hmt2[:], 0.0)
        V.memset(a1s[:], 0.0)
        V.memset(repb[:], 0.0)
        V.memset(halfc[:], 0.5)
        V.memset(epsc[:], LN_EPS)
        V.memset(zsb[:], 0.0)
        V.memset(cneg1[:], -1.0)
        V.drain()
        sy.wait(V)
        V.tensor_copy(idsb[:], idsf[:])
        V.tensor_scalar(idneg[:], idsf[:], -1.0, None, ALU.mult)
        V.drain()
        V.tensor_copy(onesb[:], onesf[:])
        V.drain()
        V.tensor_scalar(cmag[:], onesf[:], 2.0 ** -30, None, ALU.add)
        V.drain()
        V.tensor_scalar(cmag[:], cmag[:], -1.0, None, ALU.add)
        V.drain()
        V.tensor_scalar(cmag[:], cmag[:], 2.0 ** 30, None, ALU.mult)
        V.drain()
        V.tensor_scalar(cmag[:], cmag[:], C64c, C32c, ALU.mult, ALU.add)
        V.drain()
        V.tensor_scalar(cnmag[:], cmag[:], -1.0, None, ALU.mult)
        V.drain()
        bar()

        def hbm3(t, col0, nch, w):
            C = t.shape[1]
            return bass.AP(t.tensor, col0, [[C, P], [P * C, nch], [1, w]])

        def sb3(t, nch, w, tw, col0=0):
            return bass.AP(t, col0, [[nch * tw, P], [tw, nch], [1, w]])

        def stats_rows(src, nch, tw, xbb, sqbb):
            tot = nch * tw
            sp.wait(V)
            V.tensor_copy(xbb[:, 0:tot], src[:, 0:tot])
            sv.inc(V.tensor_tensor(sqbb[:, 0:tot], src[:, 0:tot], src[:, 0:tot], ALU.mult))
            sv.wait(T)
            for c in range(nch):
                T.matmul(pst[0:1, 0:tw], onesb[:, 0:1], xbb[:, c * tw:(c + 1) * tw],
                         start=(c == 0), stop=(c == nch - 1))
            for c in range(nch):
                mm = T.matmul(pst2[0:1, 0:tw], onesb[:, 0:1], sqbb[:, c * tw:(c + 1) * tw],
                              start=(c == 0), stop=(c == nch - 1))
            sp.inc(mm)
            sp.wait(V)
            mu = statr[0:1, 1024:1024 + tw]
            va = statr[0:1, 1536:1536 + tw]
            rr = statr[0:1, 0:tw]
            mr = statr[0:1, 512:512 + tw]
            V.tensor_scalar(mu, pst[0:1, 0:tw], 1.0 / (nch * P), None, ALU.mult)
            V.tensor_scalar(va, pst2[0:1, 0:tw], 1.0 / (nch * P), None, ALU.mult)
            V.drain()
            V.scalar_tensor_tensor(rr, pst[0:1, 0:tw], -1.0 / (nch * P), mu,
                                   ALU.mult, ALU.mult)
            V.drain()
            V.tensor_tensor(va, va, rr, ALU.add)
            sv.inc(V.drain())
            sv.wait(SC)
            sa.inc(SC.activation(rr, va, AF.Sqrt, bias=epsc[0:1, 0:1]))
            sa.wait(V)
            V.reciprocal(rr, rr)
            V.drain()
            V.tensor_tensor(mr, mu, rr, ALU.mult)
            sv.inc(V.drain())

        def ln_apply(src, dst, nch, tw):
            sv.wait(T)
            T.matmul(pbc[:, 0:tw], onesr[0:1, 0:P], statr[0:1, 0:tw], start=True, stop=True)
            T.matmul(pbc2[:, 0:tw], onesr[0:1, 0:P], statr[0:1, 512:512 + tw],
                     start=True, stop=True)
            tnop_inc(sp)
            sp.wait(V)
            for c in range(nch):
                V.tensor_tensor(dst[:, c * tw:(c + 1) * tw], src[:, c * tw:(c + 1) * tw],
                                pbc[:, 0:tw], ALU.mult)
            V.drain()
            for c in range(nch):
                V.tensor_tensor(dst[:, c * tw:(c + 1) * tw], dst[:, c * tw:(c + 1) * tw],
                                pbc2[:, 0:tw], ALU.subtract)
            sv.inc(V.drain())

        def quant_cols(src, nch, tw, pools, nblk, acol_dst=None, halves=1):
            ap3 = bass.AP(src.tensor if hasattr(src, "tensor") else src, 0,
                          [[nch * tw, P], [1, tw], [tw, nch]])
            sv.wait(V)
            sv.inc(V.tensor_reduce(cm[:, 0:tw], ap3, AX.X, ALU.max,
                                   apply_absolute_value=True))
            sv.wait(T)
            for j in range(nblk):
                w = min(128, tw - j * 128)
                T.transpose(pst[0:w, j * 128:j * 128 + 128],
                            cm[:, j * 128:j * 128 + w], idsf[:])
            tnop_inc(sp)
            sp.wait(V)
            for j in range(nblk):
                w = min(128, tw - j * 128)
                V.tensor_reduce(tcol[0:w, j:j + 1], pst[0:w, j * 128:(j + 1) * 128],
                                AX.X, ALU.max)
            V.drain()
            V.tensor_scalar(tcol[:, 0:nblk], tcol[:, 0:nblk], 1e-5, None, ALU.max)
            V.drain()
            if acol_dst is not None:
                V.tensor_copy(acol_dst, tcol[:, 0:nblk])
            sv.inc(V.reciprocal(stg[:, 0:nblk], tcol[:, 0:nblk]))
            sv.wait(SC)
            SC.drain()
            SC.activation(stg[:, 4:4 + nblk], tcol[:, 0:nblk], AF.Identity,
                          scale=1.0 / 127.0)
            sa.inc(SC.activation(stg[:, 0:nblk], stg[:, 0:nblk], AF.Identity,
                                 scale=127.0))

        def colbcast(psum, col_src, ncols, nblk):
            sp.wait(V)
            sa.wait(V)
            for j in range(nblk):
                op = V.tensor_scalar(repb[:, j * P:(j + 1) * P], ones128[:],
                                     col_src[:, j:j + 1], None, ALU.mult)
            sv.inc(op)
            sv.wait(T)
            for j in range(nblk):
                T.transpose(psum[0:P, j * 128:(j + 1) * 128],
                            repb[:, j * P:(j + 1) * P], idsf[:])
            tnop_inc(sp)
            return None

        def roundq_act(dst, src, tot):
            sa.inc(SC.activation(src[:, 0:tot], src[:, 0:tot], AF.Identity,
                                 bias=cmag[:, 0:1]))
            sa.wait(SC)
            SC.drain()
            sa.inc(SC.activation(dst[:, 0:tot], src[:, 0:tot], AF.Identity,
                                 bias=cnmag[:, 0:1]))

        def quant_weights(wsrc, nchunks, width, wq_t, wqhbm, rw_dst, wscb,
                          rw_scale=1.0 / 127.0):
            def slot(c):
                return wscb[:, (c % 2) * width:(c % 2) * width + width]
            qs = [(sy, SY), (sg, G)]
            dma_fence()
            vmark = [sv.n, sv.n]
            for c in range(nchunks):
                sm, Q = qs[c % 2]
                sv.wait(Q, vmark[c % 2])
                sm.inc(Q.dma_start(slot(c), wsrc[c * P:(c + 1) * P, :]), 16)
                sm.wait(V)
                sv.inc(V.tensor_reduce(cm[:, c:c + 1], slot(c), AX.X,
                                       ALU.add, apply_absolute_value=True))
                vmark[c % 2] = sv.n
            V.drain()
            V.tensor_reduce(hmt[:, 0:1], cm[:, 0:nchunks], AX.X, ALU.add)
            sv.inc(V.drain())
            sv.wait(T)
            T.matmul(pst[0:1, 0:1], hmt[:, 0:1], onesf[:, 0:1], start=True, stop=True)
            tnop_inc(sp)
            sp.wait(V)
            V.tensor_scalar(r11[0:1, 0:1], pst[0:1, 0:1], 1.0 / (nchunks * P * width),
                            None, ALU.mult)
            V.drain()
            V.tensor_scalar(r11[0:1, 0:1], r11[0:1, 0:1], 1e-5, None, ALU.max)
            V.drain()
            V.reciprocal(r11[0:1, 1:2], r11[0:1, 0:1])
            V.drain()
            V.tensor_scalar(r11[0:1, 2:3], r11[0:1, 0:1], rw_scale, None, ALU.mult)
            sv.inc(V.drain())
            sv.wait(T)
            T.matmul(pst[0:P, 8:9], onesr[0:1, 0:P], r11[0:1, 1:2], start=True, stop=True)
            T.matmul(pst[0:P, 9:10], onesr[0:1, 0:P], r11[0:1, 2:3],
                     start=True, stop=True)
            tnop_inc(sp)
            sp.wait(V)
            V.tensor_copy(swcol[:, 0:1], pst[0:P, 8:9])
            V.tensor_copy(rw_dst[:, 0:1], pst[0:P, 9:10])
            sv.inc(V.drain())
            vmark = [sv.n, sv.n]
            for c in range(nchunks):
                sm, Q = qs[c % 2]
                sv.wait(Q, vmark[c % 2])
                sm.inc(Q.dma_start(slot(c), wsrc[c * P:(c + 1) * P, :]), 16)
                sm.wait(SC)
                SC.drain()
                SC.activation(slot(c), slot(c), AF.Identity,
                              bias=cmag[:, 0:1], scale=swcol[:, 0:1])
                SC.drain()
                sa.inc(SC.activation(slot(c), slot(c), AF.Identity,
                                     bias=cnmag[:, 0:1]))
                sa.wait(V)
                dst = wq_t[:, c * width:(c + 1) * width]
                sv.inc(V.tensor_scalar(dst, slot(c), 1.0, -1.0, ALU.min, ALU.max))
                vmark[c % 2] = sv.n
                sv.wait(Q)
                sm.inc(Q.dma_start(wqhbm[c * P:(c + 1) * P, :], dst), 16)
            bar()

        def ffn_stage_a(wq, abuf, src_hbm, c0list, yq_hbm):
            xta, yya, xbb, sqbb, xq, y2, yqa, xqb, tg = abuf
            xqs = [xq, xqb]
            tw = TWA

            def prep_chunks(ii, c0):
                xq_t = xqs[ii % 2]

                def c_load():
                    sv.wait(SY)
                    sa.wait(SY)
                    dma_fence()
                    sy.inc(SY.dma_start(sb3(xta, DC, tw, tw),
                                        hbm3(src_hbm, c0 - 1, DC, tw)), 16)

                def c_stats():
                    sy.wait(V)
                    stats_rows(xta, DC, tw, xbb, sqbb)

                def c_ln():
                    ln_apply(xta, yya, DC, tw)

                def c_quant():
                    quant_cols(yya, DC, tw, None, 3)

                def c_smul():
                    colbcast(pbc, stg[:, 0:4], tw, 3)
                    sp.wait(V)
                    for c in range(DC):
                        V.tensor_tensor(yya[:, c * tw:(c + 1) * tw],
                                        yya[:, c * tw:(c + 1) * tw],
                                        pbc[:, 0:tw], ALU.mult)
                    sv.inc(V.drain())
                    sv.wait(SC)
                    roundq_act(yya, yya, DC * tw)

                def c_xq():
                    colbcast(pbc2, stg[:, 4:8], tw, 3)
                    sa.wait(V)
                    sp.wait(V)
                    for c in range(DC):
                        V.tensor_tensor(xq_t[:, c * tw:(c + 1) * tw],
                                        yya[:, c * tw:(c + 1) * tw],
                                        pbc2[:, 0:tw], ALU.mult)
                    sv.inc(V.drain())

                return [c_load, c_stats, c_ln, c_quant, c_smul, c_xq]

            nt = len(c0list)
            for ch in prep_chunks(0, c0list[0]):
                ch()
            for ti in range(nt):
                c0 = c0list[ti]
                xq_t = xqs[ti % 2]
                nxt = (prep_chunks(ti + 1, c0list[ti + 1]) if ti + 1 < nt
                       else [])
                sv.wait(T)
                for q in range(HC // 4):
                    for j in range(4):
                        mh = 4 * q + j
                        sev.wait(T, sev.n - 3 + j)
                        for kd in range(DC):
                            mm = T.matmul(pb[j][:, 0:tw],
                                          wq[:, kd * HD + mh * P:kd * HD + (mh + 1) * P],
                                          xq_t[:, kd * tw:(kd + 1) * tw],
                                          start=(kd == 0), stop=(kd == DC - 1))
                        sp.inc(mm)
                    if nxt:
                        nxt.pop(0)()
                    sp.wait(SC)
                    SC.drain()
                    for j in range(4):
                        mh = 4 * q + j
                        a1 = SC.activation(tg[:, (mh % 8) * TOA:(mh % 8 + 1) * TOA],
                                           pb[j][:, 0:tw - 2], AF.Identity,
                                           bias=convbs[:, mh:mh + 1],
                                           scale=convws[:, 3 * mh:3 * mh + 1])
                    sa.inc(a1)
                    sa.wait(V)
                    for j in range(4):
                        mh = 4 * q + j
                        V.scalar_tensor_tensor(tg[:, (mh % 8) * TOA:(mh % 8 + 1) * TOA],
                                               pb[j][:, 1:tw - 1],
                                               convws[:, 3 * mh + 1:3 * mh + 2],
                                               tg[:, (mh % 8) * TOA:(mh % 8 + 1) * TOA],
                                               ALU.mult, ALU.add)
                    V.drain()
                    for j in range(4):
                        mh = 4 * q + j
                        ev = V.scalar_tensor_tensor(tg[:, (mh % 8) * TOA:(mh % 8 + 1) * TOA],
                                                    pb[j][:, 2:tw],
                                                    convws[:, 3 * mh + 2:3 * mh + 3],
                                                    tg[:, (mh % 8) * TOA:(mh % 8 + 1) * TOA],
                                                    ALU.mult, ALU.add)
                        sev.inc(ev)
                    sv.inc(V.engine_nop())
                    sv.wait(SC)
                    SC.drain()
                    for j in range(4):
                        mh = 4 * q + j
                        a2 = SC.activation(y2[:, mh * TOA:(mh + 1) * TOA],
                                           tg[:, (mh % 8) * TOA:(mh % 8 + 1) * TOA],
                                           AF.Silu)
                    sa.inc(a2)
                sa.wait(V)
                quant_cols(y2, HC, TOA, None, 2,
                           acol_dst=a2col[:, c0 // 128:c0 // 128 + 2], halves=2)
                colbcast(pbc, stg[:, 0:4], TOA, 2)
                sp.wait(V)
                for c in range(HC):
                    V.tensor_tensor(y2[:, c * TOA:(c + 1) * TOA], y2[:, c * TOA:(c + 1) * TOA],
                                    pbc[:, 0:TOA], ALU.mult)
                sv.inc(V.drain())
                colbcast(pbc2, stg[:, 4:8], TOA, 2)
                hh = HC // 2
                for half in range(2):
                    sy.wait(SC)
                    sv.wait(SC)
                    y2h = bass.AP(y2, half * hh * TOA, [[HC * TOA, P], [1, hh * TOA]])
                    SC.drain()
                    SC.activation(y2h, y2h, AF.Identity, bias=cmag[:, 0:1])
                    SC.drain()
                    sa.inc(SC.activation(y2h, y2h, AF.Identity, bias=cnmag[:, 0:1]))
                    sa.wait(V)
                    sy.wait(V)
                    sp.wait(V)
                    for c in range(hh):
                        V.tensor_tensor(yqa[:, c * TOA:(c + 1) * TOA],
                                        bass.AP(y2, (half * hh + c) * TOA,
                                                [[HC * TOA, P], [1, TOA]]),
                                        pbc2[:, 0:TOA], ALU.mult)
                    sv.inc(V.drain())
                    sv.wait(SY)
                    dma_fence()
                    sy.inc(SY.dma_start(
                        hbm3(yq_hbm, c0, hh, TOA) if half == 0 else
                        bass.AP(yq_hbm.tensor, hh * P * yq_hbm.shape[1] + c0,
                                [[yq_hbm.shape[1], P], [P * yq_hbm.shape[1], hh], [1, TOA]]),
                        bass.AP(yqa, 0, [[hh * TOA, P], [TOA, hh], [1, TOA]])), 16)
            bar()

        def ffn_stage_b(wq, xt, yy, yqb, src_res_hbm, c0list, yq_hbm, out_hbm):
            for c0 in c0list:
                tw = 512
                sv.wait(SY)
                dma_fence()
                sy.inc(SY.dma_start(sb3(xt, DC, tw, tw), hbm3(src_res_hbm, c0, DC, tw)), 16)
                sy.inc(SY.dma_start(
                    bass.AP(yqb, 0, [[HC * tw, P], [tw, HC], [1, tw]]),
                    hbm3(yq_hbm, c0, HC, tw)), 16)
                colbcast(pbc2, a2col[:, c0 // 128:c0 // 128 + 4], tw, 4)
                sp.wait(V)
                V.tensor_copy(a1s[:, 0:tw], pbc2[:, 0:tw])
                sv.inc(V.drain())
                sy.wait(T)
                sy.wait(V)
                for md in range(DC):
                    bank = pb[md % 2]
                    hb = hmt if md % 2 == 0 else hmt2
                    sev.wait(T, sev.n - 1)
                    for kh in range(HC):
                        mm = T.matmul(bank[:, 0:tw],
                                      wq[:, kh * D + md * P:kh * D + (md + 1) * P],
                                      yqb[:, kh * tw:(kh + 1) * tw],
                                      start=(kh == 0), stop=(kh == HC - 1))
                    sp.inc(mm)
                    sp.wait(V)
                    ev = V.scalar_tensor_tensor(yy[:, md * tw:(md + 1) * tw],
                                                bank[:, 0:tw], rw2h[:, 0:1],
                                                xt[:, md * tw:(md + 1) * tw],
                                                ALU.mult, ALU.add)
                    sev.inc(ev)
                    sv.inc(V.engine_nop())
                sv.wait(SY)
                sy.inc(SY.dma_start(hbm3(out_hbm, c0, DC, tw), sb3(yy, DC, tw, tw)), 16)
            bar()

        def make_stage_a_bufs(stk):
            xta = sb("xta", [P, DC * TWA], F32, stk)
            yya = sb("yya", [P, DC * TWA], F32, stk)
            xbb = sb("xbb", [P, DC * TWA], BF16, stk)
            sqbb = sb("sqbb", [P, DC * TWA], BF16, stk)
            xq = sb("xq", [P, DC * TWA], BF16, stk)
            y2 = sb("y2", [P, HC * TOA], F32, stk)
            yqa = sb("yqa", [P, (HC // 2) * TOA], BF16, stk)
            xqb = sb("xqb", [P, DC * TWA], BF16, stk)
            tg = sb("tg", [P, 8 * TOA], F32, stk)
            return (xta, yya, xbb, sqbb, xq, y2, yqa, xqb, tg)

        ffn1 = ExitStack()
        with ffn1:
            wq = sb("wq", [P, DC * HD], BF16, ffn1)
            zf = sb("zf", [P, 512], F32, ffn1)

            for col0, w, t, nch in [(0, 130, yqbuf, HC), (4480, 128, yqbuf, HC),
                                    (0, 317, yq2buf, HC), (2560, 128, yq2buf, HC)]:
                sy.inc(SY.dma_start(hbm3(t, col0, nch, w),
                                    bass.AP(zsb, 0, [[512, P], [0, nch], [1, w]])), 16)
            V.memset(zf[:], 0.0)
            sv.inc(V.drain())
            sv.wait(SY)
            for col0, w, t, nch in [(0, YPAD, ybuf, DC), (S + YPAD, YPAD, ybuf, DC),
                                    (0, X2OFF, x2buf, DC), (2368, X2W - 2368, x2buf, DC)]:
                sy.inc(SY.dma_start(hbm3(t, col0, nch, w),
                                    bass.AP(zf, 0, [[512, P], [0, nch], [1, w]])), 16)
            bar()
            sta = ExitStack()
            with sta:
                abuf = make_stage_a_bufs(sta)
                quant_weights(w1T, DC, HD, wq, wq1buf, rw1c, abuf[5], rw_scale=1.0)
                sv.wait(V)
                V.tensor_scalar(convws[:], convws[:], rw1c[:, 0:1], None, ALU.mult)
                sv.inc(V.drain())
                bar()
                ffn_stage_a(wq, abuf, xT, FF1_A, yqbuf)
            stb = ExitStack()
            with stb:
                xt = sb("xt", [P, DC * 512], F32, stb)
                yy = sb("yy", [P, DC * 512], F32, stb)
                yqb = sb("yqb", [P, HC * 512], BF16, stb)
                quant_weights(w2T, HC, D, wq, wq2buf, rw2h, yy, rw_scale=1.0)
                sv.wait(V)
                V.tensor_scalar(rw2h[:], rw2h[:], 0.5, None, ALU.mult)
                sv.inc(V.drain())
                bar()
                ffn_stage_b(wq, xt, yy, yqb, xT, FF1_B, yqbuf, x1buf)

        carry1 = ExitStack()
        with carry1:
            cfc = sb("cfc", [P, 2 * CPAR], BF16, carry1)
            carry2 = ExitStack()
            with carry2:
                swin = sb("swin", [P, WLEN], BF16, carry2)
                if True:

                    mid = ExitStack()
                    with mid:
                        xt = sb("xt", [P, DC * 512], F32, mid)
                        xbb = sb("xbb", [P, DC * 512], BF16, mid)
                        sqbb = sb("sqbb", [P, DC * 512], BF16, mid)
                        yy = sb("yy", [P, DC * 512], F32, mid)
                        xq = sb("xq", [P, DC * 512], BF16, mid)
                        Bsb = sb("Bsb", [P, S], F32, mid)
                        states = sb("states", [P, YPAD + S + YPAD], BF16, mid)
                        BwTs = sb("BwTs", [P, DC * NS], BF16, mid)

                        V.memset(states[:], 0.0)
                        V.memset(cfc[:], 0.0)
                        sv.inc(V.drain())
                        for kd in range(DC):
                            sg.inc(G.dma_start(BwTs[:, kd * NS:(kd + 1) * NS],
                                               BwT[kd * P:(kd + 1) * P, :]), 16)
                        sy.wait(SC)
                        sa.inc(SC.activation(alcol[:, 0:1], alcol[:, 0:1], AF.Sigmoid))
                        sa.wait(SC)
                        sa.inc(SC.activation(alcol[:, 1:2], alcol[:, 0:1], AF.Ln))
                        bar()

                        for ti, c0 in enumerate(MID_T):
                            tw = 512
                            sv.wait(SY)
                            sa.wait(SY)
                            dma_fence()
                            sy.inc(SY.dma_start(sb3(xt, DC, tw, tw), hbm3(x1buf, c0, DC, tw)), 16)
                            sy.wait(V)
                            stats_rows(xt, DC, tw, xbb, sqbb)
                            ln_apply(xt, yy, DC, tw)
                            sv.wait(SY)
                            sy.inc(SY.dma_start(hbm3(ybuf, c0 - 128, DC, tw),
                                                sb3(yy, DC, tw, tw)), 16)
                            stats_rows(yy, DC, tw, xbb, sqbb)
                            ln_apply(yy, xt, DC, tw)
                            for c in range(DC):
                                V.tensor_scalar(xq[:, c * tw:(c + 1) * tw],
                                                xt[:, c * tw:(c + 1) * tw],
                                                gcol[:, c:c + 1], bcolv[:, c:c + 1],
                                                ALU.mult, ALU.add)
                            sv.inc(V.drain())
                            sv.wait(T)
                            sg.wait(T)
                            sev.wait(T, sev.n - 1)
                            for kd in range(DC):
                                mm = T.matmul(pb[ti % 2][:, 0:tw],
                                              BwTs[:, kd * NS:(kd + 1) * NS],
                                              xq[:, kd * tw:(kd + 1) * tw],
                                              start=(kd == 0), stop=(kd == DC - 1))
                            sp.inc(mm)
                            sp.wait(V)
                            sev.inc(V.tensor_scalar(Bsb[:, c0 - 256:c0 - 256 + tw],
                                                    pb[ti % 2][:, 0:tw], bbs[:, 0:1],
                                                    None, ALU.add))
                            sv.inc(V.engine_nop())
                        sa.wait(V)
                        sv.wait(V)
                        sy.wait(V)
                        sa.wait(G)
                        sv.wait(G)
                        sp.wait(G)
                        G.iota(xt[:, 0:S], [[1, S]], channel_multiplier=0,
                               allow_small_or_imprecise_dtypes=True)
                        su.inc(G.engine_nop())
                        su.wait(V)
                        V.tensor_scalar(xt[:, 0:S], xt[:, 0:S], alcol[:, 1:2], None,
                                        ALU.mult)
                        V.drain()
                        V.tensor_scalar(xt[:, 0:S], xt[:, 0:S], LN1E8, 0.0,
                                        ALU.add, ALU.min)
                        sv.inc(V.drain())
                        sv.wait(SC)
                        sa.inc(SC.activation(xt[:, 0:S], xt[:, 0:S], AF.Exp))
                        sa.wait(V)
                        V.tensor_tensor(Bsb[:, 0:S], Bsb[:, 0:S], xt[:, 0:S], ALU.mult)
                        V.drain()
                        V.memset(yy[:, 0:S], 1.0)
                        V.drain()
                        V.tensor_scalar(yy[:, 0:S], yy[:, 0:S], alcol[:, 0:1], None,
                                        ALU.mult)
                        V.drain()
                        V.tensor_tensor_scan(states[:, YPAD:YPAD + S], yy[:, 0:S],
                                             Bsb[:, 0:S], 0.0, ALU.mult, ALU.add)
                        V.drain()
                        with V.register("wo") as wo:
                            V.reg_load(wo, woffs[0:1, 0:1])
                            woi = V.snap(wo, min_val=0, max_val=YPAD + S + YPAD - WLEN)
                            sv.inc(V.tensor_copy(swin[:, 0:WLEN], states[:, ds(woi, WLEN)]))
                        if debug:
                            sv.wait(SY)
                            sy.inc(SY.dma_start(dbg3[:, :], states[:, :]), 16)
                        bar()

                carry3 = ExitStack()
                with carry3:
                    Usb = sb("Usb", [P, MC * WLEN], BF16, carry3)
                    uph = ExitStack()
                    with uph:
                        CwTs = sb("CwTs", [P, D], BF16, uph)
                        WxS = sb("WxS", [P, DC * M3], BF16, uph)
                        gbc = sb("gbc", [P, MC], F32, uph)
                        umsk = sb("umsk", [P, MC * 512], BF16, uph)
                        sow = sb("sow", [P, DC * 512], BF16, uph)
                        wsc2 = sb("wsc2", [P, 2048], F32, uph)
                        gbs = sb("gbs", [P, 16], F32, uph)

                        sg.inc(G.dma_start(CwTs[:], CwT), 16)
                        sg.inc(G.dma_start(umsk[:], umaski), 16)
                        sy.inc(SY.dma_start(gbs[:], gb16), 16)
                        sy.wait(V)
                        V.tensor_copy(gbc[:, 0:8], gbs[:, 0:8])
                        sv.inc(V.scalar_tensor_tensor(gbc[:, 8:12], gbs[:, 8:12], 0.1,
                                                      gbs[:, 12:16], ALU.mult, ALU.add))
                        for kd in range(DC):
                            sv.wait(SY)
                            sy.inc(SY.dma_start(wsc2[:], Wx4[kd * P:(kd + 1) * P, :]), 16)
                            sy.wait(V)
                            V.tensor_copy(WxS[:, kd * M3:kd * M3 + 1024], wsc2[:, 0:1024])
                            sv.inc(V.scalar_tensor_tensor(
                                WxS[:, kd * M3 + 1024:(kd + 1) * M3],
                                wsc2[:, 1024:1536], 0.1, wsc2[:, 1536:2048],
                                ALU.mult, ALU.add))
                        bar()
                        for ti, (wi0, tn) in enumerate(UW_T):
                            sv.wait(T)
                            for dc_ in range(DC):
                                bank = pb[dc_ % 2]
                                sev.wait(T, sev.n - 1)
                                sp.inc(T.matmul(bank[:, 0:tn],
                                                CwTs[:, dc_ * P:(dc_ + 1) * P],
                                                swin[:, wi0:wi0 + tn],
                                                start=True, stop=True))
                                sp.wait(V)
                                ev = V.tensor_scalar(sow[:, dc_ * 512:dc_ * 512 + tn],
                                                     bank[:, 0:tn], cbs[:, dc_:dc_ + 1],
                                                     None, ALU.add)
                                sev.inc(ev)
                            sv.inc(V.engine_nop())
                            sv.wait(T)
                            for mc_ in range(MC):
                                bank = pb[2 + mc_ % 2]
                                sev.wait(T, sev.n - 1)
                                for kd in range(DC):
                                    mm = T.matmul(
                                        bank[:, 0:tn],
                                        WxS[:, kd * M3 + mc_ * P:kd * M3 + (mc_ + 1) * P],
                                        sow[:, kd * 512:kd * 512 + tn],
                                        start=(kd == 0), stop=(kd == DC - 1))
                                sp.inc(mm)
                                sp.wait(V)
                                dstu = Usb[:, mc_ * WLEN + wi0:mc_ * WLEN + wi0 + tn]
                                if ti == 0:
                                    ev = V.scalar_tensor_tensor(
                                        dstu, bank[:, 0:tn], gbc[:, mc_:mc_ + 1],
                                        umsk[:, mc_ * 512:mc_ * 512 + tn],
                                        ALU.add, ALU.mult)
                                else:
                                    ev = V.tensor_scalar(dstu, bank[:, 0:tn],
                                                         gbc[:, mc_:mc_ + 1], None, ALU.add)
                                sev.inc(ev)
                            sv.inc(V.engine_nop())
                        if debug:
                            sv.wait(SY)
                            sy.inc(SY.dma_start(dbg2[:, :], Usb[:, :]), 16)
                        bar()

                    pic = ExitStack()
                    with pic:
                        WhS = sb("WhS", [P, KC * M3], BF16, pic)
                        wsc3 = sb("wsc3", [P, 2048], F32, pic)
                        tfb = sb("tfb", [P, 2 * 512], F32, pic)
                        tzb = sb("tzb", [P, 512], F32, pic)
                        sgbuf = sb("sgbuf", [P, 512], F32, pic)
                        tfbB = sb("tfbB", [P, 2 * 512], F32, pic)
                        tzbB = sb("tzbB", [P, 512], F32, pic)
                        sgbufB = sb("sgbufB", [P, 512], F32, pic)
                        dtb = sb("dtb", [P, 512], F32, pic)
                        ptb = sb("ptb", [P, 512], F32, pic)

                        for kk in range(KC):
                            sv.wait(SY)
                            dma_fence()
                            sy.inc(SY.dma_start(wsc3[:], Wh4[kk * P:(kk + 1) * P, :]), 16)
                            sy.wait(V)
                            V.tensor_copy(WhS[:, kk * M3:kk * M3 + 1024], wsc3[:, 0:1024])
                            sv.inc(V.scalar_tensor_tensor(WhS[:, kk * M3 + 1024:(kk + 1) * M3],
                                                          wsc3[:, 1024:1536], 0.1,
                                                          wsc3[:, 1536:2048], ALU.mult, ALU.add))
                        bar()
                        NGRP = len(UW_T) * KC
                        hw_marks = [sv.n]
                        ac_marks = []
                        g_all = 0
                        pend = []
                        for k in range(NSWEEP):
                            parR = (k % 2) * CPAR
                            parW = ((k + 1) % 2) * CPAR
                            for ti, (wi0, tn) in enumerate(UW_T):
                                for kap in range(KC):
                                    trip = ptrip[g_all % 2]
                                    T.wait_ge(sv.h, hw_marks[k])
                                    if g_all >= 2:
                                        sa.wait(T, ac_marks[g_all - 2])
                                    for gi, mpart in enumerate((kap, 4 + kap, 8 + kap)):
                                        T.matmul(trip[gi][:, 0:tn], idsb[:],
                                                 Usb[:, mpart * WLEN + wi0:
                                                     mpart * WLEN + wi0 + tn],
                                                 start=True, stop=False)
                                        for kk in range(KC):
                                            mm = T.matmul(
                                                trip[gi][:, 0:tn],
                                                WhS[:, kk * M3 + mpart * P:kk * M3 + (mpart + 1) * P],
                                                cfc[:, parR + kk * NCOL + wi0:
                                                     parR + kk * NCOL + wi0 + tn],
                                                start=False, stop=(kk == KC - 1))
                                    sp.inc(mm)
                                    sp.wait(SC)
                                    sv.wait(SC)
                                    tf_, tz_, sg2_ = ((tfb, tzb, sgbuf) if g_all % 2 == 0
                                                      else (tfbB, tzbB, sgbufB))
                                    SC.activation(tf_[:, 0:tn], trip[0][:, 0:tn], AF.Tanh)
                                    SC.activation(tf_[:, 512:512 + tn], trip[1][:, 0:tn], AF.Tanh)
                                    SC.activation(sg2_[:, 0:tn], trip[2][:, 0:tn], AF.Sigmoid)
                                    sa.inc(SC.activation(tz_[:, 0:tn], trip[2][:, 0:tn],
                                                         AF.Sigmoid, scale=cneg1[:, 0:1]))
                                    ac_marks.append(sa.n)
                                    sa.wait(V)
                                    da, pa = ((dtb, ptb) if g_all % 2 == 0
                                              else (hmt, hmt2))
                                    V.tensor_tensor(da[:, 0:tn], tf_[:, 0:tn],
                                                    tz_[:, 0:tn], ALU.mult)
                                    V.tensor_tensor(pa[:, 0:tn], tf_[:, 512:512 + tn],
                                                    sg2_[:, 0:tn], ALU.mult)
                                    if g_all % 2 == 1 or (ti == len(UW_T) - 1
                                                          and kap == KC - 1):
                                        V.drain()
                                        for (pw, da_, pa_, tn_) in pend:
                                            sv.inc(V.tensor_tensor(
                                                cfc[:, pw:pw + tn_], da_[:, 0:tn_],
                                                pa_[:, 0:tn_], ALU.add))
                                        sv.inc(V.tensor_tensor(
                                            cfc[:, parW + kap * NCOL + wi0 + 1:
                                                 parW + kap * NCOL + wi0 + 1 + tn],
                                            da[:, 0:tn], pa[:, 0:tn], ALU.add))
                                        pend = []
                                    else:
                                        pend.append((parW + kap * NCOL + wi0 + 1,
                                                     da, pa, tn))
                                    g_all += 1
                            hw_marks.append(sv.n)
                        bar()
                if debug:
                    sy.inc(SY.dma_start(dbgb[:, :], cfc[:, :]), 16)
                    bar()

            op = ExitStack()
            with op:
                owTs = sb("owTs", [P, KC * D], BF16, op)
                x1a = sb("x1a", [P, DC * 512], F32, op)
                x1b = sb("x1b", [P, DC * 512], F32, op)
                ya = sb("ya", [P, DC * 512], F32, op)
                yb = sb("yb", [P, DC * 512], F32, op)
                x2ma = sb("x2ma", [P, 512], F32, op)
                x2mr = sb("x2mr", [P, 128], F32, op)
                x2t = sb("x2t", [P, DC * 512], F32, op)

                for kk in range(KC):
                    sg.inc(G.dma_start(owTs[:, kk * D:(kk + 1) * D],
                                       owT[kk * P:(kk + 1) * P, :]), 16)
                sy.inc(SY.dma_start(x2ma[:], x2m0i), 16)
                sy.inc(SY.dma_start(x2mr[:], x2m4i), 16)
                bar()
                for ti, (wi0, tn) in enumerate(UW_T):
                    sv.wait(SY)
                    dma_fence()
                    sy.inc(SY.dma_start(sb3(x1a, DC, tn, 512), hbm3(x1buf, 130 + wi0, DC, tn)), 16)
                    sy.inc(SY.dma_start(sb3(x1b, DC, tn, 512), hbm3(x1buf, 2178 + wi0, DC, tn)), 16)
                    sy.inc(SY.dma_start(sb3(ya, DC, tn, 512), hbm3(ybuf, 2 + wi0, DC, tn)), 16)
                    sy.inc(SY.dma_start(sb3(yb, DC, tn, 512), hbm3(ybuf, 2050 + wi0, DC, tn)), 16)
                    sy.wait(V)
                    sg.wait(T)
                    for dc_ in range(DC):
                        bank = pb[dc_ % 2]
                        sev.wait(T, sev.n - 1)
                        for kk in range(KC):
                            base = kk * NCOL + wi0 + 1
                            mm = T.matmul(bank[:, 0:tn],
                                          owTs[:, kk * D + dc_ * P:kk * D + (dc_ + 1) * P],
                                          cfc[:, base:base + tn],
                                          start=(kk == 0), stop=(kk == KC - 1))
                        sp.inc(mm)
                        sp.wait(V)
                        co = dc_ * 512
                        V.tensor_tensor(hmt[:, 0:tn], x1a[:, co:co + tn], ya[:, co:co + tn],
                                        ALU.add)
                        V.tensor_tensor(hmt2[:, 0:tn], x1b[:, co:co + tn], yb[:, co:co + tn],
                                        ALU.add)
                        ev = V.tensor_scalar(x2t[:, co:co + tn], bank[:, 0:tn],
                                             obs[:, dc_:dc_ + 1], None, ALU.add)
                        sev.inc(ev)
                        V.drain()
                        V.scalar_tensor_tensor(hmt[:, 0:tn], hmt[:, 0:tn], mks[:, 0:1],
                                               x2t[:, co:co + tn], ALU.mult, ALU.add)
                        V.drain()
                        V.scalar_tensor_tensor(x2t[:, co:co + tn], hmt2[:, 0:tn],
                                               mks[:, 1:2], hmt[:, 0:tn],
                                               ALU.mult, ALU.add)
                        if ti == 0:
                            V.drain()
                            V.tensor_tensor(x2t[:, co:co + tn], x2t[:, co:co + tn],
                                            x2ma[:, 0:tn], ALU.mult)
                        elif ti == 4:
                            V.drain()
                            V.tensor_tensor(x2t[:, co:co + tn], x2t[:, co:co + tn],
                                            x2mr[:, 0:tn], ALU.mult)
                        sv.inc(V.drain())
                    sv.wait(SY)
                    sy.inc(SY.dma_start(hbm3(x2buf, X2OFF + wi0, DC, tn),
                                        sb3(x2t, DC, tn, 512)), 16)
                bar()

        ffn2 = ExitStack()
        with ffn2:
            wq = sb("wq", [P, DC * HD], BF16, ffn2)
            V.memset(a2col[:], 0.0)
            sv.inc(V.drain())
            for kd in range(DC):
                sg.inc(G.dma_start(wq[:, kd * HD:(kd + 1) * HD],
                                   wq1buf[kd * P:(kd + 1) * P, :]), 16)
            bar()
            sta = ExitStack()
            with sta:
                abuf = make_stage_a_bufs(sta)
                V.memset(abuf[5][:], 0.0)
                sv.inc(V.drain())
                bar()
                ffn_stage_a(wq, abuf, x2buf, FF2_A, yq2buf)
            stb = ExitStack()
            with stb:
                xt = sb("xt", [P, DC * 512], F32, stb)
                yy = sb("yy", [P, DC * 512], F32, stb)
                yqb = sb("yqb", [P, HC * 512], BF16, stb)
                for kh in range(HC):
                    sg.inc(G.dma_start(wq[:, kh * D:(kh + 1) * D],
                                       wq2buf[kh * P:(kh + 1) * P, :]), 16)
                bar()
                ffn_stage_b(wq, xt, yy, yqb, x2buf, FF2_B, yq2buf, outT)

        dma_fence()
        bar()
    return nc


def _prep_inputs(inputs):
    f32 = np.float32
    x = np.asarray(inputs["L0_attn_x"], f32)
    w1 = np.asarray(inputs["L0_ffn_w1"], f32)
    w2 = np.asarray(inputs["L0_ffn_w2"], f32)
    cw = np.asarray(inputs["L0_ffn_conv_w"], f32)
    cb = np.asarray(inputs["L0_ffn_conv_b"], f32)
    Bw = np.asarray(inputs["L0_attn_B_w"], f32)
    Bbv = np.asarray(inputs["L0_attn_B_b"], f32)
    Al = np.asarray(inputs["L0_attn_A_log"], f32)
    Cw = np.asarray(inputs["L0_attn_C_w"], f32)
    Cbv = np.asarray(inputs["L0_attn_C_b"], f32)
    ng = np.asarray(inputs["L0_attn_norm_g"], f32)
    nb = np.asarray(inputs["L0_attn_norm_b"], f32)
    f1w = np.asarray(inputs["L0_attn_ff1_w"], f32)
    f2w = np.asarray(inputs["L0_attn_ff2_w"], f32)
    taw = np.asarray(inputs["L0_attn_ta_w"], f32)
    tbw = np.asarray(inputs["L0_attn_tb_w"], f32)
    f1b = np.asarray(inputs["L0_attn_ff1_b"], f32)
    f2b = np.asarray(inputs["L0_attn_ff2_b"], f32)
    tab = np.asarray(inputs["L0_attn_ta_b"], f32)
    tbb = np.asarray(inputs["L0_attn_tb_b"], f32)
    ow = np.asarray(inputs["L0_attn_out_w"], f32)
    obv = np.asarray(inputs["L0_attn_out_b"], f32)

    shared = {
        "w1T": np.ascontiguousarray(w1.T),
        "w2T": np.ascontiguousarray(w2.T),
        "convw": np.ascontiguousarray(
            cw[:, 0, :].reshape(HC, P, 3).transpose(1, 0, 2).reshape(P, HC * 3)),
        "convb": np.ascontiguousarray(cb.reshape(HC, P).T),
        "normg": np.ascontiguousarray(ng.reshape(DC, P).T),
        "normb": np.ascontiguousarray(nb.reshape(DC, P).T),
        "BwT": np.ascontiguousarray(Bw.T),
        "Bb": np.ascontiguousarray(Bbv.reshape(P, 1)),
        "Alog": np.ascontiguousarray(Al.reshape(P, 1)),
        "CwT": np.ascontiguousarray(Cw.T),
        "Cb": np.ascontiguousarray(Cbv.reshape(DC, P).T),
        "Wx4": np.ascontiguousarray(np.concatenate(
            [f1w[:, :D].T, f2w[:, :D].T, taw[:, :D].T, tbw[:, :D].T], axis=1)),
        "Wh4": np.ascontiguousarray(np.concatenate(
            [f1w[:, D:].T, f2w[:, D:].T, taw[:, D:].T, tbw[:, D:].T], axis=1)),
        "gb16": np.ascontiguousarray(np.concatenate(
            [f1b.reshape(KC, P).T, f2b.reshape(KC, P).T,
             tab.reshape(KC, P).T, tbb.reshape(KC, P).T], axis=1)),
        "owT": np.ascontiguousarray(ow.T),
        "ob": np.ascontiguousarray(obv.reshape(DC, P).T),
        "ident": np.eye(P, dtype=f32),
    }
    in_maps = []
    for c in range(8):
        b, hf = c // 2, c % 2
        s = 2048 * hf
        xTc = np.zeros((D, XWX), f32)
        xTc[:, 256:256 + S] = x[b].T
        masks_ = np.zeros((P, 8), f32)
        masks_[:, 0] = 1.0 - hf
        masks_[:, 1] = float(hf)
        umask_ = np.ones((P, MC * 512), f32)
        if hf == 0:
            for m in range(MC):
                umask_[:, m * 512:m * 512 + 126] = 0.0
        x2m0_ = np.ones((P, 512), f32)
        if hf == 0:
            x2m0_[:, 0:126] = 0.0
        x2m4_ = np.ones((P, 128), f32)
        if hf == 1:
            x2m4_[:, 126:128] = 0.0
        d = dict(shared)
        d.update({
            "xT": xTc,
            "masks": masks_,
            "umask": umask_,
            "x2m0": x2m0_,
            "x2m4": x2m4_,
            "woff": np.array([[s + 2]], np.uint32),
        })
        in_maps.append(d)
    return in_maps


def kernel(**inputs):
    in_maps = _prep_inputs(inputs)
    nc = build_nc(debug=False)
    res = run_bass_kernel_spmd(nc, in_maps, core_ids=list(range(8)))
    out = np.empty((4, S, D), np.float32)
    for c in range(8):
        b, hf = c // 2, c % 2
        o = np.asarray(res.results[c]["outT"])
        out[b, 2048 * hf:2048 * (hf + 1), :] = o[:, OUT_LO:OUT_HI].T
    return out


# version 11
# speedup vs baseline: 1.2385x; 1.0466x over previous
import numpy as np
import ml_dtypes
import concourse.bass as bass
import concourse.mybir as mybir
from concourse.bass_utils import run_bass_kernel_spmd
from contextlib import ExitStack

F32 = mybir.dt.float32
BF16 = mybir.dt.bfloat16
U32 = mybir.dt.uint32
AF = mybir.ActivationFunctionType
ALU = mybir.AluOpType
AX = mybir.AxisListType
ds = bass.ds

P = 128
S = 4096
D = 1024
DC = 8
HD = 4096
HC = 32
NS = 128
H = 512
KC = 4
M3 = 1536
MC = 12
XW = 4608
WLEN = 2176
NCOL = WLEN + 1
CPAR = KC * NCOL
NSWEEP = 15
YPAD = 128
LN_EPS = 1e-5
C32c = 1.5 * 2.0 ** 23
C64c = 1.5 * 2.0 ** 52
LN1E8 = 18.420680743952367

TOA = 256
TWA = TOA + 2
XWX = 4640
FF1_A = [128 + TOA * j for j in range(17)]
FF1_B = [512 * j for j in range(9)]
MID_T = [256 + 512 * j for j in range(8)]
UW_T = [(0, 512), (512, 512), (1024, 512), (1536, 512), (2048, 128)]
X2W = 2688
X2OFF = 192
FF2_A = [256 + TOA * j for j in range(9)]
FF2_B = [512 * j for j in range(5)]
OUT_LO, OUT_HI = 318, 2366


class Sem:
    def __init__(self, stack, nc, name):
        self.h = stack.enter_context(nc.semaphore(name))
        self.n = 0

    def inc(self, instr, by=1):
        instr.then_inc(self.h, by)
        self.n += by
        return instr

    def wait(self, eng, val=None):
        v = self.n if val is None else val
        if v > 0:
            eng.wait_ge(self.h, v)

    def mark(self):
        return self.n


def build_nc(debug=False):
    nc = bass.Bass("TRN2", target_bir_lowering=False)
    A = lambda name, shape, dt=F32: nc.dram_tensor(name, shape, dt, kind="ExternalInput").ap()
    xT = A("xT", [D, XWX])
    w1T = A("w1T", [D, HD])
    w2T = A("w2T", [HD, D])
    convw = A("convw", [P, 96])
    convb = A("convb", [P, HC])
    normg = A("normg", [P, DC])
    normb = A("normb", [P, DC])
    BwT = A("BwT", [D, NS])
    Bbi = A("Bb", [P, 1])
    Alog = A("Alog", [P, 1])
    CwT = A("CwT", [P, D])
    Cbi = A("Cb", [P, DC])
    Wx4 = A("Wx4", [D, 2048])
    Wh4 = A("Wh4", [H, 2048])
    gb16 = A("gb16", [P, 16])
    owT = A("owT", [H, D])
    obi = A("ob", [P, DC])
    ident = A("ident", [P, P])
    masks = A("masks", [P, 8])
    umaski = A("umask", [P, MC * 512])
    x2m0i = A("x2m0", [P, 512])
    x2m4i = A("x2m4", [P, 128])
    woff = A("woff", [1, 1], U32)

    x1buf = nc.dram_tensor("x1buf", [D, XW], F32, kind="Internal").ap()
    ybuf = nc.dram_tensor("ybuf", [D, S + 2 * YPAD], F32, kind="Internal").ap()
    yqbuf = nc.dram_tensor("yqbuf", [HD, XW], BF16, kind="Internal").ap()
    x2buf = nc.dram_tensor("x2buf", [D, X2W], F32, kind="Internal").ap()
    yq2buf = nc.dram_tensor("yq2buf", [HD, X2W], BF16, kind="Internal").ap()
    wq1buf = nc.dram_tensor("wq1buf", [D, HD], BF16, kind="Internal").ap()
    wq2buf = nc.dram_tensor("wq2buf", [HD, D], BF16, kind="Internal").ap()
    outT = nc.dram_tensor("outT", [D, X2W], F32, kind="ExternalOutput").ap()
    if debug:
        dbgb = nc.dram_tensor("dbgb", [P, 2 * CPAR], BF16, kind="ExternalOutput").ap()
        dbg2 = nc.dram_tensor("dbg2", [P, MC * WLEN], BF16, kind="ExternalOutput").ap()
        dbg3 = nc.dram_tensor("dbg3", [P, YPAD + S + YPAD], BF16, kind="ExternalOutput").ap()

    st = ExitStack()
    _nm = [0]
    with st:
        def sb(name, shape, dt=F32, stk=None):
            _nm[0] += 1
            return (stk or st).enter_context(nc.sbuf_tensor(f"{name}_{_nm[0]}", shape, dt))

        def ps_(name):
            return st.enter_context(nc.psum_tensor(name, [P, 512], F32))

        onesb = sb("onesb", [P, 1], BF16)
        onesf = sb("onesf", [P, 1])
        idsf = sb("idsf", [P, P])
        idneg = sb("idneg", [P, P])
        cmag = sb("cmag", [P, 1])
        cnmag = sb("cnmag", [P, 1])
        halfc = sb("halfc", [P, 1])
        epsc = sb("epsc", [P, 1])
        rw1c = sb("rw1c", [P, 1])
        rw2h = sb("rw2h", [P, 1])
        swcol = sb("swcol", [P, 1])
        convws = sb("convws", [P, 96])
        convbs = sb("convbs", [P, HC])
        gcol = sb("gcol", [P, DC])
        bcolv = sb("bcolv", [P, DC])
        cbs = sb("cbs", [P, DC])
        obs = sb("obs", [P, DC])
        bbs = sb("bbs", [P, 1])
        mks = sb("mks", [P, 8])
        alcol = sb("alcol", [P, 2])
        tcol = sb("tcol", [P, 8])
        stg = sb("stg", [P, 8])
        r11 = sb("r11", [1, 8])
        statr = sb("statr", [1, 4 * 512])
        ones128 = sb("ones128", [P, P])
        repb = sb("repb", [P, 4 * P])
        a2col = sb("a2col", [P, 40])
        cm = sb("cm", [P, 512])
        cmt = sb("cmt", [P, 512])
        zsb = sb("zsb", [P, 512], BF16)
        woffs = sb("woffs", [1, 1], U32)
        hmt = sb("hmt", [P, 512])
        hmt2 = sb("hmt2", [P, 512])
        a1s = sb("a1s", [P, 512])
        onesr = sb("onesr", [1, P])
        idsb = sb("idsb", [P, P], BF16)
        cneg1 = sb("cneg1", [P, 1])

        pb = [ps_(f"pb{i}") for i in range(4)]
        pbc = ps_("pbc")
        pbc2 = ps_("pbc2")
        pst = ps_("pst")
        pst2 = ps_("pst2")
        ptrip = [[pb[0], pb[1], pb[2]], [pb[3], pst, pst2]]

        sy = Sem(st, nc, "sy")
        sg = Sem(st, nc, "sg")
        sv = Sem(st, nc, "sv")
        sa = Sem(st, nc, "sa")
        sp = Sem(st, nc, "sp")
        sev = Sem(st, nc, "sev")
        su = Sem(st, nc, "su")

        V, SC, T, G, SY = nc.vector, nc.scalar, nc.tensor, nc.gpsimd, nc.sync

        def tnop_inc(sem):
            T.drain()
            sem.inc(T.nop())

        def dma_fence():
            SY.wait_ge(sy.h, sy.n) if sy.n else None
            G.wait_ge(sg.h, sg.n) if sg.n else None

        def bar():
            dma_fence()
            nc.all_engine_barrier()

        sy.inc(SY.dma_start(idsf[:], ident), 16)
        sy.inc(SY.dma_start(convws[:], convw), 16)
        sy.inc(SY.dma_start(convbs[:], convb), 16)
        sy.inc(SY.dma_start(gcol[:], normg), 16)
        sy.inc(SY.dma_start(bcolv[:], normb), 16)
        sy.inc(SY.dma_start(cbs[:], Cbi), 16)
        sy.inc(SY.dma_start(obs[:], obi), 16)
        sy.inc(SY.dma_start(bbs[:], Bbi), 16)
        sy.inc(SY.dma_start(mks[:], masks), 16)
        sy.inc(SY.dma_start(woffs[:], woff), 16)
        sy.inc(SY.dma_start(alcol[:, 0:1], Alog), 16)
        sy.wait(V)
        V.memset(onesf[:], 1.0)
        V.memset(onesr[:], 1.0)
        V.memset(ones128[:], 1.0)
        V.memset(a2col[:], 0.0)
        V.memset(tcol[:], 1.0)
        V.memset(stg[:], 1.0)
        V.memset(cm[:], 0.0)
        V.memset(cmt[:], 0.0)
        V.memset(statr[:], 0.0)
        V.memset(hmt[:], 0.0)
        V.memset(hmt2[:], 0.0)
        V.memset(a1s[:], 0.0)
        V.memset(repb[:], 0.0)
        V.memset(halfc[:], 0.5)
        V.memset(epsc[:], LN_EPS)
        V.memset(zsb[:], 0.0)
        V.memset(cneg1[:], -1.0)
        V.drain()
        sy.wait(V)
        V.tensor_copy(idsb[:], idsf[:])
        V.tensor_scalar(idneg[:], idsf[:], -1.0, None, ALU.mult)
        V.drain()
        V.tensor_copy(onesb[:], onesf[:])
        V.drain()
        V.tensor_scalar(cmag[:], onesf[:], 2.0 ** -30, None, ALU.add)
        V.drain()
        V.tensor_scalar(cmag[:], cmag[:], -1.0, None, ALU.add)
        V.drain()
        V.tensor_scalar(cmag[:], cmag[:], 2.0 ** 30, None, ALU.mult)
        V.drain()
        V.tensor_scalar(cmag[:], cmag[:], C64c, C32c, ALU.mult, ALU.add)
        V.drain()
        V.tensor_scalar(cnmag[:], cmag[:], -1.0, None, ALU.mult)
        V.drain()
        bar()

        def hbm3(t, col0, nch, w):
            C = t.shape[1]
            return bass.AP(t.tensor, col0, [[C, P], [P * C, nch], [1, w]])

        def sb3(t, nch, w, tw, col0=0):
            return bass.AP(t, col0, [[nch * tw, P], [tw, nch], [1, w]])

        def stats_rows(src, nch, tw, xbb, sqbb):
            tot = nch * tw
            sp.wait(V)
            V.tensor_copy(xbb[:, 0:tot], src[:, 0:tot])
            sv.inc(V.tensor_tensor(sqbb[:, 0:tot], src[:, 0:tot], src[:, 0:tot], ALU.mult))
            sv.wait(T)
            for c in range(nch):
                T.matmul(pst[0:1, 0:tw], onesb[:, 0:1], xbb[:, c * tw:(c + 1) * tw],
                         start=(c == 0), stop=(c == nch - 1))
            for c in range(nch):
                mm = T.matmul(pst2[0:1, 0:tw], onesb[:, 0:1], sqbb[:, c * tw:(c + 1) * tw],
                              start=(c == 0), stop=(c == nch - 1))
            sp.inc(mm)
            sp.wait(V)
            mu = statr[0:1, 1024:1024 + tw]
            va = statr[0:1, 1536:1536 + tw]
            rr = statr[0:1, 0:tw]
            mr = statr[0:1, 512:512 + tw]
            V.tensor_scalar(mu, pst[0:1, 0:tw], 1.0 / (nch * P), None, ALU.mult)
            V.tensor_scalar(va, pst2[0:1, 0:tw], 1.0 / (nch * P), None, ALU.mult)
            V.drain()
            V.scalar_tensor_tensor(rr, pst[0:1, 0:tw], -1.0 / (nch * P), mu,
                                   ALU.mult, ALU.mult)
            V.drain()
            V.tensor_tensor(va, va, rr, ALU.add)
            sv.inc(V.drain())
            sv.wait(SC)
            sa.inc(SC.activation(rr, va, AF.Sqrt, bias=epsc[0:1, 0:1]))
            sa.wait(V)
            V.reciprocal(rr, rr)
            V.drain()
            V.tensor_tensor(mr, mu, rr, ALU.mult)
            sv.inc(V.drain())

        def ln_apply(src, dst, nch, tw):
            sv.wait(T)
            T.matmul(pbc[:, 0:tw], onesr[0:1, 0:P], statr[0:1, 0:tw], start=True, stop=True)
            T.matmul(pbc2[:, 0:tw], onesr[0:1, 0:P], statr[0:1, 512:512 + tw],
                     start=True, stop=True)
            tnop_inc(sp)
            sp.wait(V)
            for c in range(nch):
                V.tensor_tensor(dst[:, c * tw:(c + 1) * tw], src[:, c * tw:(c + 1) * tw],
                                pbc[:, 0:tw], ALU.mult)
            V.drain()
            for c in range(nch):
                V.tensor_tensor(dst[:, c * tw:(c + 1) * tw], dst[:, c * tw:(c + 1) * tw],
                                pbc2[:, 0:tw], ALU.subtract)
            sv.inc(V.drain())

        def quant_cols(src, nch, tw, pools, nblk, acol_dst=None, halves=1):
            ap3 = bass.AP(src.tensor if hasattr(src, "tensor") else src, 0,
                          [[nch * tw, P], [1, tw], [tw, nch]])
            sv.wait(V)
            sv.inc(V.tensor_reduce(cm[:, 0:tw], ap3, AX.X, ALU.max,
                                   apply_absolute_value=True))
            sv.wait(T)
            for j in range(nblk):
                w = min(128, tw - j * 128)
                T.transpose(pst[0:w, j * 128:j * 128 + 128],
                            cm[:, j * 128:j * 128 + w], idsf[:])
            tnop_inc(sp)
            sp.wait(V)
            for j in range(nblk):
                w = min(128, tw - j * 128)
                V.tensor_reduce(tcol[0:w, j:j + 1], pst[0:w, j * 128:(j + 1) * 128],
                                AX.X, ALU.max)
            V.drain()
            V.tensor_scalar(tcol[:, 0:nblk], tcol[:, 0:nblk], 1e-5, None, ALU.max)
            V.drain()
            if acol_dst is not None:
                V.tensor_copy(acol_dst, tcol[:, 0:nblk])
            sv.inc(V.reciprocal(stg[:, 0:nblk], tcol[:, 0:nblk]))
            sv.wait(SC)
            SC.drain()
            SC.activation(stg[:, 4:4 + nblk], tcol[:, 0:nblk], AF.Identity,
                          scale=1.0 / 127.0)
            sa.inc(SC.activation(stg[:, 0:nblk], stg[:, 0:nblk], AF.Identity,
                                 scale=127.0))

        def colbcast(psum, col_src, ncols, nblk):
            sp.wait(V)
            sa.wait(V)
            for j in range(nblk):
                op = V.tensor_scalar(repb[:, j * P:(j + 1) * P], ones128[:],
                                     col_src[:, j:j + 1], None, ALU.mult)
            sv.inc(op)
            sv.wait(T)
            for j in range(nblk):
                T.transpose(psum[0:P, j * 128:(j + 1) * 128],
                            repb[:, j * P:(j + 1) * P], idsf[:])
            tnop_inc(sp)
            return None

        def roundq_act(dst, src, tot):
            sa.inc(SC.activation(src[:, 0:tot], src[:, 0:tot], AF.Identity,
                                 bias=cmag[:, 0:1]))
            sa.wait(SC)
            SC.drain()
            sa.inc(SC.activation(dst[:, 0:tot], src[:, 0:tot], AF.Identity,
                                 bias=cnmag[:, 0:1]))

        def quant_weights(wsrc, nchunks, width, wq_t, wqhbm, rw_dst, wscb,
                          rw_scale=1.0 / 127.0):
            def slot(c):
                return wscb[:, (c % 2) * width:(c % 2) * width + width]
            qs = [(sy, SY), (sg, G)]
            dma_fence()
            vmark = [sv.n, sv.n]
            for c in range(nchunks):
                sm, Q = qs[c % 2]
                sv.wait(Q, vmark[c % 2])
                sm.inc(Q.dma_start(slot(c), wsrc[c * P:(c + 1) * P, :]), 16)
                sm.wait(V)
                sv.inc(V.tensor_reduce(cm[:, c:c + 1], slot(c), AX.X,
                                       ALU.add, apply_absolute_value=True))
                vmark[c % 2] = sv.n
            V.drain()
            V.tensor_reduce(hmt[:, 0:1], cm[:, 0:nchunks], AX.X, ALU.add)
            sv.inc(V.drain())
            sv.wait(T)
            T.matmul(pst[0:1, 0:1], hmt[:, 0:1], onesf[:, 0:1], start=True, stop=True)
            tnop_inc(sp)
            sp.wait(V)
            V.tensor_scalar(r11[0:1, 0:1], pst[0:1, 0:1], 1.0 / (nchunks * P * width),
                            None, ALU.mult)
            V.drain()
            V.tensor_scalar(r11[0:1, 0:1], r11[0:1, 0:1], 1e-5, None, ALU.max)
            V.drain()
            V.reciprocal(r11[0:1, 1:2], r11[0:1, 0:1])
            V.drain()
            V.tensor_scalar(r11[0:1, 2:3], r11[0:1, 0:1], rw_scale, None, ALU.mult)
            sv.inc(V.drain())
            sv.wait(T)
            T.matmul(pst[0:P, 8:9], onesr[0:1, 0:P], r11[0:1, 1:2], start=True, stop=True)
            T.matmul(pst[0:P, 9:10], onesr[0:1, 0:P], r11[0:1, 2:3],
                     start=True, stop=True)
            tnop_inc(sp)
            sp.wait(V)
            V.tensor_copy(swcol[:, 0:1], pst[0:P, 8:9])
            V.tensor_copy(rw_dst[:, 0:1], pst[0:P, 9:10])
            sv.inc(V.drain())
            vmark = [sv.n, sv.n]
            for c in range(nchunks):
                sm, Q = qs[c % 2]
                sv.wait(Q, vmark[c % 2])
                sm.inc(Q.dma_start(slot(c), wsrc[c * P:(c + 1) * P, :]), 16)
                sm.wait(SC)
                SC.drain()
                SC.activation(slot(c), slot(c), AF.Identity,
                              bias=cmag[:, 0:1], scale=swcol[:, 0:1])
                SC.drain()
                sa.inc(SC.activation(slot(c), slot(c), AF.Identity,
                                     bias=cnmag[:, 0:1]))
                sa.wait(V)
                dst = wq_t[:, c * width:(c + 1) * width]
                sv.inc(V.tensor_scalar(dst, slot(c), 1.0, -1.0, ALU.min, ALU.max))
                vmark[c % 2] = sv.n
                sv.wait(Q)
                sm.inc(Q.dma_start(wqhbm[c * P:(c + 1) * P, :], dst), 16)
            bar()

        def ffn_stage_a(wq, abuf, src_hbm, c0list, yq_hbm):
            xta, yya, xbb, sqbb, xq, y2, yqa, xqb, tg = abuf
            xqs = [xq, xqb]
            tw = TWA

            def prep_chunks(ii, c0):
                xq_t = xqs[ii % 2]

                def c_load():
                    sv.wait(SY)
                    sa.wait(SY)
                    dma_fence()
                    sy.inc(SY.dma_start(sb3(xta, DC, tw, tw),
                                        hbm3(src_hbm, c0 - 1, DC, tw)), 16)

                def c_stats():
                    sy.wait(V)
                    stats_rows(xta, DC, tw, xbb, sqbb)

                def c_ln():
                    ln_apply(xta, yya, DC, tw)

                def c_quant():
                    quant_cols(yya, DC, tw, None, 3)

                def c_smul():
                    colbcast(pbc, stg[:, 0:4], tw, 3)
                    sp.wait(V)
                    for c in range(DC):
                        V.tensor_tensor(yya[:, c * tw:(c + 1) * tw],
                                        yya[:, c * tw:(c + 1) * tw],
                                        pbc[:, 0:tw], ALU.mult)
                    sv.inc(V.drain())
                    sv.wait(SC)
                    roundq_act(yya, yya, DC * tw)

                def c_xq():
                    colbcast(pbc2, stg[:, 4:8], tw, 3)
                    sa.wait(V)
                    sp.wait(V)
                    for c in range(DC):
                        V.tensor_tensor(xq_t[:, c * tw:(c + 1) * tw],
                                        yya[:, c * tw:(c + 1) * tw],
                                        pbc2[:, 0:tw], ALU.mult)
                    sv.inc(V.drain())

                return [c_load, c_stats, c_ln, c_quant, c_smul, c_xq]

            nt = len(c0list)
            for ch in prep_chunks(0, c0list[0]):
                ch()
            for ti in range(nt):
                c0 = c0list[ti]
                xq_t = xqs[ti % 2]
                nxt = (prep_chunks(ti + 1, c0list[ti + 1]) if ti + 1 < nt
                       else [])
                sv.wait(T)
                for q in range(HC // 4):
                    for j in range(4):
                        mh = 4 * q + j
                        sev.wait(T, sev.n - 3 + j)
                        for kd in range(DC):
                            mm = T.matmul(pb[j][:, 0:tw],
                                          wq[:, kd * HD + mh * P:kd * HD + (mh + 1) * P],
                                          xq_t[:, kd * tw:(kd + 1) * tw],
                                          start=(kd == 0), stop=(kd == DC - 1))
                        sp.inc(mm)
                    if nxt:
                        nxt.pop(0)()
                    sp.wait(SC)
                    SC.drain()
                    for j in range(4):
                        mh = 4 * q + j
                        a1 = SC.activation(tg[:, (mh % 8) * TOA:(mh % 8 + 1) * TOA],
                                           pb[j][:, 0:tw - 2], AF.Identity,
                                           bias=convbs[:, mh:mh + 1],
                                           scale=convws[:, 3 * mh:3 * mh + 1])
                    sa.inc(a1)
                    sa.wait(V)
                    for j in range(4):
                        mh = 4 * q + j
                        V.scalar_tensor_tensor(tg[:, (mh % 8) * TOA:(mh % 8 + 1) * TOA],
                                               pb[j][:, 1:tw - 1],
                                               convws[:, 3 * mh + 1:3 * mh + 2],
                                               tg[:, (mh % 8) * TOA:(mh % 8 + 1) * TOA],
                                               ALU.mult, ALU.add)
                    V.drain()
                    for j in range(4):
                        mh = 4 * q + j
                        ev = V.scalar_tensor_tensor(tg[:, (mh % 8) * TOA:(mh % 8 + 1) * TOA],
                                                    pb[j][:, 2:tw],
                                                    convws[:, 3 * mh + 2:3 * mh + 3],
                                                    tg[:, (mh % 8) * TOA:(mh % 8 + 1) * TOA],
                                                    ALU.mult, ALU.add)
                        sev.inc(ev)
                    sv.inc(V.engine_nop())
                    sv.wait(SC)
                    SC.drain()
                    for j in range(4):
                        mh = 4 * q + j
                        a2 = SC.activation(y2[:, mh * TOA:(mh + 1) * TOA],
                                           tg[:, (mh % 8) * TOA:(mh % 8 + 1) * TOA],
                                           AF.Silu)
                    sa.inc(a2)
                sa.wait(V)
                quant_cols(y2, HC, TOA, None, 2, halves=2)
                colbcast(pbc, stg[:, 0:4], TOA, 2)
                sp.wait(V)
                for c in range(HC):
                    V.tensor_tensor(y2[:, c * TOA:(c + 1) * TOA], y2[:, c * TOA:(c + 1) * TOA],
                                    pbc[:, 0:TOA], ALU.mult)
                sv.inc(V.drain())
                colbcast(pbc2, stg[:, 4:8], TOA, 2)
                hh = HC // 2
                for half in range(2):
                    sy.wait(SC)
                    sv.wait(SC)
                    y2h = bass.AP(y2, half * hh * TOA, [[HC * TOA, P], [1, hh * TOA]])
                    SC.drain()
                    SC.activation(y2h, y2h, AF.Identity, bias=cmag[:, 0:1])
                    SC.drain()
                    sa.inc(SC.activation(y2h, y2h, AF.Identity, bias=cnmag[:, 0:1]))
                    sa.wait(V)
                    sy.wait(V)
                    sp.wait(V)
                    for c in range(hh):
                        V.tensor_tensor(yqa[:, c * TOA:(c + 1) * TOA],
                                        bass.AP(y2, (half * hh + c) * TOA,
                                                [[HC * TOA, P], [1, TOA]]),
                                        pbc2[:, 0:TOA], ALU.mult)
                    sv.inc(V.drain())
                    sv.wait(SY)
                    dma_fence()
                    sy.inc(SY.dma_start(
                        hbm3(yq_hbm, c0, hh, TOA) if half == 0 else
                        bass.AP(yq_hbm.tensor, hh * P * yq_hbm.shape[1] + c0,
                                [[yq_hbm.shape[1], P], [P * yq_hbm.shape[1], hh], [1, TOA]]),
                        bass.AP(yqa, 0, [[hh * TOA, P], [TOA, hh], [1, TOA]])), 16)
            bar()

        def ffn_stage_b(wq, xt, yy, yqbs, src_res_hbm, c0list, yq_hbm, out_hbm):
            tw = 512

            def load_yqb(ii):
                yqb = yqbs[ii % 2]
                sv.wait(G)
                sp.wait(G)
                G.wait_ge(sg.h, sg.n) if sg.n else None
                sg.inc(G.dma_start(
                    bass.AP(yqb, 0, [[HC * tw, P], [tw, HC], [1, tw]]),
                    hbm3(yq_hbm, c0list[ii], HC, tw)), 16)
                return sg.n

            gmark = load_yqb(0)
            for ti, c0 in enumerate(c0list):
                yqb = yqbs[ti % 2]
                sv.wait(SY)
                SY.wait_ge(sy.h, sy.n) if sy.n else None
                sy.inc(SY.dma_start(sb3(xt, DC, tw, tw),
                                    hbm3(src_res_hbm, c0, DC, tw)), 16)
                xmark = sy.n
                sg.wait(T, gmark)
                sg.wait(V, gmark)
                sy.wait(V, xmark)
                if ti + 1 < len(c0list):
                    gmark = load_yqb(ti + 1)
                for md in range(DC):
                    bank = pb[md % 2]
                    sev.wait(T, sev.n - 1)
                    for kh in range(HC):
                        mm = T.matmul(bank[:, 0:tw],
                                      wq[:, kh * D + md * P:kh * D + (md + 1) * P],
                                      yqb[:, kh * tw:(kh + 1) * tw],
                                      start=(kh == 0), stop=(kh == HC - 1))
                    sp.inc(mm)
                    sp.wait(V)
                    ev = V.scalar_tensor_tensor(yy[:, md * tw:(md + 1) * tw],
                                                bank[:, 0:tw], rw2h[:, 0:1],
                                                xt[:, md * tw:(md + 1) * tw],
                                                ALU.mult, ALU.add)
                    sev.inc(ev)
                    sv.inc(V.engine_nop())
                sv.wait(SY)
                sy.inc(SY.dma_start(hbm3(out_hbm, c0, DC, tw), sb3(yy, DC, tw, tw)), 16)
                stmark = sy.n
            bar()

        def make_stage_a_bufs(stk):
            xta = sb("xta", [P, DC * TWA], F32, stk)
            yya = sb("yya", [P, DC * TWA], F32, stk)
            xbb = sb("xbb", [P, DC * TWA], BF16, stk)
            sqbb = sb("sqbb", [P, DC * TWA], BF16, stk)
            xq = sb("xq", [P, DC * TWA], BF16, stk)
            y2 = sb("y2", [P, HC * TOA], F32, stk)
            yqa = sb("yqa", [P, (HC // 2) * TOA], BF16, stk)
            xqb = sb("xqb", [P, DC * TWA], BF16, stk)
            tg = sb("tg", [P, 8 * TOA], F32, stk)
            return (xta, yya, xbb, sqbb, xq, y2, yqa, xqb, tg)

        ffn1 = ExitStack()
        with ffn1:
            wq = sb("wq", [P, DC * HD], BF16, ffn1)
            zf = sb("zf", [P, 512], F32, ffn1)

            for col0, w, t, nch in [(0, 130, yqbuf, HC), (4480, 128, yqbuf, HC),
                                    (0, 317, yq2buf, HC), (2560, 128, yq2buf, HC)]:
                sy.inc(SY.dma_start(hbm3(t, col0, nch, w),
                                    bass.AP(zsb, 0, [[512, P], [0, nch], [1, w]])), 16)
            V.memset(zf[:], 0.0)
            sv.inc(V.drain())
            sv.wait(SY)
            for col0, w, t, nch in [(0, YPAD, ybuf, DC), (S + YPAD, YPAD, ybuf, DC),
                                    (0, X2OFF, x2buf, DC), (2368, X2W - 2368, x2buf, DC)]:
                sy.inc(SY.dma_start(hbm3(t, col0, nch, w),
                                    bass.AP(zf, 0, [[512, P], [0, nch], [1, w]])), 16)
            bar()
            sta = ExitStack()
            with sta:
                abuf = make_stage_a_bufs(sta)
                quant_weights(w1T, DC, HD, wq, wq1buf, rw1c, abuf[5], rw_scale=1.0)
                sv.wait(V)
                V.tensor_scalar(convws[:], convws[:], rw1c[:, 0:1], None, ALU.mult)
                sv.inc(V.drain())
                bar()
                ffn_stage_a(wq, abuf, xT, FF1_A, yqbuf)
            stb = ExitStack()
            with stb:
                xt = sb("xt", [P, DC * 512], F32, stb)
                yy = sb("yy", [P, DC * 512], F32, stb)
                yqb = [sb("yqb", [P, HC * 512], BF16, stb),
                       sb("yqb2", [P, HC * 512], BF16, stb)]
                quant_weights(w2T, HC, D, wq, wq2buf, rw2h, yy, rw_scale=1.0)
                sv.wait(V)
                V.tensor_scalar(rw2h[:], rw2h[:], 0.5, None, ALU.mult)
                sv.inc(V.drain())
                bar()
                ffn_stage_b(wq, xt, yy, yqb, xT, FF1_B, yqbuf, x1buf)

        carry1 = ExitStack()
        with carry1:
            cfc = sb("cfc", [P, 2 * CPAR], BF16, carry1)
            carry2 = ExitStack()
            with carry2:
                swin = sb("swin", [P, WLEN], BF16, carry2)
                if True:

                    mid = ExitStack()
                    with mid:
                        xt = sb("xt", [P, DC * 512], F32, mid)
                        xbb = sb("xbb", [P, DC * 512], BF16, mid)
                        sqbb = sb("sqbb", [P, DC * 512], BF16, mid)
                        yy = sb("yy", [P, DC * 512], F32, mid)
                        xq = sb("xq", [P, DC * 512], BF16, mid)
                        Bsb = sb("Bsb", [P, S], F32, mid)
                        states = sb("states", [P, YPAD + S + YPAD], BF16, mid)
                        BwTs = sb("BwTs", [P, DC * NS], BF16, mid)

                        V.memset(states[:], 0.0)
                        V.memset(cfc[:], 0.0)
                        sv.inc(V.drain())
                        for kd in range(DC):
                            sg.inc(G.dma_start(BwTs[:, kd * NS:(kd + 1) * NS],
                                               BwT[kd * P:(kd + 1) * P, :]), 16)
                        sy.wait(SC)
                        sa.inc(SC.activation(alcol[:, 0:1], alcol[:, 0:1], AF.Sigmoid))
                        sa.wait(SC)
                        sa.inc(SC.activation(alcol[:, 1:2], alcol[:, 0:1], AF.Ln))
                        bar()

                        for ti, c0 in enumerate(MID_T):
                            tw = 512
                            sv.wait(SY)
                            sa.wait(SY)
                            dma_fence()
                            sy.inc(SY.dma_start(sb3(xt, DC, tw, tw), hbm3(x1buf, c0, DC, tw)), 16)
                            sy.wait(V)
                            stats_rows(xt, DC, tw, xbb, sqbb)
                            ln_apply(xt, yy, DC, tw)
                            sv.wait(SY)
                            sy.inc(SY.dma_start(hbm3(ybuf, c0 - 128, DC, tw),
                                                sb3(yy, DC, tw, tw)), 16)
                            stats_rows(yy, DC, tw, xbb, sqbb)
                            ln_apply(yy, xt, DC, tw)
                            for c in range(DC):
                                V.tensor_scalar(xq[:, c * tw:(c + 1) * tw],
                                                xt[:, c * tw:(c + 1) * tw],
                                                gcol[:, c:c + 1], bcolv[:, c:c + 1],
                                                ALU.mult, ALU.add)
                            sv.inc(V.drain())
                            sv.wait(T)
                            sg.wait(T)
                            sev.wait(T, sev.n - 1)
                            for kd in range(DC):
                                mm = T.matmul(pb[ti % 2][:, 0:tw],
                                              BwTs[:, kd * NS:(kd + 1) * NS],
                                              xq[:, kd * tw:(kd + 1) * tw],
                                              start=(kd == 0), stop=(kd == DC - 1))
                            sp.inc(mm)
                            sp.wait(V)
                            sev.inc(V.tensor_scalar(Bsb[:, c0 - 256:c0 - 256 + tw],
                                                    pb[ti % 2][:, 0:tw], bbs[:, 0:1],
                                                    None, ALU.add))
                            sv.inc(V.engine_nop())
                        sa.wait(V)
                        sv.wait(V)
                        sy.wait(V)
                        sa.wait(G)
                        sv.wait(G)
                        sp.wait(G)
                        G.iota(xt[:, 0:S], [[1, S]], channel_multiplier=0,
                               allow_small_or_imprecise_dtypes=True)
                        su.inc(G.engine_nop())
                        su.wait(V)
                        V.tensor_scalar(xt[:, 0:S], xt[:, 0:S], alcol[:, 1:2], None,
                                        ALU.mult)
                        V.drain()
                        V.tensor_scalar(xt[:, 0:S], xt[:, 0:S], LN1E8, 0.0,
                                        ALU.add, ALU.min)
                        sv.inc(V.drain())
                        sv.wait(SC)
                        sa.inc(SC.activation(xt[:, 0:S], xt[:, 0:S], AF.Exp))
                        sa.wait(V)
                        V.tensor_tensor(Bsb[:, 0:S], Bsb[:, 0:S], xt[:, 0:S], ALU.mult)
                        V.drain()
                        V.memset(yy[:, 0:S], 1.0)
                        V.drain()
                        V.tensor_scalar(yy[:, 0:S], yy[:, 0:S], alcol[:, 0:1], None,
                                        ALU.mult)
                        V.drain()
                        V.tensor_tensor_scan(states[:, YPAD:YPAD + S], yy[:, 0:S],
                                             Bsb[:, 0:S], 0.0, ALU.mult, ALU.add)
                        V.drain()
                        with V.register("wo") as wo:
                            V.reg_load(wo, woffs[0:1, 0:1])
                            woi = V.snap(wo, min_val=0, max_val=YPAD + S + YPAD - WLEN)
                            sv.inc(V.tensor_copy(swin[:, 0:WLEN], states[:, ds(woi, WLEN)]))
                        if debug:
                            sv.wait(SY)
                            sy.inc(SY.dma_start(dbg3[:, :], states[:, :]), 16)
                        bar()

                carry3 = ExitStack()
                with carry3:
                    Usb = sb("Usb", [P, MC * WLEN], BF16, carry3)
                    uph = ExitStack()
                    with uph:
                        CwTs = sb("CwTs", [P, D], BF16, uph)
                        WxS = sb("WxS", [P, DC * M3], BF16, uph)
                        gbc = sb("gbc", [P, MC], F32, uph)
                        umsk = sb("umsk", [P, MC * 512], BF16, uph)
                        sow = sb("sow", [P, DC * 512], BF16, uph)
                        wsc2 = sb("wsc2", [P, 2048], F32, uph)
                        gbs = sb("gbs", [P, 16], F32, uph)

                        sg.inc(G.dma_start(CwTs[:], CwT), 16)
                        sg.inc(G.dma_start(umsk[:], umaski), 16)
                        sy.inc(SY.dma_start(gbs[:], gb16), 16)
                        sy.wait(V)
                        V.tensor_copy(gbc[:, 0:8], gbs[:, 0:8])
                        sv.inc(V.scalar_tensor_tensor(gbc[:, 8:12], gbs[:, 8:12], 0.1,
                                                      gbs[:, 12:16], ALU.mult, ALU.add))
                        for kd in range(DC):
                            sv.wait(SY)
                            sy.inc(SY.dma_start(wsc2[:], Wx4[kd * P:(kd + 1) * P, :]), 16)
                            sy.wait(V)
                            V.tensor_copy(WxS[:, kd * M3:kd * M3 + 1024], wsc2[:, 0:1024])
                            sv.inc(V.scalar_tensor_tensor(
                                WxS[:, kd * M3 + 1024:(kd + 1) * M3],
                                wsc2[:, 1024:1536], 0.1, wsc2[:, 1536:2048],
                                ALU.mult, ALU.add))
                        bar()
                        for ti, (wi0, tn) in enumerate(UW_T):
                            sv.wait(T)
                            for dc_ in range(DC):
                                bank = pb[dc_ % 2]
                                sev.wait(T, sev.n - 1)
                                sp.inc(T.matmul(bank[:, 0:tn],
                                                CwTs[:, dc_ * P:(dc_ + 1) * P],
                                                swin[:, wi0:wi0 + tn],
                                                start=True, stop=True))
                                sp.wait(V)
                                ev = V.tensor_scalar(sow[:, dc_ * 512:dc_ * 512 + tn],
                                                     bank[:, 0:tn], cbs[:, dc_:dc_ + 1],
                                                     None, ALU.add)
                                sev.inc(ev)
                            sv.inc(V.engine_nop())
                            sv.wait(T)
                            for mc_ in range(MC):
                                bank = pb[2 + mc_ % 2]
                                sev.wait(T, sev.n - 1)
                                for kd in range(DC):
                                    mm = T.matmul(
                                        bank[:, 0:tn],
                                        WxS[:, kd * M3 + mc_ * P:kd * M3 + (mc_ + 1) * P],
                                        sow[:, kd * 512:kd * 512 + tn],
                                        start=(kd == 0), stop=(kd == DC - 1))
                                sp.inc(mm)
                                sp.wait(V)
                                dstu = Usb[:, mc_ * WLEN + wi0:mc_ * WLEN + wi0 + tn]
                                if ti == 0:
                                    ev = V.scalar_tensor_tensor(
                                        dstu, bank[:, 0:tn], gbc[:, mc_:mc_ + 1],
                                        umsk[:, mc_ * 512:mc_ * 512 + tn],
                                        ALU.add, ALU.mult)
                                else:
                                    ev = V.tensor_scalar(dstu, bank[:, 0:tn],
                                                         gbc[:, mc_:mc_ + 1], None, ALU.add)
                                sev.inc(ev)
                            sv.inc(V.engine_nop())
                        if debug:
                            sv.wait(SY)
                            sy.inc(SY.dma_start(dbg2[:, :], Usb[:, :]), 16)
                        bar()

                    pic = ExitStack()
                    with pic:
                        WhS = sb("WhS", [P, KC * M3], BF16, pic)
                        wsc3 = sb("wsc3", [P, 2048], F32, pic)
                        tfb = sb("tfb", [P, 2 * 512], F32, pic)
                        tzb = sb("tzb", [P, 512], F32, pic)
                        sgbuf = sb("sgbuf", [P, 512], F32, pic)
                        tfbB = sb("tfbB", [P, 2 * 512], F32, pic)
                        tzbB = sb("tzbB", [P, 512], F32, pic)
                        sgbufB = sb("sgbufB", [P, 512], F32, pic)
                        dtb = sb("dtb", [P, 512], F32, pic)
                        ptb = sb("ptb", [P, 512], F32, pic)

                        for kk in range(KC):
                            sv.wait(SY)
                            dma_fence()
                            sy.inc(SY.dma_start(wsc3[:], Wh4[kk * P:(kk + 1) * P, :]), 16)
                            sy.wait(V)
                            V.tensor_copy(WhS[:, kk * M3:kk * M3 + 1024], wsc3[:, 0:1024])
                            sv.inc(V.scalar_tensor_tensor(WhS[:, kk * M3 + 1024:(kk + 1) * M3],
                                                          wsc3[:, 1024:1536], 0.1,
                                                          wsc3[:, 1536:2048], ALU.mult, ALU.add))
                        bar()
                        NGRP = len(UW_T) * KC
                        hw_marks = [sv.n]
                        ac_marks = []
                        g_all = 0
                        pend = []
                        for k in range(NSWEEP):
                            parR = (k % 2) * CPAR
                            parW = ((k + 1) % 2) * CPAR
                            for ti, (wi0, tn) in enumerate(UW_T):
                                for kap in range(KC):
                                    trip = ptrip[g_all % 2]
                                    T.wait_ge(sv.h, hw_marks[k])
                                    if g_all >= 2:
                                        sa.wait(T, ac_marks[g_all - 2])
                                    for gi, mpart in enumerate((kap, 4 + kap, 8 + kap)):
                                        T.matmul(trip[gi][:, 0:tn], idsb[:],
                                                 Usb[:, mpart * WLEN + wi0:
                                                     mpart * WLEN + wi0 + tn],
                                                 start=True, stop=False)
                                        for kk in range(KC):
                                            mm = T.matmul(
                                                trip[gi][:, 0:tn],
                                                WhS[:, kk * M3 + mpart * P:kk * M3 + (mpart + 1) * P],
                                                cfc[:, parR + kk * NCOL + wi0:
                                                     parR + kk * NCOL + wi0 + tn],
                                                start=False, stop=(kk == KC - 1))
                                    sp.inc(mm)
                                    sp.wait(SC)
                                    sv.wait(SC)
                                    tf_, tz_, sg2_ = ((tfb, tzb, sgbuf) if g_all % 2 == 0
                                                      else (tfbB, tzbB, sgbufB))
                                    SC.activation(tf_[:, 0:tn], trip[0][:, 0:tn], AF.Tanh)
                                    SC.activation(tf_[:, 512:512 + tn], trip[1][:, 0:tn], AF.Tanh)
                                    SC.activation(sg2_[:, 0:tn], trip[2][:, 0:tn], AF.Sigmoid)
                                    sa.inc(SC.activation(tz_[:, 0:tn], trip[2][:, 0:tn],
                                                         AF.Sigmoid, scale=cneg1[:, 0:1]))
                                    ac_marks.append(sa.n)
                                    sa.wait(V)
                                    da, pa = ((dtb, ptb) if g_all % 2 == 0
                                              else (hmt, hmt2))
                                    V.tensor_tensor(da[:, 0:tn], tf_[:, 0:tn],
                                                    tz_[:, 0:tn], ALU.mult)
                                    V.tensor_tensor(pa[:, 0:tn], tf_[:, 512:512 + tn],
                                                    sg2_[:, 0:tn], ALU.mult)
                                    if g_all % 2 == 1 or (ti == len(UW_T) - 1
                                                          and kap == KC - 1):
                                        V.drain()
                                        for (pw, da_, pa_, tn_) in pend:
                                            sv.inc(V.tensor_tensor(
                                                cfc[:, pw:pw + tn_], da_[:, 0:tn_],
                                                pa_[:, 0:tn_], ALU.add))
                                        sv.inc(V.tensor_tensor(
                                            cfc[:, parW + kap * NCOL + wi0 + 1:
                                                 parW + kap * NCOL + wi0 + 1 + tn],
                                            da[:, 0:tn], pa[:, 0:tn], ALU.add))
                                        pend = []
                                    else:
                                        pend.append((parW + kap * NCOL + wi0 + 1,
                                                     da, pa, tn))
                                    g_all += 1
                            hw_marks.append(sv.n)
                        bar()
                if debug:
                    sy.inc(SY.dma_start(dbgb[:, :], cfc[:, :]), 16)
                    bar()

            op = ExitStack()
            with op:
                owTs = sb("owTs", [P, KC * D], BF16, op)
                x1a = sb("x1a", [P, DC * 512], F32, op)
                x1b = sb("x1b", [P, DC * 512], F32, op)
                ya = sb("ya", [P, DC * 512], F32, op)
                yb = sb("yb", [P, DC * 512], F32, op)
                x2ma = sb("x2ma", [P, 512], F32, op)
                x2mr = sb("x2mr", [P, 128], F32, op)
                x2t = sb("x2t", [P, DC * 512], F32, op)

                for kk in range(KC):
                    sg.inc(G.dma_start(owTs[:, kk * D:(kk + 1) * D],
                                       owT[kk * P:(kk + 1) * P, :]), 16)
                sy.inc(SY.dma_start(x2ma[:], x2m0i), 16)
                sy.inc(SY.dma_start(x2mr[:], x2m4i), 16)
                bar()
                for ti, (wi0, tn) in enumerate(UW_T):
                    sv.wait(SY)
                    dma_fence()
                    sy.inc(SY.dma_start(sb3(x1a, DC, tn, 512), hbm3(x1buf, 130 + wi0, DC, tn)), 16)
                    sy.inc(SY.dma_start(sb3(x1b, DC, tn, 512), hbm3(x1buf, 2178 + wi0, DC, tn)), 16)
                    sy.inc(SY.dma_start(sb3(ya, DC, tn, 512), hbm3(ybuf, 2 + wi0, DC, tn)), 16)
                    sy.inc(SY.dma_start(sb3(yb, DC, tn, 512), hbm3(ybuf, 2050 + wi0, DC, tn)), 16)
                    sy.wait(V)
                    sg.wait(T)
                    for dc_ in range(DC):
                        bank = pb[dc_ % 2]
                        sev.wait(T, sev.n - 1)
                        for kk in range(KC):
                            base = kk * NCOL + wi0 + 1
                            mm = T.matmul(bank[:, 0:tn],
                                          owTs[:, kk * D + dc_ * P:kk * D + (dc_ + 1) * P],
                                          cfc[:, base:base + tn],
                                          start=(kk == 0), stop=(kk == KC - 1))
                        sp.inc(mm)
                        sp.wait(V)
                        co = dc_ * 512
                        V.tensor_tensor(hmt[:, 0:tn], x1a[:, co:co + tn], ya[:, co:co + tn],
                                        ALU.add)
                        V.tensor_tensor(hmt2[:, 0:tn], x1b[:, co:co + tn], yb[:, co:co + tn],
                                        ALU.add)
                        ev = V.tensor_scalar(x2t[:, co:co + tn], bank[:, 0:tn],
                                             obs[:, dc_:dc_ + 1], None, ALU.add)
                        sev.inc(ev)
                        V.drain()
                        V.scalar_tensor_tensor(hmt[:, 0:tn], hmt[:, 0:tn], mks[:, 0:1],
                                               x2t[:, co:co + tn], ALU.mult, ALU.add)
                        V.drain()
                        V.scalar_tensor_tensor(x2t[:, co:co + tn], hmt2[:, 0:tn],
                                               mks[:, 1:2], hmt[:, 0:tn],
                                               ALU.mult, ALU.add)
                        if ti == 0:
                            V.drain()
                            V.tensor_tensor(x2t[:, co:co + tn], x2t[:, co:co + tn],
                                            x2ma[:, 0:tn], ALU.mult)
                        elif ti == 4:
                            V.drain()
                            V.tensor_tensor(x2t[:, co:co + tn], x2t[:, co:co + tn],
                                            x2mr[:, 0:tn], ALU.mult)
                        sv.inc(V.drain())
                    sv.wait(SY)
                    sy.inc(SY.dma_start(hbm3(x2buf, X2OFF + wi0, DC, tn),
                                        sb3(x2t, DC, tn, 512)), 16)
                bar()

        ffn2 = ExitStack()
        with ffn2:
            wq = sb("wq", [P, DC * HD], BF16, ffn2)
            V.memset(a2col[:], 0.0)
            sv.inc(V.drain())
            for kd in range(DC):
                sg.inc(G.dma_start(wq[:, kd * HD:(kd + 1) * HD],
                                   wq1buf[kd * P:(kd + 1) * P, :]), 16)
            bar()
            sta = ExitStack()
            with sta:
                abuf = make_stage_a_bufs(sta)
                V.memset(abuf[5][:], 0.0)
                sv.inc(V.drain())
                bar()
                ffn_stage_a(wq, abuf, x2buf, FF2_A, yq2buf)
            stb = ExitStack()
            with stb:
                xt = sb("xt", [P, DC * 512], F32, stb)
                yy = sb("yy", [P, DC * 512], F32, stb)
                yqb = [sb("yqb", [P, HC * 512], BF16, stb),
                       sb("yqb2", [P, HC * 512], BF16, stb)]
                for kh in range(HC):
                    sg.inc(G.dma_start(wq[:, kh * D:(kh + 1) * D],
                                       wq2buf[kh * P:(kh + 1) * P, :]), 16)
                bar()
                ffn_stage_b(wq, xt, yy, yqb, x2buf, FF2_B, yq2buf, outT)

        dma_fence()
        bar()
    return nc


def _prep_inputs(inputs):
    f32 = np.float32
    x = np.asarray(inputs["L0_attn_x"], f32)
    w1 = np.asarray(inputs["L0_ffn_w1"], f32)
    w2 = np.asarray(inputs["L0_ffn_w2"], f32)
    cw = np.asarray(inputs["L0_ffn_conv_w"], f32)
    cb = np.asarray(inputs["L0_ffn_conv_b"], f32)
    Bw = np.asarray(inputs["L0_attn_B_w"], f32)
    Bbv = np.asarray(inputs["L0_attn_B_b"], f32)
    Al = np.asarray(inputs["L0_attn_A_log"], f32)
    Cw = np.asarray(inputs["L0_attn_C_w"], f32)
    Cbv = np.asarray(inputs["L0_attn_C_b"], f32)
    ng = np.asarray(inputs["L0_attn_norm_g"], f32)
    nb = np.asarray(inputs["L0_attn_norm_b"], f32)
    f1w = np.asarray(inputs["L0_attn_ff1_w"], f32)
    f2w = np.asarray(inputs["L0_attn_ff2_w"], f32)
    taw = np.asarray(inputs["L0_attn_ta_w"], f32)
    tbw = np.asarray(inputs["L0_attn_tb_w"], f32)
    f1b = np.asarray(inputs["L0_attn_ff1_b"], f32)
    f2b = np.asarray(inputs["L0_attn_ff2_b"], f32)
    tab = np.asarray(inputs["L0_attn_ta_b"], f32)
    tbb = np.asarray(inputs["L0_attn_tb_b"], f32)
    ow = np.asarray(inputs["L0_attn_out_w"], f32)
    obv = np.asarray(inputs["L0_attn_out_b"], f32)

    shared = {
        "w1T": np.ascontiguousarray(w1.T),
        "w2T": np.ascontiguousarray(w2.T),
        "convw": np.ascontiguousarray(
            cw[:, 0, :].reshape(HC, P, 3).transpose(1, 0, 2).reshape(P, HC * 3)),
        "convb": np.ascontiguousarray(cb.reshape(HC, P).T),
        "normg": np.ascontiguousarray(ng.reshape(DC, P).T),
        "normb": np.ascontiguousarray(nb.reshape(DC, P).T),
        "BwT": np.ascontiguousarray(Bw.T),
        "Bb": np.ascontiguousarray(Bbv.reshape(P, 1)),
        "Alog": np.ascontiguousarray(Al.reshape(P, 1)),
        "CwT": np.ascontiguousarray(Cw.T),
        "Cb": np.ascontiguousarray(Cbv.reshape(DC, P).T),
        "Wx4": np.ascontiguousarray(np.concatenate(
            [f1w[:, :D].T, f2w[:, :D].T, taw[:, :D].T, tbw[:, :D].T], axis=1)),
        "Wh4": np.ascontiguousarray(np.concatenate(
            [f1w[:, D:].T, f2w[:, D:].T, taw[:, D:].T, tbw[:, D:].T], axis=1)),
        "gb16": np.ascontiguousarray(np.concatenate(
            [f1b.reshape(KC, P).T, f2b.reshape(KC, P).T,
             tab.reshape(KC, P).T, tbb.reshape(KC, P).T], axis=1)),
        "owT": np.ascontiguousarray(ow.T),
        "ob": np.ascontiguousarray(obv.reshape(DC, P).T),
        "ident": np.eye(P, dtype=f32),
    }
    in_maps = []
    for c in range(8):
        b, hf = c // 2, c % 2
        s = 2048 * hf
        xTc = np.zeros((D, XWX), f32)
        xTc[:, 256:256 + S] = x[b].T
        masks_ = np.zeros((P, 8), f32)
        masks_[:, 0] = 1.0 - hf
        masks_[:, 1] = float(hf)
        umask_ = np.ones((P, MC * 512), f32)
        if hf == 0:
            for m in range(MC):
                umask_[:, m * 512:m * 512 + 126] = 0.0
        x2m0_ = np.ones((P, 512), f32)
        if hf == 0:
            x2m0_[:, 0:126] = 0.0
        x2m4_ = np.ones((P, 128), f32)
        if hf == 1:
            x2m4_[:, 126:128] = 0.0
        d = dict(shared)
        d.update({
            "xT": xTc,
            "masks": masks_,
            "umask": umask_,
            "x2m0": x2m0_,
            "x2m4": x2m4_,
            "woff": np.array([[s + 2]], np.uint32),
        })
        in_maps.append(d)
    return in_maps


def kernel(**inputs):
    in_maps = _prep_inputs(inputs)
    nc = build_nc(debug=False)
    res = run_bass_kernel_spmd(nc, in_maps, core_ids=list(range(8)))
    out = np.empty((4, S, D), np.float32)
    for c in range(8):
        b, hf = c // 2, c % 2
        o = np.asarray(res.results[c]["outT"])
        out[b, 2048 * hf:2048 * (hf + 1), :] = o[:, OUT_LO:OUT_HI].T
    return out
